# version 14
# baseline (speedup 1.0000x reference)
from contextlib import ExitStack

import numpy as np

import concourse.bass as bass
import concourse.mybir as mybir
from concourse.bass_utils import run_bass_kernel_spmd

F32 = mybir.dt.float32
BF16 = mybir.dt.bfloat16
AF = mybir.ActivationFunctionType
ALU = mybir.AluOpType
AX = mybir.AxisListType

P = 128
T = 2048
D = 1024
NTC = T // P
NDC = D // P
GRP = 256
NG = T // GRP
INNER = 2048
HEAD = 128
H0 = 4096
E0 = 8
E1 = 16
HD = 64
NH = 16
KVH = 4
LAT = 256
WIN = 256
MZ = 384

STAGE = 99
TAPS = {}

INV_S2048 = float(np.float32(1.0) / np.float32(np.sqrt(np.float32(2048.0))))
SHD = float(np.float32(1.0) / np.float32(np.sqrt(np.float32(HD))))


class Prog:
    ENGS = ("t", "v", "s", "g", "y")

    def __init__(self, nc, ctx):
        self.nc = nc
        self.ctx = ctx
        self.q = {e: [] for e in self.ENGS}
        self.sems = {}
        self.waited = {e: {} for e in self.ENGS}
        self.nops = {e: 0 for e in self.ENGS}

    def _sem(self, name):
        if name not in self.sems:
            h = self.ctx.enter_context(self.nc.semaphore(name))
            self.sems[name] = [h, 0]
        return self.sems[name]

    def op(self, eng, fn, sig=None, sigv=1, waits=()):
        if sig is None:
            sig = "p" + eng
        s = self._sem(sig)
        s[1] += sigv
        after = s[1]
        wl = []
        for evn in waits:
            if evn is None:
                continue
            wn, wv = evn
            if wv <= 0:
                continue
            self._sem(wn)
            cur = self.waited[eng].get(wn, 0)
            if wv > cur:
                wl.append((wn, wv))
                self.waited[eng][wn] = wv
        self.nops[eng] += 1

        def emit(e, fn=fn, wl=wl, sname=sig, sigv=sigv):
            for (wn, wv) in wl:
                e.wait_ge(self.sems[wn][0], wv)
            inst = fn(e)
            inst.then_inc(self.sems[sname][0], sigv)
        self.q[eng].append(emit)
        return (sig, after)

    def wait(self, eng, *evs):
        wl = []
        for evn in evs:
            if evn is None:
                continue
            wn, wv = evn
            if wv <= 0:
                continue
            self._sem(wn)
            cur = self.waited[eng].get(wn, 0)
            if wv > cur:
                wl.append((wn, wv))
                self.waited[eng][wn] = wv
        if wl:
            def emit(e, wl=wl):
                for (wn, wv) in wl:
                    e.wait_ge(self.sems[wn][0], wv)
            self.q[eng].append(emit)

    def now(self, eng):
        s = self._sem("p" + eng)
        return ("p" + eng, s[1])

    def barrier(self, engs="tvsg"):
        evs = [self.now(e) for e in engs]
        for e in engs:
            self.wait(e, *[ev for ev in evs if ev[0] != "p" + e])

    def flush(self, block):
        for ename, attr in (("g", "gpsimd"), ("s", "scalar"), ("t", "tensor"),
                            ("v", "vector"), ("y", "sync")):
            fns = self.q[ename]
            if not fns:
                continue
            getattr(block, attr)(lambda e, fns=fns: [f(e) for f in fns])


def host_constants():
    ident = np.eye(P, dtype=np.float32)

    def trig(base):
        inv = (1.0 / (np.float32(base) ** (np.arange(0, HD, 2, dtype=np.float32)
                                           / np.float32(HD)))).astype(np.float32)
        pos = np.arange(T, dtype=np.float32)
        fr = pos[:, None] * inv[None, :]
        return np.cos(fr).astype(np.float32), np.sin(fr).astype(np.float32)

    cg, sg = trig(1000000.0)
    cl, sl = trig(10000.0)

    def tm32(x):
        return x.reshape(NTC, P, 32).transpose(1, 0, 2).reshape(P, NTC * 32)

    trigpack = np.concatenate([tm32(cg), tm32(sg), tm32(cl), tm32(sl)], axis=1)
    ii = np.arange(P)[:, None]
    jj = np.arange(MZ + 512)[None, :]
    mbc = ((jj - MZ) >= ii).astype(np.float32)
    jj2 = np.arange(MZ + 512 + WIN + P)[None, :]
    d2 = jj2 - MZ - ii
    mb2 = ((d2 >= 0) & (d2 <= WIN - 1)).astype(np.float32)
    return ident, trigpack, mbc, mb2


def pack_cvec(ins):
    cols, cmap = [], {}

    def add(name, arr):
        a = np.asarray(arr, dtype=np.float32)
        assert a.size % P == 0, (name, a.shape)
        ncol = a.size // P
        cmap[name] = (sum(c.shape[1] for c in cols), ncol)
        cols.append(a.reshape(ncol, P).T)

    add("ones", np.ones(P))
    add("bin_", ins["L0_attn_bin_"])
    add("g_qq", ins["L0_attn_g_qq"]); add("b_qq", ins["L0_attn_b_qq"])
    add("g_qk", ins["L0_attn_g_qk"]); add("b_qk", ins["L0_attn_b_qk"])
    add("g_lq", ins["L0_attn_g_lq"]); add("b_lq", ins["L0_attn_b_lq"])
    add("g_lk", ins["L0_attn_g_lk"]); add("b_lk", ins["L0_attn_b_lk"])
    add("bout", ins["L0_attn_bout"])
    add("bg", ins["L0_attn_bg"])
    add("b1", ins["L0_ffn_b1"])
    add("gamma", ins["L0_ffn_gamma"]); add("beta", ins["L0_ffn_beta"])
    add("n1w", ins["L1_attn_norm1_w"]); add("n2w", ins["L1_attn_norm2_w"])
    add("eps5", np.full(P, 1e-5)); add("eps6", np.full(P, 1e-6))
    return np.concatenate(cols, axis=1).astype(np.float32), cmap


def pack_crow(ins):
    rows, rmap = [], {}

    def add(name, arr):
        a = np.asarray(arr, dtype=np.float32).reshape(-1)
        rmap[name] = (sum(r.size for r in rows), a.size)
        rows.append(a)

    add("ones", np.ones(P))
    add("bv", ins["L0_attn_bv"])
    add("bg0", ins["L0_ffn_bg"])
    add("rb", ins["L1_ffn_router_b"])
    return np.concatenate(rows).reshape(1, -1).astype(np.float32), rmap


CVEC_NCOL = [None]; CVEC_MAP = [None]
CROW_NCOL = [None]; CROW_MAP = [None]


def _init_pack_shapes():
    z = np.zeros
    dummy = {
        "L0_attn_bin_": z(HEAD), "L0_attn_g_qq": z(HEAD), "L0_attn_b_qq": z(HEAD),
        "L0_attn_g_qk": z(HEAD), "L0_attn_b_qk": z(HEAD), "L0_attn_g_lq": z(HEAD),
        "L0_attn_b_lq": z(HEAD), "L0_attn_g_lk": z(HEAD), "L0_attn_b_lk": z(HEAD),
        "L0_attn_bout": z(D), "L0_attn_bg": z(INNER), "L0_ffn_b1": z((E0, H0)),
        "L0_ffn_gamma": z(D), "L0_ffn_beta": z(D),
        "L1_attn_norm1_w": z(D), "L1_attn_norm2_w": z(D),
        "L0_attn_bv": z(INNER), "L0_ffn_bg": z(E0), "L1_ffn_router_b": z(E1),
    }
    cp, cm = pack_cvec(dummy)
    CVEC_NCOL[0], CVEC_MAP[0] = cp.shape[1], cm
    rp, rm = pack_crow(dummy)
    CROW_NCOL[0], CROW_MAP[0] = rp.shape[1], rm


_init_pack_shapes()

STATC = {"mu0": 0, "rstd0": 16, "mu1": 32, "rstd1": 48, "var1": 64,
         "s1": 80, "s2": 96, "acc1": 112, "tmp": 128}
NSTAT = 128


class K:

    def __init__(self, stage, taps):
        self.stage = stage
        self.taps = taps
        nc = self.nc = bass.Bass("TRN2", target_bir_lowering=False)
        self.ctx = ExitStack()
        self.pr = Prog(nc, self.ctx)

        dr = self.dr = {}
        for nm, sh in (
            ("x_in", (T, D)),
            ("L0_attn_Wv", (D, INNER)), ("L0_attn_Wg", (D, INNER)),
            ("L0_attn_Win", (D, HEAD)), ("L0_attn_Wout", (INNER, D)),
            ("L0_ffn_Wg", (D, E0)), ("L0_ffn_W1", (E0, D, H0)),
            ("L0_ffn_W2", (E0, H0, D)), ("L0_ffn_b2", (E0, D)),
            ("L1_attn_g_wq", (D, D)), ("L1_attn_g_kv_down", (D, LAT)),
            ("L1_attn_g_k_up", (LAT, KVH * HD)), ("L1_attn_g_v_up", (LAT, KVH * HD)),
            ("L1_attn_g_wo", (D, D)), ("L1_attn_l_wq", (D, D)),
            ("L1_attn_l_wk", (D, KVH * HD)), ("L1_attn_l_wv", (D, KVH * HD)),
            ("L1_attn_l_wo", (D, D)),
            ("L1_ffn_router_w", (E1, D)), ("L1_ffn_expert_w", (E1, D, D)),
            ("c_ident", (P, P)), ("c_trig", (P, 4 * NTC * 32)),
            ("c_mbc", (P, MZ + 512)), ("c_mb2", (P, MZ + 512 + WIN + P)),
            ("c_cvec", (P, CVEC_NCOL[0])), ("c_crow", (1, CROW_NCOL[0])),
        ):
            dr[nm] = nc.dram_tensor(nm, list(sh), F32, kind="ExternalInput").ap()
        self.out_h = nc.dram_tensor("out", [T, D], F32, kind="ExternalOutput").ap()
        self.tap_outs = {tn: nc.dram_tensor("tap_" + tn, list(sh), F32, kind="ExternalOutput").ap()
                         for tn, (sh, _f) in taps.items()}

        A = nc.alloc_sbuf_tensor
        self.x_sb = A("x_sb", [P, NDC * T], F32)
        self.xn_sb = A("xn_sb", [P, NDC * T], BF16)
        self.pool = A("pool", [P, 80 * 1024], mybir.dt.uint8)
        self.id_f = A("id_f", [P, P], F32)
        self.id_b = A("id_b", [P, P], BF16)
        self.trig = A("trig_sb", [P, 4 * NTC * 32], F32)
        self.mbc = A("mbc_sb", [P, MZ + 512], BF16)
        self.mb2 = A("mb2_sb", [P, MZ + 512 + WIN + P], BF16)
        self.cvec = A("cvec", [P, CVEC_NCOL[0]], F32)
        self.crow_b = A("crow_b", [1, CROW_NCOL[0]], BF16)
        self.onesf = A("onesf", [1, P], F32)
        self.bg0f = A("bg0f", [1, E0], F32)
        self.wg0 = A("wg0_sb", [P, NDC * E0], F32)
        self.rwt = A("rwt_sb", [P, NDC * E1], BF16)
        self.stats = A("stats", [P, NSTAT], F32)
        self.comb0 = A("comb0", [P, NTC * E0], F32)
        self.w1tm = A("w1tm", [P, NTC * E1], F32)
        self.csb8 = A("csb8", [P, E0], F32)
        self.bgb8 = A("bgb8", [P, E0], F32)
        self.bc_s = A("bc_s", [P, 1024], BF16)
        self.combrow_s = A("combrow_s", [1, 1024], BF16)
        self.w2b_s = A("w2b_s", [P, 2048], BF16)
        self.pb = [nc.alloc_psum_tensor("pb%d" % i, [P, 512], F32) for i in range(8)]
        self.bank_free = {i: None for i in range(8)}

    def psf(self, i, w=512, off=0):
        return self.pb[i][:, off:off + w]

    def psb(self, i, w=1024, off=0):
        return self.pb[i][:].bitcast(BF16)[:, off:off + w]

    def X(self, dc, c0=0, c1=T):
        return self.x_sb[:, dc * T + c0: dc * T + c1]

    def XN(self, dc, c0=0, c1=T):
        return self.xn_sb[:, dc * T + c0: dc * T + c1]

    def CV(self, name, j=0, n=1):
        c0, ncol = CVEC_MAP[0][name]
        assert j + n <= ncol, (name, j, n)
        return self.cvec[:, c0 + j: c0 + j + n]

    def CRB(self, name, j0=0, n=None):
        c0, ncol = CROW_MAP[0][name]
        n = ncol if n is None else n
        return self.crow_b[0:1, c0 + j0: c0 + j0 + n]


    def ST(self, name, j=0, n=1):
        return self.stats[:, STATC[name] + j: STATC[name] + j + n]

    def lp(self, f):
        with self.nc.allow_low_precision(reason="intentional bf16 row math"):
            return f()

    def pv(self, off, dt, w, p0=0, p1=P):
        t = self.pool[:].bitcast(dt)
        sz = {F32: 4, BF16: 2}[dt]
        assert off % sz == 0 and off + w * sz <= 80 * 1024
        return t[p0:p1, off // sz: off // sz + w]

    def trigv(self, which, tc, nrep=None):
        base = {"cosG": 0, "sinG": 1, "cosL": 2, "sinL": 3}[which] * NTC * 32
        ap = self.trig[:, base + tc * 32: base + (tc + 1) * 32]
        if nrep is not None:
            ap = ap.unsqueeze(1).broadcast_to([P, nrep, 32])
        return ap

    def phase0(self):
        pr, nc, dr = self.pr, self.nc, self.dr
        psf, psb, X, XN, CV, ST = self.psf, self.psb, self.X, self.XN, self.CV, self.ST

        _c0, _n = CROW_MAP[0]["bg0"]
        for fn in (
            lambda e: e.dma_start(out=self.id_f[:], in_=dr["c_ident"]),
            lambda e: e.dma_start(out=self.trig[:], in_=dr["c_trig"]),
            lambda e: e.dma_start(out=self.cvec[:], in_=dr["c_cvec"]),
            lambda e: e.dma_start(out=self.bg0f[:], in_=dr["c_crow"][0:1, _c0:_c0 + E0]),
        ):
            pr.op("y", fn, sig="dcy", sigv=16)
        for dc in range(NDC):
            pr.op("y", lambda e, dc=dc: e.dma_start(
                out=self.wg0[:, dc * E0:(dc + 1) * E0],
                in_=dr["L0_ffn_Wg"][dc * P:(dc + 1) * P, :],
                allow_slow_non_contiguous=True), sig="dcy", sigv=16)
        for fn in (
            lambda e: e.dma_start(out=self.id_b[:], in_=dr["c_ident"]),
            lambda e: e.dma_start(out=self.mbc[:], in_=dr["c_mbc"]),
            lambda e: e.dma_start(out=self.mb2[:], in_=dr["c_mb2"]),
            lambda e: e.dma_start(out=self.crow_b[:], in_=dr["c_crow"]),
        ):
            pr.op("g", fn, sig="dcg", sigv=16)
        ev_onf = pr.op("g", lambda e: e.memset(self.onesf[:], 1.0))
        ECY = ("dcy", pr.sems["dcy"][1])
        ECG = ("dcg", pr.sems["dcg"][1])
        self.ECY, self.ECG, self.ev_onf = ECY, ECG, ev_onf
        ev_cv = ECY
        self.ev_cv = ECY
        ev_idf = ECY
        ev_idb = ECG
        self.ev_idf, self.ev_idb = ECY, ECG
        ev_bgf = ECY
        ev_wg0 = [ECY] * NDC

        xtm = [self.pv(i * 4096, F32, 1024) for i in range(4)]
        xnm = [self.pv(16384, BF16, 1024), self.pv(18432, BF16, 1024)]
        dump = [self.pv(20480, F32, 1024), self.pv(24576, F32, 1024)]

        for dc in range(NDC):
            pr.op("t", lambda e, dc=dc: e.matmul(
                psf(6, E0)[0:1, :], CV("ones"), self.wg0[:, dc * E0:(dc + 1) * E0],
                start=(dc == 0), stop=(dc == NDC - 1)),
                waits=[ev_cv, ev_wg0[dc]])
        ev = pr.op("v", lambda e: e.tensor_copy(self.csb8[0:1, :], psf(6, E0)[0:1, :]),
                   waits=[pr.now("t")])
        ev = pr.op("t", lambda e: e.matmul(psf(6, E0), self.onesf[:], self.csb8[0:1, :],
                                           start=True, stop=True),
                   waits=[ev, ev_onf])
        ev_csb = pr.op("v", lambda e: e.tensor_copy(self.csb8[:], psf(6, E0)), waits=[ev])
        ev = pr.op("t", lambda e: e.matmul(psf(6, E0), self.onesf[:], self.bg0f[:],
                                           start=True, stop=True),
                   waits=[ev_csb, ev_bgf])
        ev_bgb = pr.op("v", lambda e: e.tensor_copy(self.bgb8[:], psf(6, E0)), waits=[ev])
        rwraw = self.pv(36864, F32, 1024, 0, 16)
        ev_rwl = pr.op("y", lambda e: e.dma_start(out=rwraw, in_=dr["L1_ffn_router_w"]),
                       sig="drw", sigv=16)
        for dc in range(NDC):
            ev_tt = pr.op("t", lambda e, dc=dc: e.transpose(
                psf(7, 128)[0:P, 0:16], rwraw[0:16, dc * P:(dc + 1) * P],
                self.id_f[0:16, 0:16]),
                waits=[ev_rwl, self.ECY, self.bank_free.get(7)])
            ev_cp = pr.op("v", lambda e, dc=dc: e.tensor_copy(
                self.rwt[:, dc * E1:(dc + 1) * E1], psf(7, 128)[0:P, 0:16]),
                waits=[ev_tt])
            self.bank_free[7] = ev_cp

        slot_free = [[], [], [], []]
        xnm_free = [None, None]
        bank_free = self.bank_free
        SX = {}

        def mk_acc(tc):
            g0 = self.pv(28672 + tc * 192, F32, 48)
            gsc = lambda j, n=1, g0=g0: g0[:, j:j + n]
            tst = lambda j, n=1, g0=g0: g0[:, 32 + j:32 + j + n]
            return gsc, tst

        def emit_load(tc):
            sl = tc % 4
            SX[tc] = d = {}
            d["x"] = pr.op("y", lambda e, tc=tc, sl=sl: e.dma_start(
                out=xtm[sl], in_=dr["x_in"][tc * P:(tc + 1) * P, :]), sig="dx%d" % sl, sigv=16,
                waits=slot_free[sl])

        def emit_stats(tc):
            sl, sl2 = tc % 4, tc % 2
            d = SX[tc]
            gsc, tst = mk_acc(tc)
            d["gsc"], d["tst"] = gsc, tst
            ev_x = d["x"]
            d["sq"] = ev_sq = pr.op("s", lambda e, sl=sl, sl2=sl2, tst=tst: e.activation(
                dump[sl2], xtm[sl], AF.Square, accum_out=tst(0)), waits=[ev_x])
            ev_sm = pr.op("v", lambda e, sl=sl, tst=tst: e.reduce_sum(tst(1), xtm[sl], axis=AX.X), waits=[ev_x])
            ev_mu = pr.op("v", lambda e, tc=tc, tst=tst: e.tensor_scalar(
                ST("mu0", tc), tst(1), 1.0 / D, None, ALU.mult), waits=[ev_sm])
            ev_e2 = pr.op("v", lambda e, tst=tst: e.tensor_scalar(tst(2), tst(0), 1.0 / D, None, ALU.mult),
                          waits=[ev_sq])
            ev_ms = pr.op("v", lambda e, tc=tc, tst=tst: e.tensor_tensor(
                tst(3), ST("mu0", tc), ST("mu0", tc), ALU.mult), waits=[ev_mu])
            ev_va = pr.op("v", lambda e, tst=tst: e.tensor_tensor(tst(2), tst(2), tst(3), ALU.subtract),
                          waits=[ev_e2, ev_ms])
            ev_sd = pr.op("s", lambda e, tst=tst: e.activation(tst(3), tst(2), AF.Sqrt, bias=CV("eps5")),
                          waits=[ev_va, self.ECY])
            d["rs"] = ev_rs = pr.op("v", lambda e, tc=tc, tst=tst: e.reciprocal(ST("rstd0", tc), tst(3)),
                                    waits=[ev_sd])
            d["xn"] = pr.op("v", lambda e, sl=sl, sl2=sl2, tc=tc: e.tensor_scalar(
                xnm[sl2], xtm[sl], ST("mu0", tc), ST("rstd0", tc), ALU.subtract, ALU.mult),
                waits=[ev_rs, xnm_free[sl2]])

        def emit_tx(tc):
            sl = tc % 4
            d = SX[tc]
            d["evs_X"] = []
            for dc in range(NDC):
                bk = dc % 2
                ev_t = pr.op("t", lambda e, sl=sl, dc=dc, bk=bk: e.transpose(
                    psf(bk, P), xtm[sl][:, dc * P:(dc + 1) * P], self.id_f[:]),
                    waits=[d["x"], ev_idf, bank_free[bk]])
                ev_c = pr.op("v", lambda e, dc=dc, tc=tc, bk=bk: e.tensor_copy(
                    X(dc, tc * P, (tc + 1) * P), psf(bk, P)), waits=[ev_t])
                bank_free[bk] = ev_c
                d["evs_X"].append(ev_c)
            slot_free[sl] = [d["sq"], d["xn"], pr.now("t")]

        def emit_txn(tc):
            sl2 = tc % 2
            d = SX[tc]
            for dc in range(NDC):
                bk2 = 2 + dc % 2
                ev_t2 = pr.op("t", lambda e, sl2=sl2, dc=dc, bk2=bk2: e.transpose(
                    psb(bk2, P), xnm[sl2][:, dc * P:(dc + 1) * P], self.id_b[:]),
                    waits=[d["xn"], ev_idb, bank_free[bk2]])
                ev_c2 = pr.op("s", lambda e, dc=dc, tc=tc, bk2=bk2: e.copy(
                    XN(dc, tc * P, (tc + 1) * P), psb(bk2, P)), waits=[ev_t2])
                bank_free[bk2] = ev_c2
            xnm_free[sl2] = pr.now("t")

        def emit_router(tc):
            d = SX[tc]
            gsc, tst, ev_rs = d["gsc"], d["tst"], d["rs"]
            for dc in range(NDC):
                pr.op("t", lambda e, dc=dc, tc=tc: e.matmul(
                    psf(4, E0), X(dc, tc * P, (tc + 1) * P),
                    self.wg0[:, dc * E0:(dc + 1) * E0],
                    start=(dc == 0), stop=(dc == NDC - 1)),
                    waits=[d["evs_X"][dc], bank_free[4]])
            ev_nm = pr.op("v", lambda e, tc=tc, tst=tst: e.tensor_scalar(
                tst(4), ST("mu0", tc), ST("rstd0", tc), -1.0, ALU.mult, ALU.mult),
                waits=[ev_rs])
            ev_a2 = pr.op("v", lambda e, tc=tc, gsc=gsc: e.tensor_scalar(
                gsc(0, 8), psf(4, E0), ST("rstd0", tc), None, ALU.mult),
                waits=[pr.now("t"), ev_rs])
            bank_free[4] = ev_a2
            ev_lg = pr.op("v", lambda e, tst=tst, gsc=gsc: e.scalar_tensor_tensor(
                gsc(0, 8), self.csb8[:], tst(4), gsc(0, 8), ALU.mult, ALU.add),
                waits=[ev_a2, ev_nm, ev_csb])
            ev_lg = pr.op("v", lambda e, gsc=gsc: e.tensor_tensor(gsc(0, 8), gsc(0, 8), self.bgb8[:], ALU.add),
                          waits=[ev_lg, ev_bgb])
            ev_m8 = pr.op("v", lambda e, gsc=gsc: e.max(out=gsc(8, 8), in_=gsc(0, 8)), waits=[ev_lg])
            ev_nv = pr.op("v", lambda e, tst=tst, gsc=gsc: e.tensor_scalar(tst(5), gsc(8, 1), -1.0, None, ALU.mult),
                          waits=[ev_m8])
            ev_ex = pr.op("s", lambda e, tst=tst, gsc=gsc: e.activation(gsc(16, 8), gsc(0, 8), AF.Exp, bias=tst(5)),
                          waits=[ev_nv, ev_lg])
            ev_in = pr.op("v", lambda e, gsc=gsc: e.tensor_scalar(gsc(24, 8), gsc(0, 8), gsc(9, 1), None, ALU.is_ge),
                          waits=[ev_m8])
            ev_wr = pr.op("v", lambda e, gsc=gsc: e.tensor_tensor(gsc(24, 8), gsc(16, 8), gsc(24, 8), ALU.mult),
                          waits=[ev_ex, ev_in])
            ev_s2 = pr.op("v", lambda e, tst=tst, gsc=gsc: e.reduce_sum(tst(6), gsc(24, 8), axis=AX.X), waits=[ev_wr])
            ev_rc = pr.op("v", lambda e, tst=tst: e.reciprocal(tst(7), tst(6)), waits=[ev_s2])
            pr.op("v", lambda e, tc=tc, tst=tst, gsc=gsc: e.tensor_scalar(
                self.comb0[:, tc * E0:(tc + 1) * E0], gsc(24, 8), tst(7), None, ALU.mult),
                waits=[ev_rc])

        emit_load(0)
        emit_load(1)
        for tc in range(NTC):
            if tc + 2 < NTC:
                emit_load(tc + 2)
            emit_stats(tc)
            emit_tx(tc)
            if tc >= 1:
                emit_txn(tc - 1)
                emit_router(tc - 1)
        emit_txn(NTC - 1)
        emit_router(NTC - 1)
        self.bank_free = bank_free

    def phase_out(self):
        pr = self.pr
        psf, X = self.psf, self.X
        otm = [self.pv(0, F32, 1024), self.pv(4096, F32, 1024)]
        prev_store = [None, None]
        for tc in range(NTC):
            sl = tc % 2
            evs = []
            for dc in range(NDC):
                bk = dc % 2
                ev_t = pr.op("t", lambda e, dc=dc, tc=tc, bk=bk: e.transpose(
                    psf(bk, P), X(dc, tc * P, (tc + 1) * P), self.id_f[:]),
                    waits=[self.bank_free.get(bk)])
                ev_c = pr.op("v", lambda e, dc=dc, bk=bk, sl=sl: e.tensor_copy(
                    otm[sl][:, dc * P:(dc + 1) * P], psf(bk, P)),
                    waits=[ev_t, prev_store[sl]])
                self.bank_free[bk] = ev_c
                evs.append(ev_c)
            prev_store[sl] = pr.op("y", lambda e, tc=tc, sl=sl: e.dma_start(
                out=self.out_h[tc * P:(tc + 1) * P, :], in_=otm[sl]), sig="so%d" % sl,
                sigv=16, waits=evs)

        tapsems = []
        for i, (tn, (shape, fn)) in enumerate(self.taps.items()):
            ap = fn(self)
            sg = "sT%d" % i
            pr.op("g", lambda e, ap=ap, tn=tn: e.dma_start(
                out=self.tap_outs[tn], in_=ap, allow_slow_non_contiguous=True),
                sig=sg, sigv=16,
                waits=[pr.now("v"), pr.now("s"), pr.now("t")])
            tapsems.append((sg, 16))
        pr.wait("y", prev_store[0], prev_store[1], *tapsems)

    def build(self):
        self.phase0()
        self.pr.barrier("tvsgy")
        if self.stage >= 1:
            self.l0_attn()
            self.pr.barrier("tvsgy")
        if self.stage >= 2:
            self.l0_ffn()
            self.pr.barrier("tvsgy")
        if self.stage >= 3:
            self.ln1()
            self.pr.barrier("tvsgy")
        if self.stage >= 4:
            self.l1_ffn()
            self.pr.barrier("tvsgy")
        if self.stage >= 5:
            self.l1_attn()
            self.pr.barrier("tvsgy")
        self.phase_out()
        with self.nc.Block() as block:
            self.pr.flush(block)
        self.ctx.close()
        for name, (h, cnt) in self.pr.sems.items():
            assert cnt < 60000, (name, cnt)
        return self.nc

    def l0_attn(self):
        pr, dr = self.pr, self.dr
        psf, psb, X, XN, CV, CRB = self.psf, self.psb, self.X, self.XN, self.CV, self.CRB
        EH = INNER // 2

        win = self.pv(0, BF16, 8 * 128)
        wv = self.pv(2048, BF16, 8 * 1024)
        wg = self.pv(18432, BF16, 8 * 1024)
        wo = self.pv(34816, BF16, 8 * 1024)
        xf = self.pv(51200, BF16, 256)
        qq = self.pv(51712, BF16, 256)
        qk = self.pv(52224, BF16, 256)
        lq = self.pv(52736, BF16, 256)
        lkf = self.pv(53248, BF16, 256)
        lkt = self.pv(53760, BF16, 256)
        at = self.pv(54272, BF16, 512)
        vt = self.pv(55296, BF16, 2048)
        gate = self.pv(59392, BF16, 2048)
        osb = self.pv(63488, BF16, 2048)
        KV = self.pv(67584, F32, 1024)
        KVb = self.pv(71680, BF16, 1024)
        rel = self.pv(73728, F32, 256)

        for dc in range(NDC):
            pr.op("s", lambda e, dc=dc: e.activation(
                X(dc), X(dc), AF.Identity, bias=CV("bout", dc)), waits=[self.ECY])

        for dc in range(NDC):
            pr.op("g", lambda e, dc=dc: e.dma_start(
                out=win[:, dc * 128:(dc + 1) * 128],
                in_=dr["L0_attn_Win"][dc * P:(dc + 1) * P, :]), sig="dwin", sigv=16)
        ev_win = ("dwin", pr.sems["dwin"][1])

        wfree = {"wv": None, "wg": None, "wo": None}
        for eh in range(2):
            for dc in range(NDC):
                pr.op("g", lambda e, dc=dc, eh=eh: e.dma_start(
                    out=wv[:, dc * 1024:(dc + 1) * 1024],
                    in_=dr["L0_attn_Wv"][dc * P:(dc + 1) * P, eh * EH:(eh + 1) * EH]),
                    sig="dwv", sigv=16, waits=[wfree["wv"]])
            ev_wv = ("dwv", pr.sems["dwv"][1])
            for dc in range(NDC):
                pr.op("g", lambda e, dc=dc, eh=eh: e.dma_start(
                    out=wg[:, dc * 1024:(dc + 1) * 1024],
                    in_=dr["L0_attn_Wg"][dc * P:(dc + 1) * P, eh * EH:(eh + 1) * EH]),
                    sig="dwg", sigv=16, waits=[wfree["wg"]])
            ev_wg = ("dwg", pr.sems["dwg"][1])
            for ec in range(8):
                pr.op("g", lambda e, ec=ec, eh=eh: e.dma_start(
                    out=wo[:, ec * 1024:(ec + 1) * 1024],
                    in_=dr["L0_attn_Wout"][eh * EH + ec * P: eh * EH + (ec + 1) * P, :]),
                    sig="dwo", sigv=16, waits=[wfree["wo"]])
            ev_wo = ("dwo", pr.sems["dwo"][1])

            ev_kvb_prev = None
            vl_reads = []
            free = {k: None for k in ("xf", "qq", "qk", "lq", "lkf", "lkt",
                                      "at", "vt", "gate", "osb", "rel")}
            last_wv_mm = None
            last_wg_mm = None
            last_wo_mm = None
            for g in range(NG):
                c0 = g * GRP
                for dc in range(NDC):
                    ev_mm = pr.op("t", lambda e, dc=dc, c0=c0: e.matmul(
                        psf(0, 256), win[:, dc * 128:(dc + 1) * 128],
                        XN(dc, c0, c0 + GRP),
                        start=(dc == 0), stop=(dc == NDC - 1)),
                        waits=[ev_win, free["xf"], self.bank_free.get(0)])
                ev_xf = pr.op("s", lambda e: e.activation(
                    xf, psf(0, 256), AF.Silu, bias=CV("bin_")), waits=[ev_mm])
                ev_qq = pr.op("v", lambda e: e.tensor_scalar(
                    qq, xf, CV("g_qq"), CV("b_qq"), ALU.mult, ALU.add),
                    waits=[ev_xf, free["qq"]])
                ev_qk = pr.op("v", lambda e: e.tensor_scalar(
                    qk, xf, CV("g_qk"), CV("b_qk"), ALU.mult, ALU.add),
                    waits=[ev_xf, free["qk"]])
                ev_lq = pr.op("v", lambda e: e.tensor_scalar(
                    lq, xf, CV("g_lq"), CV("b_lq"), ALU.mult, ALU.add),
                    waits=[ev_xf, free["lq"]])
                ev_lkf = pr.op("v", lambda e: e.tensor_scalar(
                    lkf, xf, CV("g_lk"), CV("b_lk"), ALU.mult, ALU.add),
                    waits=[ev_xf, free["lkf"]])
                free["xf"] = pr.now("v")
                ev_lkt = []
                for cc in range(2):
                    bk = 2 + cc
                    ev_tt = pr.op("t", lambda e, cc=cc, bk=bk: e.transpose(
                        psb(bk, 128), lkf[:, cc * 128:(cc + 1) * 128], self.id_b[:]),
                        waits=[ev_lkf, self.bank_free.get(bk), self.ECG])
                    ev_cp = pr.op("v", lambda e, cc=cc, bk=bk: e.tensor_copy(
                        lkt[:, cc * 128:(cc + 1) * 128], psb(bk, 128)),
                        waits=[ev_tt, free["lkt"]])
                    self.bank_free[bk] = ev_cp
                    ev_lkt.append(ev_cp)
                free["lkf"] = pr.now("t")
                ev_at = []
                for mc in range(2):
                    bk = mc
                    ev_mm = pr.op("t", lambda e, mc=mc, bk=bk: e.matmul(
                        psf(bk, 256), qk[:, mc * 128:(mc + 1) * 128], qq[:],
                        start=True, stop=True),
                        waits=[ev_qq, ev_qk, self.bank_free.get(bk), free["at"] if mc == 0 else None])
                    ev_rl = pr.op("s", lambda e, bk=bk: e.activation(
                        rel, psf(bk, 256), AF.Relu, scale=INV_S2048),
                        waits=[ev_mm, free["rel"]])
                    self.bank_free[bk] = ev_rl
                    ev_sq = pr.op("v", lambda e, mc=mc: e.tensor_tensor(
                        at[:, mc * 256:(mc + 1) * 256], rel, rel, ALU.mult),
                        waits=[ev_rl, free["at"]])
                    free["rel"] = ev_sq
                    ev_mk = pr.op("v", lambda e, mc=mc: e.tensor_tensor(
                        at[:, mc * 256:(mc + 1) * 256], at[:, mc * 256:(mc + 1) * 256],
                        self.mbc[:, MZ - 128 * mc: MZ - 128 * mc + 256], ALU.mult),
                        waits=[ev_sq])
                    ev_at.append(ev_mk)
                free["qq"] = pr.now("t")
                free["qk"] = pr.now("t")
                ev_vt = []
                for cc in range(2):
                    for ti in range(2):
                        bk = 2 + (2 * cc + ti) % 2
                        for dc in range(NDC):
                            ev_mm = pr.op("t", lambda e, dc=dc, cc=cc, ti=ti, c0=c0, bk=bk: e.matmul(
                                psf(bk, 512), XN(dc, c0 + cc * P, c0 + (cc + 1) * P),
                                wv[:, dc * 1024 + ti * 512: dc * 1024 + ti * 512 + 512],
                                start=(dc == 0), stop=False),
                                waits=[ev_wv, self.bank_free.get(bk), free["vt"] if (cc == 0 and ti == 0) else None])
                            last_wv_mm = ev_mm
                        ev_mm = pr.op("t", lambda e, cc=cc, ti=ti, eh=eh, bk=bk: e.matmul(
                            psf(bk, 512), CRB("ones", 0, P),
                            CRB("bv", eh * EH + ti * 512, 512),
                            start=False, stop=True),
                            waits=[self.ECG])
                        last_wv_mm = ev_mm
                        ev_ev = pr.op("s", lambda e, cc=cc, ti=ti, bk=bk: e.activation(
                            vt[:, cc * 1024 + ti * 512: cc * 1024 + ti * 512 + 512],
                            psf(bk, 512), AF.Silu), waits=[ev_mm, free["vt"]])
                        self.bank_free[bk] = ev_ev
                        ev_vt.append(ev_ev)
                ev_gate = []
                for ec in range(8):
                    bk = 4 + ec % 2
                    for dc in range(NDC):
                        ev_mm = pr.op("t", lambda e, dc=dc, ec=ec, c0=c0, bk=bk: e.matmul(
                            psf(bk, 256), wg[:, dc * 1024 + ec * 128: dc * 1024 + ec * 128 + 128],
                            XN(dc, c0, c0 + GRP),
                            start=(dc == 0), stop=(dc == NDC - 1)),
                            waits=[ev_wg, self.bank_free.get(bk), free["gate"] if ec < 2 else None])
                        last_wg_mm = ev_mm
                    ev_ev = pr.op("s", lambda e, ec=ec, eh=eh, bk=bk: e.activation(
                        gate[:, ec * 256:(ec + 1) * 256], psf(bk, 256), AF.Silu,
                        bias=CV("bg", eh * 8 + ec)), waits=[ev_mm, free["gate"]])
                    self.bank_free[bk] = ev_ev
                    ev_gate.append(ev_ev)
                ev_kv = []
                for ti in range(2):
                    bk = 6 + ti
                    for cc in range(2):
                        ev_mm = pr.op("t", lambda e, cc=cc, ti=ti, bk=bk: e.matmul(
                            psf(bk, 512), lkt[:, cc * 128:(cc + 1) * 128],
                            vt[:, cc * 1024 + ti * 512: cc * 1024 + ti * 512 + 512],
                            start=(cc == 0), stop=(cc == 1)),
                            waits=[ev_lkt[cc], ev_vt[2 * cc + ti], self.bank_free.get(bk)])
                    ev_kv.append(ev_mm)
                free["lkt"] = pr.now("t")
                vl_reads = []
                ev_osb = []
                for ec in range(8):
                    bk = ec % 2
                    ev_m0 = pr.op("t", lambda e, ec=ec, bk=bk: e.matmul(
                        psf(bk, 256), vt[:, 0 * 1024 + ec * 128: 0 * 1024 + ec * 128 + 128],
                        at[:, 0:256], start=True, stop=False),
                        waits=[ev_at[0], ev_vt[0], ev_vt[1], self.bank_free.get(bk),
                               free["osb"] if ec < 2 else None])
                    ev_m1 = pr.op("t", lambda e, ec=ec, bk=bk, st=(g == 0): e.matmul(
                        psf(bk, 256), vt[:, 1 * 1024 + ec * 128: 1 * 1024 + ec * 128 + 128],
                        at[:, 256:512], start=False, stop=st),
                        waits=[ev_at[1], ev_vt[2], ev_vt[3]])
                    if g > 0:
                        ev_m1 = pr.op("t", lambda e, ec=ec, bk=bk: e.matmul(
                            psf(bk, 256), KVb[:, ec * 128:(ec + 1) * 128], lq[:],
                            start=False, stop=True),
                            waits=[ev_kvb_prev, ev_lq])
                        vl_reads.append(ev_m1)
                    ev_ev = pr.op("v", lambda e, ec=ec, bk=bk: e.tensor_tensor(
                        osb[:, ec * 256:(ec + 1) * 256], psf(bk, 256),
                        gate[:, ec * 256:(ec + 1) * 256], ALU.mult),
                        waits=[ev_m1, ev_gate[ec], free["osb"]])
                    self.bank_free[bk] = ev_ev
                    ev_osb.append(ev_ev)
                free["lq"] = pr.now("t")
                free["at"] = pr.now("t")
                free["vt"] = pr.now("t")
                free["gate"] = pr.now("v")
                if g == 0:
                    ev_u0 = pr.op("v", lambda e: e.tensor_copy(KV[:, 0:512], psf(6, 512)),
                                  waits=[ev_kv[0]])
                    ev_u1 = pr.op("v", lambda e: e.tensor_copy(KV[:, 512:1024], psf(7, 512)),
                                  waits=[ev_kv[1]])
                else:
                    ev_u0 = pr.op("v", lambda e: e.tensor_tensor(
                        KV[:, 0:512], KV[:, 0:512], psf(6, 512), ALU.add),
                        waits=[ev_kv[0]])
                    ev_u1 = pr.op("v", lambda e: e.tensor_tensor(
                        KV[:, 512:1024], KV[:, 512:1024], psf(7, 512), ALU.add),
                        waits=[ev_kv[1]])
                self.bank_free[6] = ev_u0
                self.bank_free[7] = ev_u1
                if g < NG - 1:
                    ev_kvb_prev = pr.op("v", lambda e: e.tensor_copy(KVb[:], KV[:]),
                                        waits=[ev_u0, ev_u1] + vl_reads)
                ev_wouts = []
                for dc in range(NDC):
                    bk = 2 + dc % 2
                    for ec in range(8):
                        ev_mm = pr.op("t", lambda e, dc=dc, ec=ec, bk=bk: e.matmul(
                            psf(bk, 256), wo[:, ec * 1024 + dc * 128: ec * 1024 + dc * 128 + 128],
                            osb[:, ec * 256:(ec + 1) * 256],
                            start=(ec == 0), stop=(ec == 7)),
                            waits=[ev_wo, ev_osb[ec], self.bank_free.get(bk)])
                        last_wo_mm = ev_mm
                    ev_ad = pr.op("v", lambda e, dc=dc, c0=c0, bk=bk: e.tensor_tensor(
                        X(dc, c0, c0 + GRP), X(dc, c0, c0 + GRP), psf(bk, 256), ALU.add),
                        waits=[ev_mm])
                    self.bank_free[bk] = ev_ad
                    ev_wouts.append(ev_ad)
                free["osb"] = pr.now("t")
                free["qq"] = free["qk"] = free["lq"] = pr.now("t")
            wfree = {"wv": last_wv_mm, "wg": last_wg_mm, "wo": last_wo_mm}

    def l0_ffn(self):
        pr, dr = self.pr, self.dr
        psf, psb, X, XN, CV, CRB = self.psf, self.psb, self.X, self.XN, self.CV, self.CRB

        acc = self.pv(0, F32, 8 * 1024)
        hp = self.pv(32768, BF16, 16 * 1024)
        w1b = [self.pv(65536, BF16, 8 * 256), self.pv(69632, BF16, 8 * 256)]
        w2b = [self.pv(73728, BF16, 2048), self.pv(77824, BF16, 2048),
               self.w2b_s[:]]
        bc = self.bc_s[:]
        combrow = self.combrow_s[0:1, :]

        for th in range(2):
            t0 = th * 1024
            first = [True] * NDC
            w1free = [None, None]
            w2free = [None, None, None]
            w1n = 0
            w2n = 0
            hp_free = None
            bc_free = None
            cr_free = None
            for e in range(E0):
                ev_cr = []
                for tcl in range(8):
                    tcg = th * 8 + tcl
                    bkr = 6
                    ev_tt = pr.op("t", lambda e2, tcg=tcg, e=e, bkr=bkr: e2.transpose(
                        psf(bkr, P)[0:1, :], self.comb0[:, tcg * E0 + e: tcg * E0 + e + 1],
                        self.id_f[:]),
                        waits=[self.bank_free.get(bkr), cr_free])
                    ev_cp = pr.op("v", lambda e2, tcl=tcl, bkr=bkr: e2.tensor_copy(
                        combrow[0:1, tcl * P:(tcl + 1) * P], psf(bkr, P)[0:1, :]),
                        waits=[ev_tt, cr_free])
                    self.bank_free[bkr] = ev_cp
                    ev_cr.append(ev_cp)
                ev_bc = []
                for tt in range(2):
                    bkr = 7
                    ev_mm = pr.op("t", lambda e2, tt=tt, bkr=bkr: e2.matmul(
                        psf(bkr, 512), CRB("ones", 0, P), combrow[0:1, tt * 512: tt * 512 + 512],
                        start=True, stop=True),
                        waits=ev_cr + [self.bank_free.get(bkr), self.ECG])
                    cr_free = ev_mm
                    ev_cp = pr.op("v", lambda e2, tt=tt, bkr=bkr: e2.tensor_copy(
                        bc[:, tt * 512: tt * 512 + 512], psf(bkr, 512)),
                        waits=[ev_mm, bc_free])
                    self.bank_free[bkr] = ev_cp
                    ev_bc.append(ev_cp)
                for hh in range(2):
                    ev_hp = {}
                    for hs in range(8):
                        sl = w1n % 2
                        ev_w1 = pr.op("g", lambda e2, e=e, hh=hh, hs=hs, sl=sl: e2.dma_start(
                            out=w1b[sl].rearrange("p (dc h) -> p dc h", h=256),
                            in_=dr["L0_ffn_W1"][e, :, hh * 2048 + hs * 256: hh * 2048 + (hs + 1) * 256].rearrange(
                                "(dc p) h -> p dc h", p=P)),
                            sig="dw1%d" % sl, sigv=16, waits=[w1free[sl]])
                        w1n += 1
                        last_mm = None
                        for hc2 in range(2):
                            hcl = hs * 2 + hc2
                            for tt in range(2):
                                bk = (2 * hc2 + tt) % 2
                                for dc in range(NDC):
                                    ev_mm = pr.op("t", lambda e2, dc=dc, hc2=hc2, tt=tt, t0=t0, sl=sl, bk=bk: e2.matmul(
                                        psf(bk, 512),
                                        w1b[sl][:, dc * 256 + hc2 * 128: dc * 256 + hc2 * 128 + 128],
                                        XN(dc, t0 + tt * 512, t0 + (tt + 1) * 512),
                                        start=(dc == 0), stop=(dc == NDC - 1)),
                                        waits=[ev_w1, self.bank_free.get(bk), hp_free if hcl == 0 and tt == 0 else None])
                                last_mm = ev_mm
                                ev_gl = pr.op("s", lambda e2, hcl=hcl, tt=tt, e=e, hh=hh, bk=bk: e2.activation(
                                    hp[:, hcl * 1024 + tt * 512: hcl * 1024 + tt * 512 + 512],
                                    psf(bk, 512), AF.Gelu, bias=CV("b1", e * 32 + hh * 16 + hcl)),
                                    waits=[ev_mm, hp_free])
                                self.bank_free[bk] = ev_gl
                                ev_tt2 = pr.op("v", lambda e2, hcl=hcl, tt=tt: e2.tensor_tensor(
                                    hp[:, hcl * 1024 + tt * 512: hcl * 1024 + tt * 512 + 512],
                                    hp[:, hcl * 1024 + tt * 512: hcl * 1024 + tt * 512 + 512],
                                    bc[:, tt * 512: tt * 512 + 512], ALU.mult),
                                    waits=[ev_gl, ev_bc[tt]])
                                ev_hp[(hcl, tt)] = ev_tt2
                        bc_free = ev_tt2
                        w1free[sl] = last_mm
                    for tt in range(2):
                        ev_dc = {}
                        for hp2 in range(8):
                            sl = w2n % 3
                            ev_w2 = pr.op("g", lambda e2, e=e, hh=hh, hp2=hp2, sl=sl: e2.dma_start(
                                out=w2b[sl].rearrange("p (j c) -> p j c", c=1024),
                                in_=dr["L0_ffn_W2"][e, hh * 2048 + hp2 * 256: hh * 2048 + (hp2 + 1) * 256, :].rearrange(
                                    "(j p) c -> p j c", p=P)),
                                sig="dw2%d" % sl, sigv=16, waits=[w2free[sl]])
                            w2n += 1
                            for j in range(2):
                                hcl = hp2 * 2 + j
                                for dc in range(NDC):
                                    ev_mm = pr.op("t", lambda e2, dc=dc, hcl=hcl, tt=tt, sl=sl, j=j: e2.matmul(
                                        psf(dc, 512), w2b[sl][:, j * 1024 + dc * 128: j * 1024 + dc * 128 + 128],
                                        hp[:, hcl * 1024 + tt * 512: hcl * 1024 + tt * 512 + 512],
                                        start=(hcl == 0), stop=(hcl == 15)),
                                        waits=[ev_w2, ev_hp[(hcl, tt)],
                                               self.bank_free.get(dc) if hcl == 0 else None])
                                    ev_dc[dc] = ev_mm
                            w2free[sl] = ev_mm
                        hp_free = ev_mm
                        for dc in range(NDC):
                            if first[dc] and tt == 0:
                                pass
                            if first[dc]:
                                ev_ad = pr.op("v", lambda e2, dc=dc, tt=tt, t0=t0: e2.tensor_copy(
                                    acc[:, dc * 1024 + tt * 512: dc * 1024 + tt * 512 + 512],
                                    psf(dc, 512)), waits=[ev_dc[dc]])
                            else:
                                ev_ad = pr.op("v", lambda e2, dc=dc, tt=tt, t0=t0: e2.tensor_tensor(
                                    acc[:, dc * 1024 + tt * 512: dc * 1024 + tt * 512 + 512],
                                    acc[:, dc * 1024 + tt * 512: dc * 1024 + tt * 512 + 512],
                                    psf(dc, 512), ALU.add), waits=[ev_dc[dc]])
                            self.bank_free[dc] = ev_ad
                        if tt == 1:
                            for dc in range(NDC):
                                first[dc] = False
                    if hh == 1:
                        for dc in range(NDC):
                            first[dc] = False
            comb8 = self.pv(32768, BF16, 1024, 0, E0)
            b2w = self.pv(36864, BF16, 1024, 0, E0)
            ev_b2l = pr.op("g", lambda e2: e2.dma_start(out=b2w, in_=dr["L0_ffn_b2"]),
                           sig="db2", sigv=16, waits=[hp_free])
            ev_c8 = []
            for tcl in range(8):
                tc = th * 8 + tcl
                bkr = 2 + tcl % 2
                ev_tt3 = pr.op("t", lambda e2, tc=tc, bkr=bkr: e2.transpose(
                    psf(bkr, 128)[0:E0, :], self.comb0[:, tc * E0:(tc + 1) * E0],
                    self.id_f[:]),
                    waits=[self.bank_free.get(bkr), hp_free])
                ev_cp3 = pr.op("v", lambda e2, tcl=tcl, bkr=bkr: e2.tensor_copy(
                    comb8[:, tcl * P:(tcl + 1) * P], psf(bkr, 128)[0:E0, :]),
                    waits=[ev_tt3, hp_free])
                self.bank_free[bkr] = ev_cp3
                ev_c8.append(ev_cp3)
            for dco in range(NDC):
                for tt in range(2):
                    bkr = 4 + (2 * dco + tt) % 2
                    ev_mmb = pr.op("t", lambda e2, dco=dco, tt=tt, bkr=bkr: e2.matmul(
                        psf(bkr, 512), b2w[:, dco * 128:(dco + 1) * 128],
                        comb8[:, tt * 512:(tt + 1) * 512],
                        start=True, stop=True),
                        waits=ev_c8 + [ev_b2l, self.bank_free.get(bkr)])
                    ev_ab = pr.op("v", lambda e2, dco=dco, tt=tt, bkr=bkr: e2.tensor_tensor(
                        acc[:, dco * 1024 + tt * 512: dco * 1024 + tt * 512 + 512],
                        acc[:, dco * 1024 + tt * 512: dco * 1024 + tt * 512 + 512],
                        psf(bkr, 512), ALU.add),
                        waits=[ev_mmb, self.bank_free.get(dco)])
                    self.bank_free[bkr] = ev_ab
            ev_r = []
            for dc in range(NDC):
                ev = pr.op("v", lambda e2, dc=dc, t0=t0: e2.tensor_tensor(
                    acc[:, dc * 1024:(dc + 1) * 1024],
                    acc[:, dc * 1024:(dc + 1) * 1024],
                    XN(dc, t0, t0 + 1024), ALU.add), waits=[pr.now("v")])
                ev_r.append(ev)
            rtm = self.pv(32768, F32, 1024)
            lntm = self.pv(36864, F32, 1024)
            ttmp = self.pv(40960, F32, 128)
            sc9 = self.pv(41472, F32, 8)
            for tcl in range(8):
                tc = th * 8 + tcl
                evs_t = []
                for dc in range(NDC):
                    bk = dc % 2
                    ev_tt = pr.op("t", lambda e2, dc=dc, tcl=tcl, bk=bk: e2.transpose(
                        psf(bk, P), acc[:, dc * 1024 + tcl * P: dc * 1024 + (tcl + 1) * P],
                        self.id_f[:]),
                        waits=[ev_r[dc], self.bank_free.get(bk)])
                    ev_cp = pr.op("v", lambda e2, dc=dc, bk=bk: e2.tensor_copy(
                        rtm[:, dc * P:(dc + 1) * P], psf(bk, P)), waits=[ev_tt])
                    self.bank_free[bk] = ev_cp
                    evs_t.append(ev_cp)
                ev_sq = pr.op("s", lambda e2: e2.activation(
                    lntm, rtm, AF.Square, accum_out=sc9[:, 0:1]), waits=evs_t)
                ev_sm = pr.op("v", lambda e2: e2.reduce_sum(sc9[:, 1:2], rtm, axis=AX.X), waits=evs_t)
                ev_mu = pr.op("v", lambda e2: e2.tensor_scalar(
                    sc9[:, 2:3], sc9[:, 1:2], 1.0 / D, None, ALU.mult), waits=[ev_sm])
                ev_e2 = pr.op("v", lambda e2: e2.tensor_scalar(
                    sc9[:, 3:4], sc9[:, 0:1], 1.0 / D, None, ALU.mult), waits=[ev_sq])
                ev_ms = pr.op("v", lambda e2: e2.tensor_tensor(
                    sc9[:, 4:5], sc9[:, 2:3], sc9[:, 2:3], ALU.mult), waits=[ev_mu])
                ev_va = pr.op("v", lambda e2: e2.tensor_tensor(
                    sc9[:, 3:4], sc9[:, 3:4], sc9[:, 4:5], ALU.subtract), waits=[ev_e2, ev_ms])
                ev_sd = pr.op("s", lambda e2: e2.activation(
                    sc9[:, 4:5], sc9[:, 3:4], AF.Sqrt, bias=CV("eps5")), waits=[ev_va])
                ev_rs = pr.op("v", lambda e2: e2.reciprocal(sc9[:, 5:6], sc9[:, 4:5]), waits=[ev_sd])
                ev_ln = pr.op("v", lambda e2: e2.tensor_scalar(
                    lntm, rtm, sc9[:, 2:3], sc9[:, 5:6], ALU.subtract, ALU.mult),
                    waits=[ev_rs, ev_sq])
                for dc in range(NDC):
                    bk = 2 + dc % 2
                    ev_tt = pr.op("t", lambda e2, dc=dc, bk=bk: e2.transpose(
                        psf(bk, P), lntm[:, dc * P:(dc + 1) * P], self.id_f[:]),
                        waits=[ev_ln, self.bank_free.get(bk)])
                    ev_gm = pr.op("s", lambda e2, dc=dc, bk=bk: e2.activation(
                        ttmp, psf(bk, P), AF.Identity, scale=CV("gamma", dc), bias=CV("beta", dc)),
                        waits=[ev_tt])
                    self.bank_free[bk] = ev_gm
                    ev_ax = pr.op("v", lambda e2, dc=dc, tc=tc: e2.tensor_tensor(
                        X(dc, tc * P, (tc + 1) * P), X(dc, tc * P, (tc + 1) * P),
                        ttmp, ALU.add), waits=[ev_gm])
                    pr.wait("s", ev_ax)

    def ln1(self):
        pr = self.pr
        psf, psb, X, XN, CV, ST = self.psf, self.psb, self.X, self.XN, self.CV, self.ST
        CRB = self.CRB
        self.ev_w1tm_ln1 = {}
        xtm = [self.pv(0, F32, 1024), self.pv(4096, F32, 1024)]
        xnm = [self.pv(8192, BF16, 1024), self.pv(10240, BF16, 1024)]
        dump = [self.pv(12288, F32, 1024), self.pv(16384, F32, 1024)]
        slot_free = [None, None]
        slot_vfree = [None, None]
        xnm_free = [None, None]
        for tc in range(NTC):
            sl = tc % 2
            g0 = self.pv(20480 + tc * 64, F32, 16)
            tst = lambda j, n=1, g0=g0: g0[:, j:j + n]
            evs_w = []
            for dc in range(NDC):
                bk = dc % 2
                ev_tt = pr.op("t", lambda e, dc=dc, tc=tc, bk=bk: e.transpose(
                    psf(bk, P), X(dc, tc * P, (tc + 1) * P), self.id_f[:]),
                    waits=[self.bank_free.get(bk)])
                ev_cp = pr.op("v", lambda e, dc=dc, bk=bk, sl=sl: e.tensor_copy(
                    xtm[sl][:, dc * P:(dc + 1) * P], psf(bk, P)),
                    waits=[ev_tt, slot_free[sl], slot_vfree[sl]])
                self.bank_free[bk] = ev_cp
                evs_w.append(ev_cp)
            ev_sq = pr.op("s", lambda e, sl=sl, tst=tst: e.activation(
                dump[sl], xtm[sl], AF.Square, accum_out=tst(0)), waits=evs_w)
            ev_sm = pr.op("v", lambda e, sl=sl, tst=tst: e.reduce_sum(tst(1), xtm[sl], axis=AX.X),
                          waits=evs_w)
            ev_mu = pr.op("v", lambda e, tc=tc, tst=tst: e.tensor_scalar(
                ST("mu1", tc), tst(1), 1.0 / D, None, ALU.mult), waits=[ev_sm])
            ev_e2 = pr.op("v", lambda e, tst=tst: e.tensor_scalar(tst(2), tst(0), 1.0 / D, None, ALU.mult),
                          waits=[ev_sq])
            ev_ms = pr.op("v", lambda e, tc=tc, tst=tst: e.tensor_tensor(
                tst(3), ST("mu1", tc), ST("mu1", tc), ALU.mult), waits=[ev_mu])
            ev_va = pr.op("v", lambda e, tc=tc, tst=tst: e.tensor_tensor(
                ST("var1", tc), tst(2), tst(3), ALU.subtract), waits=[ev_e2, ev_ms])
            ev_sd = pr.op("s", lambda e, tc=tc, tst=tst: e.activation(
                tst(3), ST("var1", tc), AF.Sqrt, bias=CV("eps5")), waits=[ev_va])
            ev_rs = pr.op("v", lambda e, tc=tc, tst=tst: e.reciprocal(ST("rstd1", tc), tst(3)),
                          waits=[ev_sd])
            ev_xn = pr.op("v", lambda e, sl=sl, tc=tc: e.tensor_scalar(
                xnm[sl], xtm[sl], ST("mu1", tc), ST("rstd1", tc), ALU.subtract, ALU.mult),
                waits=[ev_rs, xnm_free[sl]])
            evs_c2 = []
            for dc in range(NDC):
                bk2 = 2 + dc % 2
                ev_t2 = pr.op("t", lambda e, sl=sl, dc=dc, bk2=bk2: e.transpose(
                    psb(bk2, P), xnm[sl][:, dc * P:(dc + 1) * P], self.id_b[:]),
                    waits=[ev_xn, self.bank_free.get(bk2)])
                ev_c2 = pr.op("s", lambda e, dc=dc, tc=tc, bk2=bk2: e.copy(
                    XN(dc, tc * P, (tc + 1) * P), psb(bk2, P)), waits=[ev_t2])
                self.bank_free[bk2] = ev_c2
                evs_c2.append(ev_c2)
            xnm_free[sl] = pr.now("t")
            slot_free[sl] = pr.now("s")
            slot_vfree[sl] = ev_xn
            g0g = self.pv(60416 + tc * 192, F32, 48)
            gc = lambda j, n=1, g0g=g0g: g0g[:, j:j + n]
            bkg = 4 + tc % 2
            for dc in range(NDC):
                ev_mm = pr.op("t", lambda e, dc=dc, tc=tc, bkg=bkg: e.matmul(
                    psf(bkg, E1), XN(dc, tc * P, (tc + 1) * P),
                    self.rwt[:, dc * E1:(dc + 1) * E1],
                    start=(dc == 0), stop=False),
                    waits=[evs_c2[dc], self.bank_free.get(bkg)])
            ev_mm = pr.op("t", lambda e, tc=tc, bkg=bkg: e.matmul(
                psf(bkg, E1), CRB("ones", 0, P), CRB("rb"),
                start=False, stop=True), waits=[self.ECG])
            ev_mx = pr.op("v", lambda e, gc=gc, bkg=bkg: e.reduce_max(gc(0), psf(bkg, E1), axis=AX.X),
                          waits=[ev_mm])
            ev_nm = pr.op("v", lambda e, gc=gc: e.tensor_scalar(gc(0), gc(0), -1.0, None, ALU.mult),
                          waits=[ev_mx])
            ev_ex = pr.op("s", lambda e, bkg=bkg, gc=gc: e.activation(
                gc(1, 16), psf(bkg, E1), AF.Exp, bias=gc(0), accum_out=gc(17)),
                waits=[ev_nm])
            self.bank_free[bkg] = ev_ex
            ev_rs2 = pr.op("v", lambda e, gc=gc: e.reciprocal(gc(18), gc(17)), waits=[ev_ex])
            ev_p = pr.op("v", lambda e, gc=gc: e.tensor_scalar(
                gc(1, 16), gc(1, 16), gc(18), None, ALU.mult), waits=[ev_rs2])
            ev_m8 = pr.op("v", lambda e, gc=gc: e.max(out=gc(19, 8), in_=gc(1, 16)), waits=[ev_p])
            ev_in = pr.op("v", lambda e, gc=gc: e.tensor_scalar(
                gc(27, 16), gc(1, 16), gc(24), None, ALU.is_ge), waits=[ev_m8])
            ev_wr = pr.op("v", lambda e, gc=gc: e.tensor_tensor(
                gc(27, 16), gc(27, 16), gc(1, 16), ALU.mult), waits=[ev_in])
            ev_s = pr.op("v", lambda e, gc=gc: e.reduce_sum(gc(43), gc(27, 16), axis=AX.X),
                         waits=[ev_wr])
            ev_sp = pr.op("v", lambda e, gc=gc: e.tensor_scalar(gc(44), gc(43), 1e-10, None, ALU.add),
                          waits=[ev_s])
            ev_r2 = pr.op("v", lambda e, gc=gc: e.reciprocal(gc(45), gc(44)), waits=[ev_sp])
            self.ev_w1tm_ln1[tc] = pr.op("v", lambda e, tc=tc, gc=gc: e.tensor_scalar(
                self.w1tm[:, tc * E1:(tc + 1) * E1], gc(27, 16), gc(45), None, ALU.mult),
                waits=[ev_r2])
            pr.op("v", lambda e, tc=tc, gc=gc: e.tensor_tensor(
                ST("acc1", tc), gc(43), gc(45), ALU.mult), waits=[ev_r2])

        wfbc = [self.pv(32768 + e * 1024, BF16, 512) for e in range(E1)]
        qrow = self.pv(63488, BF16, 2048, 0, 1)
        self.ev_bc0 = {}
        prev_q = None
        for e_ in range(E1):
            evq = []
            for q4 in range(4):
                tc = q4
                bk = q4 % 2
                ev_tt2 = pr.op("t", lambda e, tc=tc, e_=e_, bk=bk: e.transpose(
                    psf(bk, P)[0:1, :], self.w1tm[:, tc * E1 + e_: tc * E1 + e_ + 1],
                    self.id_f[:]),
                    waits=[self.ev_w1tm_ln1[tc], self.bank_free.get(bk)])
                ev_cp = pr.op("v", lambda e, q4=q4, bk=bk: e.tensor_copy(
                    qrow[0:1, q4 * P:(q4 + 1) * P], psf(bk, P)[0:1, :]),
                    waits=[ev_tt2, prev_q])
                self.bank_free[bk] = ev_cp
                evq.append(ev_cp)
            ev_k1 = pr.op("t", lambda e: e.matmul(
                psf(2, 512), CRB("ones", 0, P), qrow[0:1, 0:512],
                start=True, stop=True),
                waits=evq + [self.bank_free.get(2), self.ECG])
            prev_q = ev_k1
            ev_cb = pr.op("v", lambda e, e_=e_: e.tensor_copy(wfbc[e_][:], psf(2, 512)),
                          waits=[ev_k1])
            self.bank_free[2] = ev_cb
            self.ev_bc0[e_] = ev_cb

    def l1_attn(self):
        pr, dr = self.pr, self.dr
        psf, psb, X, XN, CV, CRB, ST = (self.psf, self.psb, self.X, self.XN,
                                        self.CV, self.CRB, self.ST)
        latf = self.pv(0, BF16, 2 * 2048)
        wkvd = self.pv(8192, BF16, 8 * 256)
        kup = self.pv(12288, BF16, 2 * 256)
        vup = self.pv(13312, BF16, 2 * 256)
        qfpA = self.pv(0, BF16, 2048)
        qfpB = self.pv(4096, BF16, 2048)
        pt = [self.pv(8192 + i * 1024, BF16, 512) for i in range(4)]
        wqp = [self.pv(12288, BF16, 8 * 128), self.pv(14336, BF16, 8 * 128)]
        wo_sb = self.pv(0, BF16, 8 * 1024)
        wkL = self.pv(0, BF16, 8 * 256)
        wvL = self.pv(4096, BF16, 8 * 256)
        kf = [self.pv(16384, BF16, 2048), self.pv(20480, BF16, 2048)]
        vt = self.pv(24576, BF16, 16 * 260)
        of = self.pv(32896, BF16, 8 * 2048)
        SCR = 65792
        qA = self.pv(SCR, BF16, 2048)
        qB = self.pv(SCR + 4096, BF16, 2048)
        qD = self.pv(SCR + 8192, BF16, 2048)
        qda = self.pv(SCR + 8192, BF16, 1024)
        qdb = self.pv(SCR + 10240, BF16, 1024)
        sml = self.pv(SCR + 12288, F32, 64)
        s1bc = self.pv(SCR + 12544, BF16, 512)
        rr = self.pv(SCR + 13568, BF16, 512, 0, 1)
        bcn = self.pv(SCR + 14592, BF16, 512, 0, 64)
        utm = self.pv(SCR + 8192, BF16, 1024)
        kA = self.pv(32896, BF16, 4096)
        kB = self.pv(41088, BF16, 4096)
        kD = self.pv(49280, BF16, 4096)
        kda = self.pv(49280, BF16, 2048)
        kdb = self.pv(53376, BF16, 2048)
        s1row = self.pv(61568, BF16, 2048, 0, 1)

        def rope_norm_b(ev_src, bufA, bufB, bda, bdb, nseg, base, fold, rkey):
            W = nseg * 64
            n = NTC * W
            nh = NTC * nseg
            cosw = ("cosG" if base == "G" else "cosL")
            sinw = ("sinG" if base == "G" else "sinL")
            tb = {"cosG": 0, "sinG": 1, "cosL": 2, "sinL": 3}
            cosb = self.trig[:, tb[cosw] * NTC * 32:(tb[cosw] + 1) * NTC * 32].rearrange(
                "p (t f) -> p t f", f=32).unsqueeze(2).broadcast_to([P, NTC, nseg, 32])
            sinb = self.trig[:, tb[sinw] * NTC * 32:(tb[sinw] + 1) * NTC * 32].rearrange(
                "p (t f) -> p t f", f=32).unsqueeze(2).broadcast_to([P, NTC, nseg, 32])
            A4 = lambda b, o: b[:, 0:n].rearrange("p (t h j) -> p t h j", h=nseg, j=64)[:, :, :, o:64:2]
            D4 = lambda b: b[:, 0:n // 2].rearrange("p (t h j) -> p t h j", h=nseg, j=32)
            SEG = lambda b: b[:, 0:n].rearrange("p (s j) -> p s j", j=64)
            ev_sq = pr.op("s", lambda e: e.activation(qD[:, 0:n] if bufA is qA else kD[:, 0:n],
                                                      bufA[:, 0:n], AF.Square),
                          waits=[ev_src, rkey.get("red")])
            ev_rd = pr.op("v", lambda e: e.reduce_sum(
                sml[:, 0:nh], SEG(qD if bufA is qA else kD), axis=AX.X),
                waits=[ev_sq, rkey.get("out")])
            rkey["red"] = ev_rd
            ev_sr = pr.op("s", lambda e: e.activation(
                sml[:, 0:nh], sml[:, 0:nh], AF.Sqrt, bias=CV("eps6"), scale=1.0 / 64.0),
                waits=[ev_rd])
            ev_rc = pr.op("v", lambda e: e.reciprocal(sml[:, 0:nh], sml[:, 0:nh]), waits=[ev_sr])
            if fold != 1.0:
                ev_rc = pr.op("v", lambda e: e.tensor_scalar(
                    sml[:, 0:nh], sml[:, 0:nh], float(fold), None, ALU.mult), waits=[ev_rc])
            e1 = pr.op("v", lambda e: e.tensor_tensor(D4(bda), A4(bufA, 0), cosb, ALU.mult),
                       waits=[ev_src, ev_rd, self.ECY, rkey.get("out")])
            e2 = pr.op("g", lambda e: e.tensor_tensor(D4(bdb), A4(bufA, 1), sinb, ALU.mult),
                       waits=[ev_src, ev_rd, self.ECY, self.ECG, rkey.get("out")])
            e5 = pr.op("v", lambda e: e.tensor_tensor(A4(bufB, 0), D4(bda), D4(bdb), ALU.subtract),
                       waits=[e1, e2, rkey.get("out")])
            e3 = pr.op("v", lambda e: e.tensor_tensor(D4(bda), A4(bufA, 0), sinb, ALU.mult),
                       waits=[e5])
            e4 = pr.op("g", lambda e: e.tensor_tensor(D4(bdb), A4(bufA, 1), cosb, ALU.mult),
                       waits=[e5])
            e6 = pr.op("v", lambda e: e.tensor_tensor(A4(bufB, 1), D4(bda), D4(bdb), ALU.add),
                       waits=[e3, e4])
            ev_o = pr.op("v", lambda e: e.tensor_tensor(
                bufA[:, 0:n].rearrange("p (s j) -> p s j", j=64),
                bufB[:, 0:n].rearrange("p (s j) -> p s j", j=64),
                sml[:, 0:nh].unsqueeze(2).broadcast_to([P, nh, 64]), ALU.mult),
                waits=[e5, e6, ev_rc, rkey.get("ard")])
            rkey["out"] = ev_o
            return ev_o

        rkG = {}
        rkQ = {}

        ev_a = pr.op("v", lambda e: e.tensor_scalar(
            sml[:, 0:16], ST("var1", 0, 16), CV("eps5"), None, ALU.add), waits=[self.ECY])
        ev_b = pr.op("v", lambda e: e.reciprocal(sml[:, 16:32], sml[:, 0:16]), waits=[ev_a])
        ev_c = pr.op("v", lambda e: e.tensor_tensor(
            sml[:, 0:16], ST("var1", 0, 16), sml[:, 16:32], ALU.mult), waits=[ev_b])
        ev_d = pr.op("s", lambda e: e.activation(
            sml[:, 16:32], sml[:, 0:16], AF.Sqrt, bias=CV("eps6")), waits=[ev_c])
        ev_s1 = pr.op("v", lambda e: e.reciprocal(ST("s1", 0, 16), sml[:, 16:32]), waits=[ev_d])
        rkG["red"] = ev_s1
        rkQ["red"] = ev_s1
        ev_s1r = []
        for tc in range(NTC):
            ev_tt = pr.op("t", lambda e, tc=tc: e.transpose(
                psf(6, P)[0:1, :], ST("s1", tc), self.id_f[:]),
                waits=[ev_s1, self.bank_free.get(6)])
            ev_cp = pr.op("v", lambda e, tc=tc: e.tensor_copy(
                s1row[0:1, tc * P:(tc + 1) * P], psf(6, P)[0:1, :]), waits=[ev_tt])
            self.bank_free[6] = ev_cp
            ev_s1r.append(ev_cp)

        for dc in range(NDC):
            pr.op("g", lambda e, dc=dc: e.dma_start(
                out=wkvd[:, dc * 256:(dc + 1) * 256],
                in_=dr["L1_attn_g_kv_down"][dc * P:(dc + 1) * P, :]), sig="dkvd", sigv=16)
        ev_wkvd0 = ("dkvd", pr.sems["dkvd"][1])
        for lc in range(2):
            pr.op("g", lambda e, lc=lc: e.dma_start(
                out=kup[:, lc * 256:(lc + 1) * 256],
                in_=dr["L1_attn_g_k_up"][lc * P:(lc + 1) * P, :]), sig="dkup", sigv=16)
            pr.op("g", lambda e, lc=lc: e.dma_start(
                out=vup[:, lc * 256:(lc + 1) * 256],
                in_=dr["L1_attn_g_v_up"][lc * P:(lc + 1) * P, :]), sig="dvup", sigv=16)
        ev_kup = ("dkup", pr.sems["dkup"][1])
        ev_vup = ("dvup", pr.sems["dvup"][1])
        ev_wkvd = []
        for dc in range(NDC):
            ev_wkvd.append(pr.op("v", lambda e, dc=dc: e.tensor_scalar(
                wkvd[:, dc * 256:(dc + 1) * 256], wkvd[:, dc * 256:(dc + 1) * 256],
                CV("n1w", dc), None, ALU.mult), waits=[ev_wkvd0]))

        ev_latf = {}
        ev_s1bc_reader = None
        for tt in range(4):
            ev_k1 = pr.op("t", lambda e, tt=tt: e.matmul(
                psf(6, 512), CRB("ones", 0, P), s1row[0:1, tt * 512: tt * 512 + 512],
                start=True, stop=True),
                waits=ev_s1r + [self.bank_free.get(6), self.ECG])
            ev_sb = pr.op("v", lambda e: e.tensor_copy(s1bc[:], psf(6, 512)),
                          waits=[ev_k1, ev_s1bc_reader])
            self.bank_free[6] = ev_sb
            for lc in range(2):
                bk = (2 * tt + lc) % 2
                for dc in range(NDC):
                    ev_mm = pr.op("t", lambda e, dc=dc, lc=lc, tt=tt, bk=bk: e.matmul(
                        psf(bk, 512), wkvd[:, dc * 256 + lc * 128: dc * 256 + lc * 128 + 128],
                        XN(dc, tt * 512, (tt + 1) * 512),
                        start=(dc == 0), stop=(dc == NDC - 1)),
                        waits=[ev_wkvd[dc], self.bank_free.get(bk)])
                ev_ev = pr.op("v", lambda e, lc=lc, tt=tt, bk=bk: e.tensor_tensor(
                    latf[:, lc * 2048 + tt * 512: lc * 2048 + tt * 512 + 512],
                    psf(bk, 512), s1bc[:], ALU.mult), waits=[ev_mm, ev_sb])
                self.bank_free[bk] = ev_ev
                ev_latf[(lc, tt)] = ev_ev
                ev_s1bc_reader = ev_ev
        evl = list(ev_latf.values())

        ev_vt_ones = pr.op("g", lambda e: e.memset(vt[:], 1.0))
        ev_kA = []
        ev_vte = {}
        for tc in range(NTC):
            bk = 2 + tc % 2
            for lc in range(2):
                ev_mm = pr.op("t", lambda e, lc=lc, tc=tc, bk=bk: e.matmul(
                    psf(bk, 256), latf[:, lc * 2048 + tc * P: lc * 2048 + (tc + 1) * P],
                    kup[:, lc * 256:(lc + 1) * 256],
                    start=(lc == 0), stop=(lc == 1)),
                    waits=[ev_latf[(lc, tc // 4)], ev_kup, self.bank_free.get(bk)])
            ev_cp = pr.op("v", lambda e, tc=tc, bk=bk: e.tensor_copy(
                kA[:, tc * 256:(tc + 1) * 256], psf(bk, 256)),
                waits=[ev_mm, rkG.get("out"), rkG.get("ard")])
            self.bank_free[bk] = ev_cp
            ev_kA.append(ev_cp)
            bk2 = 4 + tc % 2
            for lc in range(2):
                ev_mm = pr.op("t", lambda e, lc=lc, tc=tc, bk2=bk2: e.matmul(
                    psf(bk2, 256), latf[:, lc * 2048 + tc * P: lc * 2048 + (tc + 1) * P],
                    vup[:, lc * 256:(lc + 1) * 256],
                    start=(lc == 0), stop=(lc == 1)),
                    waits=[ev_latf[(lc, tc // 4)], ev_vup, self.bank_free.get(bk2)])
            ev_c3 = pr.op("v", lambda e, tc=tc, bk2=bk2: e.tensor_copy(
                vt[:, tc * 260: tc * 260 + 260].rearrange("p (h u) -> p h u", u=65)[:, :, 0:64],
                psf(bk2, 256).rearrange("p (h u) -> p h u", u=64)),
                waits=[ev_mm, ev_vt_ones])
            self.bank_free[bk2] = ev_c3
            ev_vte[tc] = ev_c3
        ev_roK = rope_norm_b(ev_kA[-1] if False else ("pv", pr.sems["pv"][1]),
                             kA, kB, kda, kdb, 4, "G", 1.0, rkG)
        ev_kf = []
        for tc in range(NTC):
            for sl in range(2):
                bk2 = 6 + sl
                ev_tt = pr.op("t", lambda e, tc=tc, sl=sl, bk2=bk2: e.transpose(
                    psb(bk2, 128), kA[:, tc * 256 + sl * 128: tc * 256 + sl * 128 + 128],
                    self.id_b[:]),
                    waits=[ev_roK, self.bank_free.get(bk2)])
                ev_c2 = pr.op("s", lambda e, sl=sl, tc=tc, bk2=bk2: e.copy(
                    kf[sl][:, tc * P:(tc + 1) * P], psb(bk2, 128)), waits=[ev_tt])
                self.bank_free[bk2] = ev_c2
                ev_kf.append(ev_c2)
        rkG["ard"] = pr.now("t")

        state = {"ptr": [None, None, None, None], "rrr": None, "bcnr": None}

        def sdpa_heads(heads, kfS, vtS, vt_evs, ofS, ofchunk, src_sb, wname,
                       nw_name, s_name, base, maskf, mc_range, wq_sems, kf_evs,
                       src_gate=None):
            ev_ofd = {}
            qstate = {"prev_q_readers": None}
            pairs = list(range(heads[0] // 2, heads[0] // 2 + len(heads) // 2))

            def qprep1(pair):
                sl = wqn[0] % 2
                ev_wq = pr.op("g", lambda e, pair=pair, sl=sl, wname=wname: e.dma_start(
                    out=wqp[sl].rearrange("p (dc c) -> p dc c", c=128),
                    in_=dr[wname][:, pair * 128:(pair + 1) * 128].rearrange(
                        "(dc p) c -> p dc c", p=P)),
                    sig="dwq%d" % sl, sigv=16, waits=[wq_sems[sl]])
                wqn[0] += 1
                evs_sc = []
                for dc in range(NDC):
                    evs_sc.append(pr.op("g", lambda e, dc=dc, sl=sl: e.tensor_scalar(
                        wqp[sl][:, dc * 128:(dc + 1) * 128], wqp[sl][:, dc * 128:(dc + 1) * 128],
                        CV(nw_name, dc), None, ALU.mult), waits=[ev_wq]))
                for tc in range(NTC):
                    bk = 3
                    for dc in range(NDC):
                        ev_mm = pr.op("t", lambda e, dc=dc, tc=tc, sl=sl, bk=bk, src_sb=src_sb: e.matmul(
                            psf(bk, 128), src_sb(dc, tc),
                            wqp[sl][:, dc * 128:(dc + 1) * 128],
                            start=(dc == 0), stop=(dc == NDC - 1)),
                            waits=[evs_sc[dc], self.bank_free.get(bk), src_gate])
                    ev_cp = pr.op("v", lambda e, tc=tc, bk=bk, s_name=s_name: e.tensor_scalar(
                        qA[:, tc * 128:(tc + 1) * 128], psf(bk, 128), ST(s_name, tc), None, ALU.mult),
                        waits=[ev_mm, rkQ.get("out"), rkQ.get("ard")])
                    self.bank_free[bk] = ev_cp
                wq_sems[sl] = pr.now("t")
                return rope_norm_b(("pv", pr.sems["pv"][1]), qA, qB, qda, qdb,
                                   2, base, SHD, rkQ)

            def qprep2(pair, ev_roQ):
                bp = ((pair // 2) % 2) * 64
                ev_qf = []
                for tc in range(NTC):
                    bk2 = 5
                    ev_tt = pr.op("t", lambda e, tc=tc, bk2=bk2: e.transpose(
                        psb(bk2, 128), qA[:, tc * 128:(tc + 1) * 128], self.id_b[:]),
                        waits=[ev_roQ, self.bank_free.get(bk2), qstate["prev_q_readers"]])
                    ev_c2 = pr.op("s", lambda e, tc=tc, bk2=bk2, bp=bp: e.copy(
                        qfpA[bp:bp + 64, tc * P:(tc + 1) * P], psb(bk2, 128)[0:64, :]),
                        waits=[ev_tt])
                    ev_c3 = pr.op("s", lambda e, tc=tc, bk2=bk2, bp=bp: e.copy(
                        qfpB[bp:bp + 64, tc * P:(tc + 1) * P], psb(bk2, 128)[64:128, :]),
                        waits=[ev_tt])
                    self.bank_free[bk2] = ev_c3
                    ev_qf += [ev_c2, ev_c3]
                rkQ["ard"] = pr.now("t")
                return ev_qf

            def flush_norm():
                pend = qstate.get("pend")
                if pend is None:
                    return
                h, nt, oc, bko, ev_p = pend
                ev_rr = pr.op("v", lambda e, bko=bko: self.lp(lambda: e.reciprocal(
                    rr[0:1, :], psf(bko, 512)[64:65, :])),
                    waits=[ev_p, state["rrr"]])
                ev_k1 = pr.op("t", lambda e: e.matmul(
                    psf(4, 512)[0:64, :], CRB("ones", 0, 64), rr[0:1, :],
                    start=True, stop=True),
                    waits=[ev_rr, self.bank_free.get(4)])
                state["rrr"] = ev_k1
                ev_bc = pr.op("v", lambda e: e.tensor_copy(
                    bcn[0:64, :], psf(4, 512)[0:64, :]),
                    waits=[ev_k1, state["bcnr"]])
                self.bank_free[4] = ev_bc
                ev_o = pr.op("v", lambda e, h=h, nt=nt, bko=bko, oc=oc: e.tensor_tensor(
                    ofS[(h % 2) * 64: (h % 2) * 64 + 64,
                        oc * 2048 + nt * 512: oc * 2048 + (nt + 1) * 512],
                    psf(bko, 512)[0:64, :], bcn[0:64, :], ALU.mult),
                    waits=[ev_p, ev_bc])
                state["bcnr"] = ev_o
                self.bank_free[bko] = ev_o
                ev_ofd[(h, nt)] = ev_o
                qstate["pend"] = None

            def one_head(pair, hh, ev_qf):
                h = 2 * pair + hh
                kvh = h // 4
                kslot, kbase = kvh // 2, (kvh % 2) * 64
                oc = ofchunk(h)
                for nt in range(4):
                    bko = 6 + nt % 2
                    mcs = mc_range(nt)
                    n_mc = len(mcs)
                    LOOKAHEAD = 2
                    masks = {}
                    for ii in range(n_mc + LOOKAHEAD):
                        if ii < n_mc:
                            i, mc = ii, mcs[ii]
                            bks = i % 3
                            sl2 = i % 4
                            qsb = qfpA if hh == 0 else qfpB
                            ev_s = pr.op("t", lambda e, mc=mc, nt=nt, kslot=kslot, kbase=kbase, bks=bks, qsb=qsb: e.matmul(
                                psf(bks, 512), kfS[kslot][kbase:kbase + 64, mc * P:(mc + 1) * P],
                                qsb[kbase:kbase + 64, nt * 512:(nt + 1) * 512],
                                start=True, stop=True),
                                waits=ev_qf + kf_evs + [self.bank_free.get(bks)])
                            ev_e = pr.op("s", lambda e, sl2=sl2, bks=bks: e.activation(
                                pt[sl2], psf(bks, 512), AF.Exp),
                                waits=[ev_s, state["ptr"][sl2]])
                            self.bank_free[bks] = ev_e
                            masks[i] = maskf(mc, nt, sl2, ev_e)
                        j = ii - LOOKAHEAD
                        if 0 <= j < n_mc:
                            mc_j = mcs[j]
                            sl2j = j % 4
                            last = (j == n_mc - 1)
                            ev_p = pr.op("t", lambda e, mc=mc_j, j=j, sl2=sl2j, bko=bko, last=last, kvh=kvh: e.matmul(
                                psf(bko, 512)[0:65, :],
                                vtS[:, mc * 260 + kvh * 65: mc * 260 + kvh * 65 + 65],
                                pt[sl2], start=(j == 0), stop=last),
                                waits=[masks[j], vt_evs[mc_j],
                                       self.bank_free.get(bko) if j == 0 else None])
                            state["ptr"][sl2j] = ev_p
                    pend_prev = qstate.get("pend")
                    qstate["pend"], qstate["_new"] = pend_prev, (h, nt, oc, bko, ev_p)
                    flush_norm()
                    qstate["pend"] = qstate.pop("_new")

            ev_ro_next = qprep1(pairs[0])
            ev_qf = qprep2(pairs[0], ev_ro_next)
            for idx, pair in enumerate(pairs):
                if idx + 1 < len(pairs):
                    ev_ro_next = qprep1(pairs[idx + 1])
                one_head(pair, 0, ev_qf)
                one_head(pair, 1, ev_qf)
                qstate["prev_q_readers"] = pr.now("t")
                if idx + 1 < len(pairs):
                    ev_qf = qprep2(pairs[idx + 1], ev_ro_next)
            flush_norm()
            return ev_ofd

        wqn = [0]
        kv_done = pr.now("t")
        wq_sems = [kv_done, kv_done]
        src_xn = lambda dc, tc: self.xn_sb[:, dc * T + tc * P: dc * T + (tc + 1) * P]
        src_u = lambda dc, tc: u_bf[:, dc * 2048 + tc * P: dc * 2048 + (tc + 1) * P]

        def mask_g(mc, nt, sl2, ev_e):
            if mc < 4 * nt:
                return ev_e
            b = nt * 512 - mc * P
            return pr.op("g" if sl2 % 2 else "v", lambda e, sl2=sl2, b=b: e.tensor_tensor(
                pt[sl2], pt[sl2], self.mbc[:, MZ + b: MZ + b + 512], ALU.mult),
                waits=[ev_e])

        ev_ofg = sdpa_heads(list(range(16)), kf, vt, ev_vte, of, lambda h: h // 2,
                            src_xn, "L1_attn_g_wq", "n1w", "s1", "G",
                            mask_g, lambda nt: list(range(0, 4 * nt + 4)),
                            wq_sems, ev_kf)

        kfvt_dead = pr.now("t")
        evxn = []
        for dc in range(NDC):
            evxn.append(pr.op("v", lambda e, dc=dc: e.tensor_tensor(
                X(dc), X(dc), XN(dc), ALU.add), waits=[kfvt_dead]))

        for oc in range(NDC):
            pr.op("g", lambda e, oc=oc: e.dma_start(
                out=wo_sb[:, oc * 1024:(oc + 1) * 1024],
                in_=dr["L1_attn_g_wo"][oc * P:(oc + 1) * P, :]), sig="dwog", sigv=16,
                waits=[kfvt_dead])
        ev_wo = ("dwog", pr.sems["dwog"][1])
        ev_u = {}
        for dc in range(NDC):
            for tt in range(4):
                bk = (4 * dc + tt) % 2
                for oc in range(NDC):
                    ev_mm = pr.op("t", lambda e, dc=dc, tt=tt, oc=oc, bk=bk: e.matmul(
                        psf(bk, 512), wo_sb[:, oc * 1024 + dc * 128: oc * 1024 + dc * 128 + 128],
                        of[:, oc * 2048 + tt * 512: oc * 2048 + tt * 512 + 512],
                        start=(oc == 0), stop=(oc == NDC - 1)),
                        waits=[ev_wo, self.bank_free.get(bk),
                               ev_ofg[(2 * oc, tt)], ev_ofg[(2 * oc + 1, tt)]])
                ev_ax = pr.op("v", lambda e, dc=dc, tt=tt, bk=bk: e.tensor_tensor(
                    X(dc, tt * 512, (tt + 1) * 512), X(dc, tt * 512, (tt + 1) * 512),
                    psf(bk, 512), ALU.add), waits=[ev_mm, evxn[dc]])
                ev_uu = pr.op("v", lambda e, dc=dc, tt=tt, bk=bk: e.tensor_tensor(
                    XN(dc, tt * 512, (tt + 1) * 512), psf(bk, 512),
                    XN(dc, tt * 512, (tt + 1) * 512), ALU.add),
                    waits=[ev_mm, evxn[dc]])
                self.bank_free[bk] = ev_uu
                ev_u[(dc, tt)] = ev_uu
        wo_done = pr.now("t")

        ev_s2l = []
        prev_utm_reader = None
        for tc in range(NTC):
            evs_t = []
            for dc in range(NDC):
                bk = 2 + dc % 2
                ev_tt = pr.op("t", lambda e, dc=dc, tc=tc, bk=bk: e.transpose(
                    psb(bk, 128), XN(dc, tc * P, (tc + 1) * P), self.id_b[:]),
                    waits=[ev_u[(dc, tc // 4)], self.bank_free.get(bk)])
                ev_cp = pr.op("v", lambda e, dc=dc, bk=bk: e.tensor_copy(
                    utm[:, dc * 128:(dc + 1) * 128], psb(bk, 128)),
                    waits=[ev_tt, prev_utm_reader])
                self.bank_free[bk] = ev_cp
                evs_t.append(ev_cp)
            acc_evs = []
            for b4 in range(4):
                acc_evs.append(pr.op("s", lambda e, b4=b4: e.activation(
                    qB[:, 0:256], utm[:, b4 * 256:(b4 + 1) * 256], AF.Square,
                    accum_out=sml[:, b4:b4 + 1]),
                    waits=evs_t + [rkQ.get("red"), acc_evs[-1] if acc_evs else None]))
            prev_utm_reader = acc_evs[-1]
            ev_s2a = pr.op("v", lambda e: e.reduce_sum(
                sml[:, 4:5], sml[:, 0:4], axis=AX.X), waits=acc_evs)
            rkQ["red"] = ev_s2a
            rkG["red"] = ev_s2a
            ev_s2b = pr.op("s", lambda e: e.activation(
                sml[:, 5:6], sml[:, 4:5], AF.Sqrt, bias=CV("eps6"), scale=1.0 / D),
                waits=[ev_s2a])
            ev_s2c = pr.op("v", lambda e, tc=tc: e.reciprocal(ST("s2", tc), sml[:, 5:6]),
                           waits=[ev_s2b])
            ev_s2l.append(ev_s2c)

        wo_readers_done = pr.now("t")
        for dc in range(NDC):
            pr.op("g", lambda e, dc=dc: e.dma_start(
                out=wkL[:, dc * 256:(dc + 1) * 256],
                in_=dr["L1_attn_l_wk"][dc * P:(dc + 1) * P, :]), sig="dwkL", sigv=16,
                waits=[wo_readers_done])
            pr.op("g", lambda e, dc=dc: e.dma_start(
                out=wvL[:, dc * 256:(dc + 1) * 256],
                in_=dr["L1_attn_l_wv"][dc * P:(dc + 1) * P, :]), sig="dwvL", sigv=16,
                waits=[wo_readers_done])
        ev_wkL0 = ("dwkL", pr.sems["dwkL"][1])
        ev_wvL0 = ("dwvL", pr.sems["dwvL"][1])
        ev_wkL = []
        ev_wvL = []
        for dc in range(NDC):
            ev_wkL.append(pr.op("v", lambda e, dc=dc: e.tensor_scalar(
                wkL[:, dc * 256:(dc + 1) * 256], wkL[:, dc * 256:(dc + 1) * 256],
                CV("n2w", dc), None, ALU.mult), waits=[ev_wkL0]))
            ev_wvL.append(pr.op("v", lambda e, dc=dc: e.tensor_scalar(
                wvL[:, dc * 256:(dc + 1) * 256], wvL[:, dc * 256:(dc + 1) * 256],
                CV("n2w", dc), None, ALU.mult), waits=[ev_wvL0]))
        ev_vtL_ones = pr.op("g", lambda e: e.memset(vt[:], 1.0), waits=[wo_done])
        ev_kA = []
        ev_vtLe = {}
        for tc in range(NTC):
            bk = 2 + tc % 2
            for dc in range(NDC):
                ev_mm = pr.op("t", lambda e, dc=dc, tc=tc, bk=bk: e.matmul(
                    psf(bk, 256), XN(dc, tc * P, (tc + 1) * P),
                    wkL[:, dc * 256:(dc + 1) * 256],
                    start=(dc == 0), stop=(dc == NDC - 1)),
                    waits=[ev_wkL[dc], self.bank_free.get(bk), ev_u[(dc, tc // 4)]])
            ev_cp = pr.op("v", lambda e, tc=tc, bk=bk: e.tensor_scalar(
                kA[:, tc * 256:(tc + 1) * 256], psf(bk, 256), ST("s2", tc), None, ALU.mult),
                waits=[ev_mm, ev_s2l[tc], rkG.get("out"), rkG.get("ard"), wo_readers_done])
            self.bank_free[bk] = ev_cp
            ev_kA.append(ev_cp)
            bk2 = 4 + tc % 2
            for dc in range(NDC):
                ev_mm = pr.op("t", lambda e, dc=dc, tc=tc, bk2=bk2: e.matmul(
                    psf(bk2, 256), XN(dc, tc * P, (tc + 1) * P),
                    wvL[:, dc * 256:(dc + 1) * 256],
                    start=(dc == 0), stop=(dc == NDC - 1)),
                    waits=[ev_wvL[dc], self.bank_free.get(bk2), ev_u[(dc, tc // 4)]])
            ev_c3 = pr.op("v", lambda e, tc=tc, bk2=bk2: e.tensor_scalar(
                vt[:, tc * 260: tc * 260 + 260].rearrange("p (h u) -> p h u", u=65)[:, :, 0:64],
                psf(bk2, 256).rearrange("p (h u) -> p h u", u=64),
                ST("s2", tc), None, ALU.mult),
                waits=[ev_mm, ev_vtL_ones, ev_s2l[tc]])
            self.bank_free[bk2] = ev_c3
            ev_vtLe[tc] = ev_c3
        ev_roKL = rope_norm_b(("pv", pr.sems["pv"][1]), kA, kB, kda, kdb, 4, "L", 1.0, rkG)
        ev_kfL = []
        for tc in range(NTC):
            for sl in range(2):
                bk2 = 6 + sl
                ev_tt = pr.op("t", lambda e, tc=tc, sl=sl, bk2=bk2: e.transpose(
                    psb(bk2, 128), kA[:, tc * 256 + sl * 128: tc * 256 + sl * 128 + 128],
                    self.id_b[:]),
                    waits=[ev_roKL, self.bank_free.get(bk2)])
                ev_c2 = pr.op("s", lambda e, sl=sl, tc=tc, bk2=bk2: e.copy(
                    kf[sl][:, tc * P:(tc + 1) * P], psb(bk2, 128)), waits=[ev_tt])
                self.bank_free[bk2] = ev_c2
                ev_kfL.append(ev_c2)
        rkG["ard"] = pr.now("t")

        def mask_l(mc, nt, sl2, ev_e):
            b = nt * 512 - mc * P
            return pr.op("g" if sl2 % 2 else "v", lambda e, sl2=sl2, b=b: e.tensor_tensor(
                pt[sl2], pt[sl2], self.mb2[:, MZ + b: MZ + b + 512], ALU.mult),
                waits=[ev_e])

        def mc_range_l(nt):
            return list(range(max(0, 4 * nt - 2), 4 * nt + 4))

        src_u = lambda dc, tc: self.xn_sb[:, dc * T + tc * P: dc * T + (tc + 1) * P]
        state["bcnr"] = wo_done
        u_done = pr.now("v")
        ev_ofl = sdpa_heads(list(range(16)), kf, vt, ev_vtLe, of, lambda h: h // 2,
                            src_u, "L1_attn_l_wq", "n2w", "s2", "L",
                            mask_l, mc_range_l, wq_sems, ev_kfL, src_gate=u_done)

        kfvt_dead2 = pr.now("t")
        for oc in range(NDC):
            pr.op("g", lambda e, oc=oc: e.dma_start(
                out=wo_sb[:, oc * 1024:(oc + 1) * 1024],
                in_=dr["L1_attn_l_wo"][oc * P:(oc + 1) * P, :]), sig="dwol", sigv=16,
                waits=[kfvt_dead2])
        ev_wol = ("dwol", pr.sems["dwol"][1])
        for dc in range(NDC):
            for tt in range(4):
                bk = (4 * dc + tt) % 2
                for oc in range(NDC):
                    ev_mm = pr.op("t", lambda e, dc=dc, tt=tt, oc=oc, bk=bk: e.matmul(
                        psf(bk, 512), wo_sb[:, oc * 1024 + dc * 128: oc * 1024 + dc * 128 + 128],
                        of[:, oc * 2048 + tt * 512: oc * 2048 + tt * 512 + 512],
                        start=(oc == 0), stop=(oc == NDC - 1)),
                        waits=[ev_wol, self.bank_free.get(bk),
                               ev_ofl[(2 * oc, tt)], ev_ofl[(2 * oc + 1, tt)]])
                ev_ax = pr.op("v", lambda e, dc=dc, tt=tt, bk=bk: e.tensor_tensor(
                    X(dc, tt * 512, (tt + 1) * 512), X(dc, tt * 512, (tt + 1) * 512),
                    psf(bk, 512), ALU.add), waits=[ev_mm, pr.now("v") if (dc == 0 and tt == 0) else None])
                self.bank_free[bk] = ev_ax

    def l1_ffn(self):
        pr, dr = self.pr, self.dr
        psf, psb, X, XN, CV, CRB, ST = (self.psf, self.psb, self.X, self.XN,
                                        self.CV, self.CRB, self.ST)
        ewb = [self.pv(0, BF16, 8 * 1024), self.pv(16384, BF16, 8 * 1024)]
        wfbc = [self.pv(32768 + e * 1024, BF16, 512) for e in range(E1)]
        xp = self.pv(49152, BF16, 8 * 512)
        raccbc = self.pv(57344, BF16, 512)
        tmp1 = self.pv(58368, F32, 512)
        accrow = self.pv(67584, BF16, 2048, 0, 1)
        raccrow = self.pv(71680, BF16, 2048, 0, 1)
        eps2row = self.pv(75776, BF16, 2048, 0, 1)
        qrow = self.pv(63488, BF16, 2048, 0, 1)

        ev_w1tm = {tc: None for tc in range(NTC)}
        ev_acc = {tc: None for tc in range(NTC)}

        ev_ar = []
        for tc in range(NTC):
            bk = 2 + tc % 2
            ev_tt = pr.op("t", lambda e, tc=tc, bk=bk: e.transpose(
                psf(bk, P)[0:1, :], ST("acc1", tc), self.id_f[:]),
                waits=[ev_acc[tc], self.bank_free.get(bk)])
            ev_cp = pr.op("v", lambda e, tc=tc, bk=bk: e.tensor_copy(
                accrow[0:1, tc * P:(tc + 1) * P], psf(bk, P)[0:1, :]), waits=[ev_tt])
            self.bank_free[bk] = ev_cp
            ev_ar.append(ev_cp)
        ev_rr2 = pr.op("v", lambda e: self.lp(lambda: e.reciprocal(raccrow[0:1, :], accrow[0:1, :])),
                       waits=ev_ar)
        ev_e2a = pr.op("v", lambda e: e.tensor_tensor(
            eps2row[0:1, :], accrow[0:1, :], accrow[0:1, :], ALU.mult), waits=ev_ar)
        ev_e2b = pr.op("v", lambda e: e.tensor_scalar(
            eps2row[0:1, :], eps2row[0:1, :], 1e-10, None, ALU.mult), waits=[ev_e2a])

        ewn = [0]
        ewfree = [None, None]
        for tt in range(4):
            if tt == 0:
                ev_bc = dict(self.ev_bc0)
                prev_q = None
            else:
                ev_bc = {}
                prev_q = None
            for e_ in ([] if tt == 0 else range(E1)):
                evq = []
                for q4 in range(4):
                    tc = tt * 4 + q4
                    bk = q4 % 2
                    ev_tt2 = pr.op("t", lambda e, tc=tc, e_=e_, bk=bk: e.transpose(
                        psf(bk, P)[0:1, :], self.w1tm[:, tc * E1 + e_: tc * E1 + e_ + 1],
                        self.id_f[:]),
                        waits=[ev_w1tm[tc], self.bank_free.get(bk)])
                    ev_cp = pr.op("v", lambda e, q4=q4, bk=bk: e.tensor_copy(
                        qrow[0:1, q4 * P:(q4 + 1) * P], psf(bk, P)[0:1, :]),
                        waits=[ev_tt2, prev_q])
                    self.bank_free[bk] = ev_cp
                    evq.append(ev_cp)
                ev_k1 = pr.op("t", lambda e: e.matmul(
                    psf(2, 512), CRB("ones", 0, P), qrow[0:1, 0:512],
                    start=True, stop=True),
                    waits=evq + [self.bank_free.get(2), self.ECG])
                prev_q = ev_k1
                ev_cb = pr.op("v", lambda e, e_=e_: e.tensor_copy(wfbc[e_][:], psf(2, 512)),
                              waits=[ev_k1, ewfree[0], ewfree[1]])
                self.bank_free[2] = ev_cb
                ev_bc[e_] = ev_cb
            ev_k2 = pr.op("t", lambda e, tt=tt: e.matmul(
                psf(3, 512), CRB("ones", 0, P), raccrow[0:1, tt * 512:(tt + 1) * 512],
                start=True, stop=True),
                waits=[ev_rr2, self.bank_free.get(3)])
            ev_rb = pr.op("v", lambda e: e.tensor_copy(raccbc[:], psf(3, 512)),
                          waits=[ev_k2, pr.now("v")])
            self.bank_free[3] = ev_rb
            ev_xp_reader = None
            ev_dc = {}
            for ie, e_ in enumerate(range(E1)):
                sl = ewn[0] % 2
                ev_ew = pr.op("g", lambda e, e_=e_, sl=sl: e.dma_start(
                    out=ewb[sl].rearrange("p (dk c) -> p dk c", c=1024),
                    in_=dr["L1_ffn_expert_w"][e_].rearrange("(dk p) c -> p dk c", p=P)),
                    sig="dew%d" % sl, sigv=16, waits=[ewfree[sl]])
                ewn[0] += 1
                ev_xps = []
                for dck in range(NDC):
                    ev_xps.append(pr.op("v", lambda e, dck=dck, tt=tt, e_=e_: e.tensor_tensor(
                        xp[:, dck * 512:(dck + 1) * 512],
                        XN(dck, tt * 512, (tt + 1) * 512), wfbc[e_][:], ALU.mult),
                        waits=[ev_bc[e_], ev_xp_reader]))
                last = None
                for dco in range(NDC):
                    for dck in range(NDC):
                        last = pr.op("t", lambda e, dco=dco, dck=dck, sl=sl, st=(ie == 0 and dck == 0): e.matmul(
                            psf(dco, 512), ewb[sl][:, dck * 1024 + dco * 128: dck * 1024 + dco * 128 + 128],
                            xp[:, dck * 512:(dck + 1) * 512],
                            start=st, stop=False),
                            waits=[ev_ew, ev_xps[dck],
                                   self.bank_free.get(dco) if ie == 0 and dck == 0 else None])
                    ev_dc[dco] = last
                ewfree[sl] = last
                ev_xp_reader = last
            prev_ax = None
            for dco in range(NDC):
                ev_ep = pr.op("t", lambda e, dco=dco, tt=tt: e.matmul(
                    psf(dco, 512), CRB("ones", 0, P), eps2row[0:1, tt * 512:(tt + 1) * 512],
                    start=False, stop=True),
                    waits=[ev_e2b, ev_dc[dco]])
                ev_t1 = pr.op("v", lambda e, dco=dco: e.tensor_tensor(
                    tmp1[:], psf(dco, 512), raccbc[:], ALU.mult),
                    waits=[ev_ep, ev_rb, prev_ax])
                ev_ax = pr.op("v", lambda e, dco=dco, tt=tt: e.tensor_tensor(
                    X(dco, tt * 512, (tt + 1) * 512), X(dco, tt * 512, (tt + 1) * 512),
                    tmp1[:], ALU.add), waits=[ev_t1])
                prev_ax = ev_ax
                self.bank_free[dco] = ev_ax


_NC_CACHE = {}


def kernel(**inputs):
    key = (STAGE, tuple(sorted(TAPS)))
    if key not in _NC_CACHE:
        _NC_CACHE[key] = K(STAGE, dict(TAPS)).build()
    nc = _NC_CACHE[key]

    ident, trigpack, mbc, mb2 = host_constants()
    cvec, _ = pack_cvec(inputs)
    crow, _ = pack_crow(inputs)

    used = {
        "L0_attn_Wv", "L0_attn_Wg", "L0_attn_Win", "L0_attn_Wout",
        "L0_ffn_Wg", "L0_ffn_W1", "L0_ffn_W2", "L0_ffn_b2",
        "L1_attn_g_wq", "L1_attn_g_kv_down", "L1_attn_g_k_up", "L1_attn_g_v_up",
        "L1_attn_g_wo", "L1_attn_l_wq", "L1_attn_l_wk", "L1_attn_l_wv",
        "L1_attn_l_wo", "L1_ffn_router_w", "L1_ffn_expert_w",
    }
    base = {k: np.ascontiguousarray(np.asarray(v, dtype=np.float32))
            for k, v in inputs.items() if k in used}
    base["c_ident"] = ident
    base["c_trig"] = trigpack
    base["c_mbc"] = mbc
    base["c_mb2"] = mb2
    base["c_cvec"] = cvec
    base["c_crow"] = crow
    x = np.ascontiguousarray(np.asarray(inputs["L0_attn_x"], dtype=np.float32))
    in_maps = []
    for c in range(8):
        m = dict(base)
        m["x_in"] = x[c]
        in_maps.append(m)
    res = run_bass_kernel_spmd(nc, in_maps, core_ids=list(range(8)))
    return np.stack([np.asarray(r["out"]) for r in res.results], axis=0)


# version 15
# speedup vs baseline: 1.0505x; 1.0505x over previous
from contextlib import ExitStack

import numpy as np

import concourse.bass as bass
import concourse.mybir as mybir
from concourse.bass_utils import run_bass_kernel_spmd

F32 = mybir.dt.float32
BF16 = mybir.dt.bfloat16
AF = mybir.ActivationFunctionType
ALU = mybir.AluOpType
AX = mybir.AxisListType

P = 128
T = 2048
D = 1024
NTC = T // P
NDC = D // P
GRP = 256
NG = T // GRP
INNER = 2048
HEAD = 128
H0 = 4096
E0 = 8
E1 = 16
HD = 64
NH = 16
KVH = 4
LAT = 256
WIN = 256
MZ = 384

STAGE = 99
TAPS = {}

INV_S2048 = float(np.float32(1.0) / np.float32(np.sqrt(np.float32(2048.0))))
SHD = float(np.float32(1.0) / np.float32(np.sqrt(np.float32(HD))))


class Prog:
    ENGS = ("t", "v", "s", "g", "y")

    def __init__(self, nc, ctx):
        self.nc = nc
        self.ctx = ctx
        self.q = {e: [] for e in self.ENGS}
        self.sems = {}
        self.waited = {e: {} for e in self.ENGS}
        self.nops = {e: 0 for e in self.ENGS}

    def _sem(self, name):
        if name not in self.sems:
            h = self.ctx.enter_context(self.nc.semaphore(name))
            self.sems[name] = [h, 0]
        return self.sems[name]

    def op(self, eng, fn, sig=None, sigv=1, waits=()):
        if sig is None:
            sig = "p" + eng
        s = self._sem(sig)
        s[1] += sigv
        after = s[1]
        wl = []
        for evn in waits:
            if evn is None:
                continue
            wn, wv = evn
            if wv <= 0:
                continue
            self._sem(wn)
            cur = self.waited[eng].get(wn, 0)
            if wv > cur:
                wl.append((wn, wv))
                self.waited[eng][wn] = wv
        self.nops[eng] += 1

        def emit(e, fn=fn, wl=wl, sname=sig, sigv=sigv):
            for (wn, wv) in wl:
                e.wait_ge(self.sems[wn][0], wv)
            inst = fn(e)
            inst.then_inc(self.sems[sname][0], sigv)
        self.q[eng].append(emit)
        return (sig, after)

    def wait(self, eng, *evs):
        wl = []
        for evn in evs:
            if evn is None:
                continue
            wn, wv = evn
            if wv <= 0:
                continue
            self._sem(wn)
            cur = self.waited[eng].get(wn, 0)
            if wv > cur:
                wl.append((wn, wv))
                self.waited[eng][wn] = wv
        if wl:
            def emit(e, wl=wl):
                for (wn, wv) in wl:
                    e.wait_ge(self.sems[wn][0], wv)
            self.q[eng].append(emit)

    def now(self, eng):
        s = self._sem("p" + eng)
        return ("p" + eng, s[1])

    def barrier(self, engs="tvsg"):
        evs = [self.now(e) for e in engs]
        for e in engs:
            self.wait(e, *[ev for ev in evs if ev[0] != "p" + e])

    def flush(self, block):
        for ename, attr in (("g", "gpsimd"), ("s", "scalar"), ("t", "tensor"),
                            ("v", "vector"), ("y", "sync")):
            fns = self.q[ename]
            if not fns:
                continue
            getattr(block, attr)(lambda e, fns=fns: [f(e) for f in fns])


def host_constants():
    ident = np.eye(P, dtype=np.float32)

    def trig(base):
        inv = (1.0 / (np.float32(base) ** (np.arange(0, HD, 2, dtype=np.float32)
                                           / np.float32(HD)))).astype(np.float32)
        pos = np.arange(T, dtype=np.float32)
        fr = pos[:, None] * inv[None, :]
        return np.cos(fr).astype(np.float32), np.sin(fr).astype(np.float32)

    cg, sg = trig(1000000.0)
    cl, sl = trig(10000.0)

    def tm32(x):
        return x.reshape(NTC, P, 32).transpose(1, 0, 2).reshape(P, NTC * 32)

    trigpack = np.concatenate([tm32(cg), tm32(sg), tm32(cl), tm32(sl)], axis=1)
    ii = np.arange(P)[:, None]
    jj = np.arange(MZ + 512)[None, :]
    mbc = ((jj - MZ) >= ii).astype(np.float32)
    jj2 = np.arange(MZ + 512 + WIN + P)[None, :]
    d2 = jj2 - MZ - ii
    mb2 = ((d2 >= 0) & (d2 <= WIN - 1)).astype(np.float32)
    return ident, trigpack, mbc, mb2


def pack_cvec(ins):
    cols, cmap = [], {}

    def add(name, arr):
        a = np.asarray(arr, dtype=np.float32)
        assert a.size % P == 0, (name, a.shape)
        ncol = a.size // P
        cmap[name] = (sum(c.shape[1] for c in cols), ncol)
        cols.append(a.reshape(ncol, P).T)

    add("ones", np.ones(P))
    add("bin_", ins["L0_attn_bin_"])
    add("g_qq", ins["L0_attn_g_qq"]); add("b_qq", ins["L0_attn_b_qq"])
    add("g_qk", ins["L0_attn_g_qk"]); add("b_qk", ins["L0_attn_b_qk"])
    add("g_lq", ins["L0_attn_g_lq"]); add("b_lq", ins["L0_attn_b_lq"])
    add("g_lk", ins["L0_attn_g_lk"]); add("b_lk", ins["L0_attn_b_lk"])
    add("bout", ins["L0_attn_bout"])
    add("bg", ins["L0_attn_bg"])
    add("b1", ins["L0_ffn_b1"])
    add("gamma", ins["L0_ffn_gamma"]); add("beta", ins["L0_ffn_beta"])
    add("n1w", ins["L1_attn_norm1_w"]); add("n2w", ins["L1_attn_norm2_w"])
    add("eps5", np.full(P, 1e-5)); add("eps6", np.full(P, 1e-6))
    return np.concatenate(cols, axis=1).astype(np.float32), cmap


def pack_crow(ins):
    rows, rmap = [], {}

    def add(name, arr):
        a = np.asarray(arr, dtype=np.float32).reshape(-1)
        rmap[name] = (sum(r.size for r in rows), a.size)
        rows.append(a)

    add("ones", np.ones(P))
    add("bv", ins["L0_attn_bv"])
    add("bg0", ins["L0_ffn_bg"])
    add("rb", ins["L1_ffn_router_b"])
    return np.concatenate(rows).reshape(1, -1).astype(np.float32), rmap


CVEC_NCOL = [None]; CVEC_MAP = [None]
CROW_NCOL = [None]; CROW_MAP = [None]


def _init_pack_shapes():
    z = np.zeros
    dummy = {
        "L0_attn_bin_": z(HEAD), "L0_attn_g_qq": z(HEAD), "L0_attn_b_qq": z(HEAD),
        "L0_attn_g_qk": z(HEAD), "L0_attn_b_qk": z(HEAD), "L0_attn_g_lq": z(HEAD),
        "L0_attn_b_lq": z(HEAD), "L0_attn_g_lk": z(HEAD), "L0_attn_b_lk": z(HEAD),
        "L0_attn_bout": z(D), "L0_attn_bg": z(INNER), "L0_ffn_b1": z((E0, H0)),
        "L0_ffn_gamma": z(D), "L0_ffn_beta": z(D),
        "L1_attn_norm1_w": z(D), "L1_attn_norm2_w": z(D),
        "L0_attn_bv": z(INNER), "L0_ffn_bg": z(E0), "L1_ffn_router_b": z(E1),
    }
    cp, cm = pack_cvec(dummy)
    CVEC_NCOL[0], CVEC_MAP[0] = cp.shape[1], cm
    rp, rm = pack_crow(dummy)
    CROW_NCOL[0], CROW_MAP[0] = rp.shape[1], rm


_init_pack_shapes()

STATC = {"mu0": 0, "rstd0": 16, "mu1": 32, "rstd1": 48, "var1": 64,
         "s1": 80, "s2": 96, "acc1": 112, "tmp": 128}
NSTAT = 128


class K:

    def __init__(self, stage, taps):
        self.stage = stage
        self.taps = taps
        nc = self.nc = bass.Bass("TRN2", target_bir_lowering=False)
        self.ctx = ExitStack()
        self.pr = Prog(nc, self.ctx)

        dr = self.dr = {}
        for nm, sh in (
            ("x_in", (T, D)),
            ("L0_attn_Wv", (D, INNER)), ("L0_attn_Wg", (D, INNER)),
            ("L0_attn_Win", (D, HEAD)), ("L0_attn_Wout", (INNER, D)),
            ("L0_ffn_Wg", (D, E0)), ("L0_ffn_W1", (E0, D, H0)),
            ("L0_ffn_W2", (E0, H0, D)), ("L0_ffn_b2", (E0, D)),
            ("L1_attn_g_wq", (D, D)), ("L1_attn_g_kv_down", (D, LAT)),
            ("L1_attn_g_k_up", (LAT, KVH * HD)), ("L1_attn_g_v_up", (LAT, KVH * HD)),
            ("L1_attn_g_wo", (D, D)), ("L1_attn_l_wq", (D, D)),
            ("L1_attn_l_wk", (D, KVH * HD)), ("L1_attn_l_wv", (D, KVH * HD)),
            ("L1_attn_l_wo", (D, D)),
            ("L1_ffn_router_w", (E1, D)), ("L1_ffn_expert_w", (E1, D, D)),
            ("c_ident", (P, P)), ("c_trig", (P, 4 * NTC * 32)),
            ("c_mbc", (P, MZ + 512)), ("c_mb2", (P, MZ + 512 + WIN + P)),
            ("c_cvec", (P, CVEC_NCOL[0])), ("c_crow", (1, CROW_NCOL[0])),
        ):
            dr[nm] = nc.dram_tensor(nm, list(sh), F32, kind="ExternalInput").ap()
        self.out_h = nc.dram_tensor("out", [T, D], F32, kind="ExternalOutput").ap()
        self.tap_outs = {tn: nc.dram_tensor("tap_" + tn, list(sh), F32, kind="ExternalOutput").ap()
                         for tn, (sh, _f) in taps.items()}

        A = nc.alloc_sbuf_tensor
        self.x_sb = A("x_sb", [P, NDC * T], F32)
        self.xn_sb = A("xn_sb", [P, NDC * T], BF16)
        self.pool = A("pool", [P, 80 * 1024], mybir.dt.uint8)
        self.id_f = A("id_f", [P, P], F32)
        self.id_b = A("id_b", [P, P], BF16)
        self.trig = A("trig_sb", [P, 4 * NTC * 32], F32)
        self.mbc = A("mbc_sb", [P, MZ + 512], BF16)
        self.mb2 = A("mb2_sb", [P, MZ + 512 + WIN + P], BF16)
        self.cvec = A("cvec", [P, CVEC_NCOL[0]], F32)
        self.crow_b = A("crow_b", [1, CROW_NCOL[0]], BF16)
        self.onesf = A("onesf", [1, P], F32)
        self.bg0f = A("bg0f", [1, E0], F32)
        self.wg0 = A("wg0_sb", [P, NDC * E0], F32)
        self.rwt = A("rwt_sb", [P, NDC * E1], BF16)
        self.stats = A("stats", [P, NSTAT], F32)
        self.comb0 = A("comb0", [P, NTC * E0], F32)
        self.w1tm = A("w1tm", [P, NTC * E1], F32)
        self.csb8 = A("csb8", [P, E0], F32)
        self.bgb8 = A("bgb8", [P, E0], F32)
        self.bc_s = A("bc_s", [P, 1024], BF16)
        self.combrow_s = A("combrow_s", [1, 1024], BF16)
        self.w2b_s = A("w2b_s", [P, 2048], BF16)
        self.pb = [nc.alloc_psum_tensor("pb%d" % i, [P, 512], F32) for i in range(8)]
        self.bank_free = {i: None for i in range(8)}

    def psf(self, i, w=512, off=0):
        return self.pb[i][:, off:off + w]

    def psb(self, i, w=1024, off=0):
        return self.pb[i][:].bitcast(BF16)[:, off:off + w]

    def X(self, dc, c0=0, c1=T):
        return self.x_sb[:, dc * T + c0: dc * T + c1]

    def XN(self, dc, c0=0, c1=T):
        return self.xn_sb[:, dc * T + c0: dc * T + c1]

    def CV(self, name, j=0, n=1):
        c0, ncol = CVEC_MAP[0][name]
        assert j + n <= ncol, (name, j, n)
        return self.cvec[:, c0 + j: c0 + j + n]

    def CRB(self, name, j0=0, n=None):
        c0, ncol = CROW_MAP[0][name]
        n = ncol if n is None else n
        return self.crow_b[0:1, c0 + j0: c0 + j0 + n]


    def ST(self, name, j=0, n=1):
        return self.stats[:, STATC[name] + j: STATC[name] + j + n]

    def lp(self, f):
        with self.nc.allow_low_precision(reason="intentional bf16 row math"):
            return f()

    def pv(self, off, dt, w, p0=0, p1=P):
        t = self.pool[:].bitcast(dt)
        sz = {F32: 4, BF16: 2}[dt]
        assert off % sz == 0 and off + w * sz <= 80 * 1024
        return t[p0:p1, off // sz: off // sz + w]

    def trigv(self, which, tc, nrep=None):
        base = {"cosG": 0, "sinG": 1, "cosL": 2, "sinL": 3}[which] * NTC * 32
        ap = self.trig[:, base + tc * 32: base + (tc + 1) * 32]
        if nrep is not None:
            ap = ap.unsqueeze(1).broadcast_to([P, nrep, 32])
        return ap

    def phase0(self):
        pr, nc, dr = self.pr, self.nc, self.dr
        psf, psb, X, XN, CV, ST = self.psf, self.psb, self.X, self.XN, self.CV, self.ST

        _c0, _n = CROW_MAP[0]["bg0"]
        for fn in (
            lambda e: e.dma_start(out=self.id_f[:], in_=dr["c_ident"]),
            lambda e: e.dma_start(out=self.trig[:], in_=dr["c_trig"]),
            lambda e: e.dma_start(out=self.cvec[:], in_=dr["c_cvec"]),
            lambda e: e.dma_start(out=self.bg0f[:], in_=dr["c_crow"][0:1, _c0:_c0 + E0]),
        ):
            pr.op("y", fn, sig="dcy", sigv=16)
        for dc in range(NDC):
            pr.op("y", lambda e, dc=dc: e.dma_start(
                out=self.wg0[:, dc * E0:(dc + 1) * E0],
                in_=dr["L0_ffn_Wg"][dc * P:(dc + 1) * P, :],
                allow_slow_non_contiguous=True), sig="dcy", sigv=16)
        for fn in (
            lambda e: e.dma_start(out=self.id_b[:], in_=dr["c_ident"]),
            lambda e: e.dma_start(out=self.mbc[:], in_=dr["c_mbc"]),
            lambda e: e.dma_start(out=self.mb2[:], in_=dr["c_mb2"]),
            lambda e: e.dma_start(out=self.crow_b[:], in_=dr["c_crow"]),
        ):
            pr.op("g", fn, sig="dcg", sigv=16)
        ev_onf = pr.op("g", lambda e: e.memset(self.onesf[:], 1.0))
        ECY = ("dcy", pr.sems["dcy"][1])
        ECG = ("dcg", pr.sems["dcg"][1])
        self.ECY, self.ECG, self.ev_onf = ECY, ECG, ev_onf
        ev_cv = ECY
        self.ev_cv = ECY
        ev_idf = ECY
        ev_idb = ECG
        self.ev_idf, self.ev_idb = ECY, ECG
        ev_bgf = ECY
        ev_wg0 = [ECY] * NDC

        xtm = [self.pv(i * 4096, F32, 1024) for i in range(4)]
        xnm = [self.pv(16384, BF16, 1024), self.pv(18432, BF16, 1024)]
        dump = [self.pv(20480, F32, 1024), self.pv(24576, F32, 1024)]

        for dc in range(NDC):
            pr.op("t", lambda e, dc=dc: e.matmul(
                psf(6, E0)[0:1, :], CV("ones"), self.wg0[:, dc * E0:(dc + 1) * E0],
                start=(dc == 0), stop=(dc == NDC - 1)),
                waits=[ev_cv, ev_wg0[dc]])
        ev = pr.op("v", lambda e: e.tensor_copy(self.csb8[0:1, :], psf(6, E0)[0:1, :]),
                   waits=[pr.now("t")])
        ev = pr.op("t", lambda e: e.matmul(psf(6, E0), self.onesf[:], self.csb8[0:1, :],
                                           start=True, stop=True),
                   waits=[ev, ev_onf])
        ev_csb = pr.op("v", lambda e: e.tensor_copy(self.csb8[:], psf(6, E0)), waits=[ev])
        ev = pr.op("t", lambda e: e.matmul(psf(6, E0), self.onesf[:], self.bg0f[:],
                                           start=True, stop=True),
                   waits=[ev_csb, ev_bgf])
        ev_bgb = pr.op("v", lambda e: e.tensor_copy(self.bgb8[:], psf(6, E0)), waits=[ev])
        rwraw = self.pv(36864, F32, 1024, 0, 16)
        ev_rwl = pr.op("y", lambda e: e.dma_start(out=rwraw, in_=dr["L1_ffn_router_w"]),
                       sig="drw", sigv=16)
        for dc in range(NDC):
            ev_tt = pr.op("t", lambda e, dc=dc: e.transpose(
                psf(7, 128)[0:P, 0:16], rwraw[0:16, dc * P:(dc + 1) * P],
                self.id_f[0:16, 0:16]),
                waits=[ev_rwl, self.ECY, self.bank_free.get(7)])
            ev_cp = pr.op("v", lambda e, dc=dc: e.tensor_copy(
                self.rwt[:, dc * E1:(dc + 1) * E1], psf(7, 128)[0:P, 0:16]),
                waits=[ev_tt])
            self.bank_free[7] = ev_cp

        slot_free = [[], [], [], []]
        xnm_free = [None, None]
        bank_free = self.bank_free
        for tc in range(NTC):
            sl = tc % 4
            sl2 = tc % 2
            g0 = self.pv(28672 + tc * 192, F32, 48)
            gsc = lambda j, n=1, g0=g0: g0[:, j:j + n]
            tst = lambda j, n=1, g0=g0: g0[:, 32 + j:32 + j + n]
            ev_x = pr.op("y", lambda e, tc=tc, sl=sl: e.dma_start(
                out=xtm[sl], in_=dr["x_in"][tc * P:(tc + 1) * P, :]), sig="dx%d" % sl, sigv=16,
                waits=slot_free[sl])
            ev_sq = pr.op("s", lambda e, sl=sl, sl2=sl2, tst=tst: e.activation(
                dump[sl2], xtm[sl], AF.Square, accum_out=tst(0)), waits=[ev_x])
            ev_sm = pr.op("v", lambda e, sl=sl, tst=tst: e.reduce_sum(tst(1), xtm[sl], axis=AX.X), waits=[ev_x])
            ev_mu = pr.op("v", lambda e, tc=tc, tst=tst: e.tensor_scalar(
                ST("mu0", tc), tst(1), 1.0 / D, None, ALU.mult), waits=[ev_sm])
            ev_e2 = pr.op("v", lambda e, tst=tst: e.tensor_scalar(tst(2), tst(0), 1.0 / D, None, ALU.mult),
                          waits=[ev_sq])
            ev_ms = pr.op("v", lambda e, tc=tc, tst=tst: e.tensor_tensor(
                tst(3), ST("mu0", tc), ST("mu0", tc), ALU.mult), waits=[ev_mu])
            ev_va = pr.op("v", lambda e, tst=tst: e.tensor_tensor(tst(2), tst(2), tst(3), ALU.subtract),
                          waits=[ev_e2, ev_ms])
            ev_sd = pr.op("s", lambda e, tst=tst: e.activation(tst(3), tst(2), AF.Sqrt, bias=CV("eps5")),
                          waits=[ev_va, self.ECY])
            ev_rs = pr.op("v", lambda e, tc=tc, tst=tst: e.reciprocal(ST("rstd0", tc), tst(3)), waits=[ev_sd])
            ev_xn = pr.op("v", lambda e, sl=sl, sl2=sl2, tc=tc: e.tensor_scalar(
                xnm[sl2], xtm[sl], ST("mu0", tc), ST("rstd0", tc), ALU.subtract, ALU.mult),
                waits=[ev_rs, xnm_free[sl2]])
            evs_slot = [ev_sq, ev_xn]
            for dc in range(NDC):
                bk = dc % 2
                ev_t = pr.op("t", lambda e, sl=sl, dc=dc, bk=bk: e.transpose(
                    psf(bk, P), xtm[sl][:, dc * P:(dc + 1) * P], self.id_f[:]),
                    waits=[ev_x, ev_idf, bank_free[bk]])
                ev_c = pr.op("v", lambda e, dc=dc, tc=tc, bk=bk: e.tensor_copy(
                    X(dc, tc * P, (tc + 1) * P), psf(bk, P)), waits=[ev_t])
                bank_free[bk] = ev_c
                evs_slot.append(ev_c)
                bk2 = 2 + dc % 2
                ev_t2 = pr.op("t", lambda e, sl2=sl2, dc=dc, bk2=bk2: e.transpose(
                    psb(bk2, P), xnm[sl2][:, dc * P:(dc + 1) * P], self.id_b[:]),
                    waits=[ev_xn, ev_idb, bank_free[bk2]])
                ev_c2 = pr.op("s", lambda e, dc=dc, tc=tc, bk2=bk2: e.copy(
                    XN(dc, tc * P, (tc + 1) * P), psb(bk2, P)), waits=[ev_t2])
                bank_free[bk2] = ev_c2
                evs_slot.append(ev_c2)
            xnm_free[sl2] = pr.now("t")
            slot_free[sl] = [ev_sq, ev_xn, pr.now("t")]

            for dc in range(NDC):
                pr.op("t", lambda e, dc=dc, tc=tc: e.matmul(
                    psf(4, E0), X(dc, tc * P, (tc + 1) * P),
                    self.wg0[:, dc * E0:(dc + 1) * E0],
                    start=(dc == 0), stop=(dc == NDC - 1)),
                    waits=[pr.now("v"), bank_free[4]])
            ev_nm = pr.op("v", lambda e, tc=tc, tst=tst: e.tensor_scalar(
                tst(4), ST("mu0", tc), ST("rstd0", tc), -1.0, ALU.mult, ALU.mult),
                waits=[ev_rs])
            ev_a2 = pr.op("v", lambda e, tc=tc, gsc=gsc: e.tensor_scalar(
                gsc(0, 8), psf(4, E0), ST("rstd0", tc), None, ALU.mult),
                waits=[pr.now("t"), ev_rs])
            bank_free[4] = ev_a2
            ev_lg = pr.op("v", lambda e, tst=tst, gsc=gsc: e.scalar_tensor_tensor(
                gsc(0, 8), self.csb8[:], tst(4), gsc(0, 8), ALU.mult, ALU.add),
                waits=[ev_a2, ev_nm, ev_csb])
            ev_lg = pr.op("v", lambda e, gsc=gsc: e.tensor_tensor(gsc(0, 8), gsc(0, 8), self.bgb8[:], ALU.add),
                          waits=[ev_lg, ev_bgb])
            ev_m8 = pr.op("v", lambda e, gsc=gsc: e.max(out=gsc(8, 8), in_=gsc(0, 8)), waits=[ev_lg])
            ev_nv = pr.op("v", lambda e, tst=tst, gsc=gsc: e.tensor_scalar(tst(5), gsc(8, 1), -1.0, None, ALU.mult),
                          waits=[ev_m8])
            ev_ex = pr.op("s", lambda e, tst=tst, gsc=gsc: e.activation(gsc(16, 8), gsc(0, 8), AF.Exp, bias=tst(5)),
                          waits=[ev_nv, ev_lg])
            ev_in = pr.op("v", lambda e, gsc=gsc: e.tensor_scalar(gsc(24, 8), gsc(0, 8), gsc(9, 1), None, ALU.is_ge),
                          waits=[ev_m8])
            ev_wr = pr.op("v", lambda e, gsc=gsc: e.tensor_tensor(gsc(24, 8), gsc(16, 8), gsc(24, 8), ALU.mult),
                          waits=[ev_ex, ev_in])
            ev_s2 = pr.op("v", lambda e, tst=tst, gsc=gsc: e.reduce_sum(tst(6), gsc(24, 8), axis=AX.X), waits=[ev_wr])
            ev_rc = pr.op("v", lambda e, tst=tst: e.reciprocal(tst(7), tst(6)), waits=[ev_s2])
            pr.op("v", lambda e, tc=tc, tst=tst, gsc=gsc: e.tensor_scalar(
                self.comb0[:, tc * E0:(tc + 1) * E0], gsc(24, 8), tst(7), None, ALU.mult),
                waits=[ev_rc])
        self.bank_free = bank_free

    def phase_out(self):
        pr = self.pr
        psf, X = self.psf, self.X
        otm = [self.pv(0, F32, 1024), self.pv(4096, F32, 1024)]
        prev_store = [None, None]
        for tc in range(NTC):
            sl = tc % 2
            evs = []
            for dc in range(NDC):
                bk = dc % 2
                ev_t = pr.op("t", lambda e, dc=dc, tc=tc, bk=bk: e.transpose(
                    psf(bk, P), X(dc, tc * P, (tc + 1) * P), self.id_f[:]),
                    waits=[self.bank_free.get(bk)])
                ev_c = pr.op("v", lambda e, dc=dc, bk=bk, sl=sl: e.tensor_copy(
                    otm[sl][:, dc * P:(dc + 1) * P], psf(bk, P)),
                    waits=[ev_t, prev_store[sl]])
                self.bank_free[bk] = ev_c
                evs.append(ev_c)
            prev_store[sl] = pr.op("y", lambda e, tc=tc, sl=sl: e.dma_start(
                out=self.out_h[tc * P:(tc + 1) * P, :], in_=otm[sl]), sig="so%d" % sl,
                sigv=16, waits=evs)

        tapsems = []
        for i, (tn, (shape, fn)) in enumerate(self.taps.items()):
            ap = fn(self)
            sg = "sT%d" % i
            pr.op("g", lambda e, ap=ap, tn=tn: e.dma_start(
                out=self.tap_outs[tn], in_=ap, allow_slow_non_contiguous=True),
                sig=sg, sigv=16,
                waits=[pr.now("v"), pr.now("s"), pr.now("t")])
            tapsems.append((sg, 16))
        pr.wait("y", prev_store[0], prev_store[1], *tapsems)

    def build(self):
        self.phase0()
        self.pr.barrier("tvsgy")
        if self.stage >= 1:
            self.l0_attn()
            self.pr.barrier("tvsgy")
        if self.stage >= 2:
            self.l0_ffn()
            self.pr.barrier("tvsgy")
        if self.stage >= 3:
            self.ln1()
            self.pr.barrier("tvsgy")
        if self.stage >= 4:
            self.l1_ffn()
            self.pr.barrier("tvsgy")
        if self.stage >= 5:
            self.l1_attn()
            self.pr.barrier("tvsgy")
        self.phase_out()
        with self.nc.Block() as block:
            self.pr.flush(block)
        self.ctx.close()
        for name, (h, cnt) in self.pr.sems.items():
            assert cnt < 60000, (name, cnt)
        return self.nc

    def l0_attn(self):
        pr, dr = self.pr, self.dr
        psf, psb, X, XN, CV, CRB = self.psf, self.psb, self.X, self.XN, self.CV, self.CRB
        EH = INNER // 2

        win = self.pv(0, BF16, 8 * 128)
        wv = self.pv(2048, BF16, 8 * 1024)
        wg = self.pv(18432, BF16, 8 * 1024)
        wo = self.pv(34816, BF16, 8 * 1024)
        xf = self.pv(51200, BF16, 256)
        qq = self.pv(51712, BF16, 256)
        qk = self.pv(52224, BF16, 256)
        lq = self.pv(52736, BF16, 256)
        lkf = self.pv(53248, BF16, 256)
        lkt = self.pv(53760, BF16, 256)
        at = self.pv(54272, BF16, 512)
        vt = self.pv(55296, BF16, 2048)
        gate = self.pv(59392, BF16, 2048)
        osb = self.pv(63488, BF16, 2048)
        KV = self.pv(67584, F32, 1024)
        KVb = self.pv(71680, BF16, 1024)
        rel = self.pv(73728, F32, 256)

        for dc in range(NDC):
            pr.op("s", lambda e, dc=dc: e.activation(
                X(dc), X(dc), AF.Identity, bias=CV("bout", dc)), waits=[self.ECY])

        for dc in range(NDC):
            pr.op("g", lambda e, dc=dc: e.dma_start(
                out=win[:, dc * 128:(dc + 1) * 128],
                in_=dr["L0_attn_Win"][dc * P:(dc + 1) * P, :]), sig="dwin", sigv=16)
        ev_win = ("dwin", pr.sems["dwin"][1])

        wfree = {"wv": None, "wg": None, "wo": None}
        for eh in range(2):
            for dc in range(NDC):
                pr.op("g", lambda e, dc=dc, eh=eh: e.dma_start(
                    out=wv[:, dc * 1024:(dc + 1) * 1024],
                    in_=dr["L0_attn_Wv"][dc * P:(dc + 1) * P, eh * EH:(eh + 1) * EH]),
                    sig="dwv", sigv=16, waits=[wfree["wv"]])
            ev_wv = ("dwv", pr.sems["dwv"][1])
            for dc in range(NDC):
                pr.op("g", lambda e, dc=dc, eh=eh: e.dma_start(
                    out=wg[:, dc * 1024:(dc + 1) * 1024],
                    in_=dr["L0_attn_Wg"][dc * P:(dc + 1) * P, eh * EH:(eh + 1) * EH]),
                    sig="dwg", sigv=16, waits=[wfree["wg"]])
            ev_wg = ("dwg", pr.sems["dwg"][1])
            for ec in range(8):
                pr.op("g", lambda e, ec=ec, eh=eh: e.dma_start(
                    out=wo[:, ec * 1024:(ec + 1) * 1024],
                    in_=dr["L0_attn_Wout"][eh * EH + ec * P: eh * EH + (ec + 1) * P, :]),
                    sig="dwo", sigv=16, waits=[wfree["wo"]])
            ev_wo = ("dwo", pr.sems["dwo"][1])

            ev_kvb_prev = None
            vl_reads = []
            free = {k: None for k in ("xf", "qq", "qk", "lq", "lkf", "lkt",
                                      "at", "vt", "gate", "osb", "rel")}
            last_wv_mm = None
            last_wg_mm = None
            last_wo_mm = None
            for g in range(NG):
                c0 = g * GRP
                for dc in range(NDC):
                    ev_mm = pr.op("t", lambda e, dc=dc, c0=c0: e.matmul(
                        psf(0, 256), win[:, dc * 128:(dc + 1) * 128],
                        XN(dc, c0, c0 + GRP),
                        start=(dc == 0), stop=(dc == NDC - 1)),
                        waits=[ev_win, free["xf"], self.bank_free.get(0)])
                ev_xf = pr.op("s", lambda e: e.activation(
                    xf, psf(0, 256), AF.Silu, bias=CV("bin_")), waits=[ev_mm])
                ev_qq = pr.op("v", lambda e: e.tensor_scalar(
                    qq, xf, CV("g_qq"), CV("b_qq"), ALU.mult, ALU.add),
                    waits=[ev_xf, free["qq"]])
                ev_qk = pr.op("v", lambda e: e.tensor_scalar(
                    qk, xf, CV("g_qk"), CV("b_qk"), ALU.mult, ALU.add),
                    waits=[ev_xf, free["qk"]])
                ev_lq = pr.op("v", lambda e: e.tensor_scalar(
                    lq, xf, CV("g_lq"), CV("b_lq"), ALU.mult, ALU.add),
                    waits=[ev_xf, free["lq"]])
                ev_lkf = pr.op("v", lambda e: e.tensor_scalar(
                    lkf, xf, CV("g_lk"), CV("b_lk"), ALU.mult, ALU.add),
                    waits=[ev_xf, free["lkf"]])
                free["xf"] = pr.now("v")
                ev_lkt = []
                for cc in range(2):
                    bk = 2 + cc
                    ev_tt = pr.op("t", lambda e, cc=cc, bk=bk: e.transpose(
                        psb(bk, 128), lkf[:, cc * 128:(cc + 1) * 128], self.id_b[:]),
                        waits=[ev_lkf, self.bank_free.get(bk), self.ECG])
                    ev_cp = pr.op("v", lambda e, cc=cc, bk=bk: e.tensor_copy(
                        lkt[:, cc * 128:(cc + 1) * 128], psb(bk, 128)),
                        waits=[ev_tt, free["lkt"]])
                    self.bank_free[bk] = ev_cp
                    ev_lkt.append(ev_cp)
                free["lkf"] = pr.now("t")
                ev_at = []
                for mc in range(2):
                    bk = mc
                    ev_mm = pr.op("t", lambda e, mc=mc, bk=bk: e.matmul(
                        psf(bk, 256), qk[:, mc * 128:(mc + 1) * 128], qq[:],
                        start=True, stop=True),
                        waits=[ev_qq, ev_qk, self.bank_free.get(bk), free["at"] if mc == 0 else None])
                    ev_rl = pr.op("s", lambda e, bk=bk: e.activation(
                        rel, psf(bk, 256), AF.Relu, scale=INV_S2048),
                        waits=[ev_mm, free["rel"]])
                    self.bank_free[bk] = ev_rl
                    ev_sq = pr.op("v", lambda e, mc=mc: e.tensor_tensor(
                        at[:, mc * 256:(mc + 1) * 256], rel, rel, ALU.mult),
                        waits=[ev_rl, free["at"]])
                    free["rel"] = ev_sq
                    ev_mk = pr.op("v", lambda e, mc=mc: e.tensor_tensor(
                        at[:, mc * 256:(mc + 1) * 256], at[:, mc * 256:(mc + 1) * 256],
                        self.mbc[:, MZ - 128 * mc: MZ - 128 * mc + 256], ALU.mult),
                        waits=[ev_sq])
                    ev_at.append(ev_mk)
                free["qq"] = pr.now("t")
                free["qk"] = pr.now("t")
                ev_vt = []
                for cc in range(2):
                    for ti in range(2):
                        bk = 2 + (2 * cc + ti) % 2
                        for dc in range(NDC):
                            ev_mm = pr.op("t", lambda e, dc=dc, cc=cc, ti=ti, c0=c0, bk=bk: e.matmul(
                                psf(bk, 512), XN(dc, c0 + cc * P, c0 + (cc + 1) * P),
                                wv[:, dc * 1024 + ti * 512: dc * 1024 + ti * 512 + 512],
                                start=(dc == 0), stop=False),
                                waits=[ev_wv, self.bank_free.get(bk), free["vt"] if (cc == 0 and ti == 0) else None])
                            last_wv_mm = ev_mm
                        ev_mm = pr.op("t", lambda e, cc=cc, ti=ti, eh=eh, bk=bk: e.matmul(
                            psf(bk, 512), CRB("ones", 0, P),
                            CRB("bv", eh * EH + ti * 512, 512),
                            start=False, stop=True),
                            waits=[self.ECG])
                        last_wv_mm = ev_mm
                        ev_ev = pr.op("s", lambda e, cc=cc, ti=ti, bk=bk: e.activation(
                            vt[:, cc * 1024 + ti * 512: cc * 1024 + ti * 512 + 512],
                            psf(bk, 512), AF.Silu), waits=[ev_mm, free["vt"]])
                        self.bank_free[bk] = ev_ev
                        ev_vt.append(ev_ev)
                ev_gate = []
                for ec in range(8):
                    bk = 4 + ec % 2
                    for dc in range(NDC):
                        ev_mm = pr.op("t", lambda e, dc=dc, ec=ec, c0=c0, bk=bk: e.matmul(
                            psf(bk, 256), wg[:, dc * 1024 + ec * 128: dc * 1024 + ec * 128 + 128],
                            XN(dc, c0, c0 + GRP),
                            start=(dc == 0), stop=(dc == NDC - 1)),
                            waits=[ev_wg, self.bank_free.get(bk), free["gate"] if ec < 2 else None])
                        last_wg_mm = ev_mm
                    ev_ev = pr.op("s", lambda e, ec=ec, eh=eh, bk=bk: e.activation(
                        gate[:, ec * 256:(ec + 1) * 256], psf(bk, 256), AF.Silu,
                        bias=CV("bg", eh * 8 + ec)), waits=[ev_mm, free["gate"]])
                    self.bank_free[bk] = ev_ev
                    ev_gate.append(ev_ev)
                ev_kv = []
                for ti in range(2):
                    bk = 6 + ti
                    for cc in range(2):
                        ev_mm = pr.op("t", lambda e, cc=cc, ti=ti, bk=bk: e.matmul(
                            psf(bk, 512), lkt[:, cc * 128:(cc + 1) * 128],
                            vt[:, cc * 1024 + ti * 512: cc * 1024 + ti * 512 + 512],
                            start=(cc == 0), stop=(cc == 1)),
                            waits=[ev_lkt[cc], ev_vt[2 * cc + ti], self.bank_free.get(bk)])
                    ev_kv.append(ev_mm)
                free["lkt"] = pr.now("t")
                vl_reads = []
                ev_osb = []
                for ec in range(8):
                    bk = ec % 2
                    ev_m0 = pr.op("t", lambda e, ec=ec, bk=bk: e.matmul(
                        psf(bk, 256), vt[:, 0 * 1024 + ec * 128: 0 * 1024 + ec * 128 + 128],
                        at[:, 0:256], start=True, stop=False),
                        waits=[ev_at[0], ev_vt[0], ev_vt[1], self.bank_free.get(bk),
                               free["osb"] if ec < 2 else None])
                    ev_m1 = pr.op("t", lambda e, ec=ec, bk=bk, st=(g == 0): e.matmul(
                        psf(bk, 256), vt[:, 1 * 1024 + ec * 128: 1 * 1024 + ec * 128 + 128],
                        at[:, 256:512], start=False, stop=st),
                        waits=[ev_at[1], ev_vt[2], ev_vt[3]])
                    if g > 0:
                        ev_m1 = pr.op("t", lambda e, ec=ec, bk=bk: e.matmul(
                            psf(bk, 256), KVb[:, ec * 128:(ec + 1) * 128], lq[:],
                            start=False, stop=True),
                            waits=[ev_kvb_prev, ev_lq])
                        vl_reads.append(ev_m1)
                    ev_ev = pr.op("v", lambda e, ec=ec, bk=bk: e.tensor_tensor(
                        osb[:, ec * 256:(ec + 1) * 256], psf(bk, 256),
                        gate[:, ec * 256:(ec + 1) * 256], ALU.mult),
                        waits=[ev_m1, ev_gate[ec], free["osb"]])
                    self.bank_free[bk] = ev_ev
                    ev_osb.append(ev_ev)
                free["lq"] = pr.now("t")
                free["at"] = pr.now("t")
                free["vt"] = pr.now("t")
                free["gate"] = pr.now("v")
                if g == 0:
                    ev_u0 = pr.op("v", lambda e: e.tensor_copy(KV[:, 0:512], psf(6, 512)),
                                  waits=[ev_kv[0]])
                    ev_u1 = pr.op("v", lambda e: e.tensor_copy(KV[:, 512:1024], psf(7, 512)),
                                  waits=[ev_kv[1]])
                else:
                    ev_u0 = pr.op("v", lambda e: e.tensor_tensor(
                        KV[:, 0:512], KV[:, 0:512], psf(6, 512), ALU.add),
                        waits=[ev_kv[0]])
                    ev_u1 = pr.op("v", lambda e: e.tensor_tensor(
                        KV[:, 512:1024], KV[:, 512:1024], psf(7, 512), ALU.add),
                        waits=[ev_kv[1]])
                self.bank_free[6] = ev_u0
                self.bank_free[7] = ev_u1
                if g < NG - 1:
                    ev_kvb_prev = pr.op("v", lambda e: e.tensor_copy(KVb[:], KV[:]),
                                        waits=[ev_u0, ev_u1] + vl_reads)
                ev_wouts = []
                for dc in range(NDC):
                    bk = 2 + dc % 2
                    for ec in range(8):
                        ev_mm = pr.op("t", lambda e, dc=dc, ec=ec, bk=bk: e.matmul(
                            psf(bk, 256), wo[:, ec * 1024 + dc * 128: ec * 1024 + dc * 128 + 128],
                            osb[:, ec * 256:(ec + 1) * 256],
                            start=(ec == 0), stop=(ec == 7)),
                            waits=[ev_wo, ev_osb[ec], self.bank_free.get(bk)])
                        last_wo_mm = ev_mm
                    ev_ad = pr.op("v", lambda e, dc=dc, c0=c0, bk=bk: e.tensor_tensor(
                        X(dc, c0, c0 + GRP), X(dc, c0, c0 + GRP), psf(bk, 256), ALU.add),
                        waits=[ev_mm])
                    self.bank_free[bk] = ev_ad
                    ev_wouts.append(ev_ad)
                free["osb"] = pr.now("t")
                free["qq"] = free["qk"] = free["lq"] = pr.now("t")
            wfree = {"wv": last_wv_mm, "wg": last_wg_mm, "wo": last_wo_mm}

    def l0_ffn(self):
        pr, dr = self.pr, self.dr
        psf, psb, X, XN, CV, CRB = self.psf, self.psb, self.X, self.XN, self.CV, self.CRB

        acc = self.pv(0, F32, 8 * 1024)
        hp = self.pv(32768, BF16, 16 * 1024)
        w1b = [self.pv(65536, BF16, 8 * 256), self.pv(69632, BF16, 8 * 256)]
        w2b = [self.pv(73728, BF16, 2048), self.pv(77824, BF16, 2048),
               self.w2b_s[:]]
        bc = self.bc_s[:]
        combrow = self.combrow_s[0:1, :]

        for th in range(2):
            t0 = th * 1024
            first = [True] * NDC
            w1free = [None, None]
            w2free = [None, None, None]
            w1n = 0
            w2n = 0
            hp_free = None
            bc_free = None
            cr_free = None
            for e in range(E0):
                ev_cr = []
                for tcl in range(8):
                    tcg = th * 8 + tcl
                    bkr = 6
                    ev_tt = pr.op("t", lambda e2, tcg=tcg, e=e, bkr=bkr: e2.transpose(
                        psf(bkr, P)[0:1, :], self.comb0[:, tcg * E0 + e: tcg * E0 + e + 1],
                        self.id_f[:]),
                        waits=[self.bank_free.get(bkr), cr_free])
                    ev_cp = pr.op("v", lambda e2, tcl=tcl, bkr=bkr: e2.tensor_copy(
                        combrow[0:1, tcl * P:(tcl + 1) * P], psf(bkr, P)[0:1, :]),
                        waits=[ev_tt, cr_free])
                    self.bank_free[bkr] = ev_cp
                    ev_cr.append(ev_cp)
                ev_bc = []
                for tt in range(2):
                    bkr = 7
                    ev_mm = pr.op("t", lambda e2, tt=tt, bkr=bkr: e2.matmul(
                        psf(bkr, 512), CRB("ones", 0, P), combrow[0:1, tt * 512: tt * 512 + 512],
                        start=True, stop=True),
                        waits=ev_cr + [self.bank_free.get(bkr), self.ECG])
                    cr_free = ev_mm
                    ev_cp = pr.op("v", lambda e2, tt=tt, bkr=bkr: e2.tensor_copy(
                        bc[:, tt * 512: tt * 512 + 512], psf(bkr, 512)),
                        waits=[ev_mm, bc_free])
                    self.bank_free[bkr] = ev_cp
                    ev_bc.append(ev_cp)
                for hh in range(2):
                    ev_hp = {}
                    for hs in range(8):
                        sl = w1n % 2
                        ev_w1 = pr.op("g", lambda e2, e=e, hh=hh, hs=hs, sl=sl: e2.dma_start(
                            out=w1b[sl].rearrange("p (dc h) -> p dc h", h=256),
                            in_=dr["L0_ffn_W1"][e, :, hh * 2048 + hs * 256: hh * 2048 + (hs + 1) * 256].rearrange(
                                "(dc p) h -> p dc h", p=P)),
                            sig="dw1%d" % sl, sigv=16, waits=[w1free[sl]])
                        w1n += 1
                        last_mm = None
                        for hc2 in range(2):
                            hcl = hs * 2 + hc2
                            for tt in range(2):
                                bk = (2 * hc2 + tt) % 2
                                for dc in range(NDC):
                                    ev_mm = pr.op("t", lambda e2, dc=dc, hc2=hc2, tt=tt, t0=t0, sl=sl, bk=bk: e2.matmul(
                                        psf(bk, 512),
                                        w1b[sl][:, dc * 256 + hc2 * 128: dc * 256 + hc2 * 128 + 128],
                                        XN(dc, t0 + tt * 512, t0 + (tt + 1) * 512),
                                        start=(dc == 0), stop=(dc == NDC - 1)),
                                        waits=[ev_w1, self.bank_free.get(bk), hp_free if hcl == 0 and tt == 0 else None])
                                last_mm = ev_mm
                                ev_gl = pr.op("s", lambda e2, hcl=hcl, tt=tt, e=e, hh=hh, bk=bk: e2.activation(
                                    hp[:, hcl * 1024 + tt * 512: hcl * 1024 + tt * 512 + 512],
                                    psf(bk, 512), AF.Gelu, bias=CV("b1", e * 32 + hh * 16 + hcl)),
                                    waits=[ev_mm, hp_free])
                                self.bank_free[bk] = ev_gl
                                ev_tt2 = pr.op("v", lambda e2, hcl=hcl, tt=tt: e2.tensor_tensor(
                                    hp[:, hcl * 1024 + tt * 512: hcl * 1024 + tt * 512 + 512],
                                    hp[:, hcl * 1024 + tt * 512: hcl * 1024 + tt * 512 + 512],
                                    bc[:, tt * 512: tt * 512 + 512], ALU.mult),
                                    waits=[ev_gl, ev_bc[tt]])
                                ev_hp[(hcl, tt)] = ev_tt2
                        bc_free = ev_tt2
                        w1free[sl] = last_mm
                    for tt in range(2):
                        ev_dc = {}
                        for hp2 in range(8):
                            sl = w2n % 3
                            ev_w2 = pr.op("g", lambda e2, e=e, hh=hh, hp2=hp2, sl=sl: e2.dma_start(
                                out=w2b[sl].rearrange("p (j c) -> p j c", c=1024),
                                in_=dr["L0_ffn_W2"][e, hh * 2048 + hp2 * 256: hh * 2048 + (hp2 + 1) * 256, :].rearrange(
                                    "(j p) c -> p j c", p=P)),
                                sig="dw2%d" % sl, sigv=16, waits=[w2free[sl]])
                            w2n += 1
                            for j in range(2):
                                hcl = hp2 * 2 + j
                                for dc in range(NDC):
                                    ev_mm = pr.op("t", lambda e2, dc=dc, hcl=hcl, tt=tt, sl=sl, j=j: e2.matmul(
                                        psf(dc, 512), w2b[sl][:, j * 1024 + dc * 128: j * 1024 + dc * 128 + 128],
                                        hp[:, hcl * 1024 + tt * 512: hcl * 1024 + tt * 512 + 512],
                                        start=(hcl == 0), stop=(hcl == 15)),
                                        waits=[ev_w2, ev_hp[(hcl, tt)],
                                               self.bank_free.get(dc) if hcl == 0 else None])
                                    ev_dc[dc] = ev_mm
                            w2free[sl] = ev_mm
                        hp_free = ev_mm
                        for dc in range(NDC):
                            if first[dc] and tt == 0:
                                pass
                            if first[dc]:
                                ev_ad = pr.op("v", lambda e2, dc=dc, tt=tt, t0=t0: e2.tensor_copy(
                                    acc[:, dc * 1024 + tt * 512: dc * 1024 + tt * 512 + 512],
                                    psf(dc, 512)), waits=[ev_dc[dc]])
                            else:
                                ev_ad = pr.op("v", lambda e2, dc=dc, tt=tt, t0=t0: e2.tensor_tensor(
                                    acc[:, dc * 1024 + tt * 512: dc * 1024 + tt * 512 + 512],
                                    acc[:, dc * 1024 + tt * 512: dc * 1024 + tt * 512 + 512],
                                    psf(dc, 512), ALU.add), waits=[ev_dc[dc]])
                            self.bank_free[dc] = ev_ad
                        if tt == 1:
                            for dc in range(NDC):
                                first[dc] = False
                    if hh == 1:
                        for dc in range(NDC):
                            first[dc] = False
            comb8 = self.pv(32768, BF16, 1024, 0, E0)
            b2w = self.pv(36864, BF16, 1024, 0, E0)
            ev_b2l = pr.op("g", lambda e2: e2.dma_start(out=b2w, in_=dr["L0_ffn_b2"]),
                           sig="db2", sigv=16, waits=[hp_free])
            ev_c8 = []
            for tcl in range(8):
                tc = th * 8 + tcl
                bkr = 2 + tcl % 2
                ev_tt3 = pr.op("t", lambda e2, tc=tc, bkr=bkr: e2.transpose(
                    psf(bkr, 128)[0:E0, :], self.comb0[:, tc * E0:(tc + 1) * E0],
                    self.id_f[:]),
                    waits=[self.bank_free.get(bkr), hp_free])
                ev_cp3 = pr.op("v", lambda e2, tcl=tcl, bkr=bkr: e2.tensor_copy(
                    comb8[:, tcl * P:(tcl + 1) * P], psf(bkr, 128)[0:E0, :]),
                    waits=[ev_tt3, hp_free])
                self.bank_free[bkr] = ev_cp3
                ev_c8.append(ev_cp3)
            for dco in range(NDC):
                for tt in range(2):
                    bkr = 4 + (2 * dco + tt) % 2
                    ev_mmb = pr.op("t", lambda e2, dco=dco, tt=tt, bkr=bkr: e2.matmul(
                        psf(bkr, 512), b2w[:, dco * 128:(dco + 1) * 128],
                        comb8[:, tt * 512:(tt + 1) * 512],
                        start=True, stop=True),
                        waits=ev_c8 + [ev_b2l, self.bank_free.get(bkr)])
                    ev_ab = pr.op("v", lambda e2, dco=dco, tt=tt, bkr=bkr: e2.tensor_tensor(
                        acc[:, dco * 1024 + tt * 512: dco * 1024 + tt * 512 + 512],
                        acc[:, dco * 1024 + tt * 512: dco * 1024 + tt * 512 + 512],
                        psf(bkr, 512), ALU.add),
                        waits=[ev_mmb, self.bank_free.get(dco)])
                    self.bank_free[bkr] = ev_ab
            ev_r = []
            for dc in range(NDC):
                ev = pr.op("v", lambda e2, dc=dc, t0=t0: e2.tensor_tensor(
                    acc[:, dc * 1024:(dc + 1) * 1024],
                    acc[:, dc * 1024:(dc + 1) * 1024],
                    XN(dc, t0, t0 + 1024), ALU.add), waits=[pr.now("v")])
                ev_r.append(ev)
            rtm = self.pv(32768, F32, 1024)
            lntm = self.pv(36864, F32, 1024)
            ttmp = self.pv(40960, F32, 128)
            sc9 = self.pv(41472, F32, 8)
            for tcl in range(8):
                tc = th * 8 + tcl
                evs_t = []
                for dc in range(NDC):
                    bk = dc % 2
                    ev_tt = pr.op("t", lambda e2, dc=dc, tcl=tcl, bk=bk: e2.transpose(
                        psf(bk, P), acc[:, dc * 1024 + tcl * P: dc * 1024 + (tcl + 1) * P],
                        self.id_f[:]),
                        waits=[ev_r[dc], self.bank_free.get(bk)])
                    ev_cp = pr.op("v", lambda e2, dc=dc, bk=bk: e2.tensor_copy(
                        rtm[:, dc * P:(dc + 1) * P], psf(bk, P)), waits=[ev_tt])
                    self.bank_free[bk] = ev_cp
                    evs_t.append(ev_cp)
                ev_sq = pr.op("s", lambda e2: e2.activation(
                    lntm, rtm, AF.Square, accum_out=sc9[:, 0:1]), waits=evs_t)
                ev_sm = pr.op("v", lambda e2: e2.reduce_sum(sc9[:, 1:2], rtm, axis=AX.X), waits=evs_t)
                ev_mu = pr.op("v", lambda e2: e2.tensor_scalar(
                    sc9[:, 2:3], sc9[:, 1:2], 1.0 / D, None, ALU.mult), waits=[ev_sm])
                ev_e2 = pr.op("v", lambda e2: e2.tensor_scalar(
                    sc9[:, 3:4], sc9[:, 0:1], 1.0 / D, None, ALU.mult), waits=[ev_sq])
                ev_ms = pr.op("v", lambda e2: e2.tensor_tensor(
                    sc9[:, 4:5], sc9[:, 2:3], sc9[:, 2:3], ALU.mult), waits=[ev_mu])
                ev_va = pr.op("v", lambda e2: e2.tensor_tensor(
                    sc9[:, 3:4], sc9[:, 3:4], sc9[:, 4:5], ALU.subtract), waits=[ev_e2, ev_ms])
                ev_sd = pr.op("s", lambda e2: e2.activation(
                    sc9[:, 4:5], sc9[:, 3:4], AF.Sqrt, bias=CV("eps5")), waits=[ev_va])
                ev_rs = pr.op("v", lambda e2: e2.reciprocal(sc9[:, 5:6], sc9[:, 4:5]), waits=[ev_sd])
                ev_ln = pr.op("v", lambda e2: e2.tensor_scalar(
                    lntm, rtm, sc9[:, 2:3], sc9[:, 5:6], ALU.subtract, ALU.mult),
                    waits=[ev_rs, ev_sq])
                for dc in range(NDC):
                    bk = 2 + dc % 2
                    ev_tt = pr.op("t", lambda e2, dc=dc, bk=bk: e2.transpose(
                        psf(bk, P), lntm[:, dc * P:(dc + 1) * P], self.id_f[:]),
                        waits=[ev_ln, self.bank_free.get(bk)])
                    ev_gm = pr.op("s", lambda e2, dc=dc, bk=bk: e2.activation(
                        ttmp, psf(bk, P), AF.Identity, scale=CV("gamma", dc), bias=CV("beta", dc)),
                        waits=[ev_tt])
                    self.bank_free[bk] = ev_gm
                    ev_ax = pr.op("v", lambda e2, dc=dc, tc=tc: e2.tensor_tensor(
                        X(dc, tc * P, (tc + 1) * P), X(dc, tc * P, (tc + 1) * P),
                        ttmp, ALU.add), waits=[ev_gm])
                    pr.wait("s", ev_ax)

    def ln1(self):
        pr = self.pr
        psf, psb, X, XN, CV, ST = self.psf, self.psb, self.X, self.XN, self.CV, self.ST
        CRB = self.CRB
        self.ev_w1tm_ln1 = {}
        xtm = [self.pv(0, F32, 1024), self.pv(4096, F32, 1024)]
        xnm = [self.pv(8192, BF16, 1024), self.pv(10240, BF16, 1024)]
        dump = [self.pv(12288, F32, 1024), self.pv(16384, F32, 1024)]
        slot_free = [None, None]
        slot_vfree = [None, None]
        xnm_free = [None, None]
        for tc in range(NTC):
            sl = tc % 2
            g0 = self.pv(20480 + tc * 64, F32, 16)
            tst = lambda j, n=1, g0=g0: g0[:, j:j + n]
            evs_w = []
            for dc in range(NDC):
                bk = dc % 2
                ev_tt = pr.op("t", lambda e, dc=dc, tc=tc, bk=bk: e.transpose(
                    psf(bk, P), X(dc, tc * P, (tc + 1) * P), self.id_f[:]),
                    waits=[self.bank_free.get(bk)])
                ev_cp = pr.op("v", lambda e, dc=dc, bk=bk, sl=sl: e.tensor_copy(
                    xtm[sl][:, dc * P:(dc + 1) * P], psf(bk, P)),
                    waits=[ev_tt, slot_free[sl], slot_vfree[sl]])
                self.bank_free[bk] = ev_cp
                evs_w.append(ev_cp)
            ev_sq = pr.op("s", lambda e, sl=sl, tst=tst: e.activation(
                dump[sl], xtm[sl], AF.Square, accum_out=tst(0)), waits=evs_w)
            ev_sm = pr.op("v", lambda e, sl=sl, tst=tst: e.reduce_sum(tst(1), xtm[sl], axis=AX.X),
                          waits=evs_w)
            ev_mu = pr.op("v", lambda e, tc=tc, tst=tst: e.tensor_scalar(
                ST("mu1", tc), tst(1), 1.0 / D, None, ALU.mult), waits=[ev_sm])
            ev_e2 = pr.op("v", lambda e, tst=tst: e.tensor_scalar(tst(2), tst(0), 1.0 / D, None, ALU.mult),
                          waits=[ev_sq])
            ev_ms = pr.op("v", lambda e, tc=tc, tst=tst: e.tensor_tensor(
                tst(3), ST("mu1", tc), ST("mu1", tc), ALU.mult), waits=[ev_mu])
            ev_va = pr.op("v", lambda e, tc=tc, tst=tst: e.tensor_tensor(
                ST("var1", tc), tst(2), tst(3), ALU.subtract), waits=[ev_e2, ev_ms])
            ev_sd = pr.op("s", lambda e, tc=tc, tst=tst: e.activation(
                tst(3), ST("var1", tc), AF.Sqrt, bias=CV("eps5")), waits=[ev_va])
            ev_rs = pr.op("v", lambda e, tc=tc, tst=tst: e.reciprocal(ST("rstd1", tc), tst(3)),
                          waits=[ev_sd])
            ev_xn = pr.op("v", lambda e, sl=sl, tc=tc: e.tensor_scalar(
                xnm[sl], xtm[sl], ST("mu1", tc), ST("rstd1", tc), ALU.subtract, ALU.mult),
                waits=[ev_rs, xnm_free[sl]])
            evs_c2 = []
            for dc in range(NDC):
                bk2 = 2 + dc % 2
                ev_t2 = pr.op("t", lambda e, sl=sl, dc=dc, bk2=bk2: e.transpose(
                    psb(bk2, P), xnm[sl][:, dc * P:(dc + 1) * P], self.id_b[:]),
                    waits=[ev_xn, self.bank_free.get(bk2)])
                ev_c2 = pr.op("s", lambda e, dc=dc, tc=tc, bk2=bk2: e.copy(
                    XN(dc, tc * P, (tc + 1) * P), psb(bk2, P)), waits=[ev_t2])
                self.bank_free[bk2] = ev_c2
                evs_c2.append(ev_c2)
            xnm_free[sl] = pr.now("t")
            slot_free[sl] = pr.now("s")
            slot_vfree[sl] = ev_xn
            g0g = self.pv(60416 + tc * 192, F32, 48)
            gc = lambda j, n=1, g0g=g0g: g0g[:, j:j + n]
            bkg = 4 + tc % 2
            for dc in range(NDC):
                ev_mm = pr.op("t", lambda e, dc=dc, tc=tc, bkg=bkg: e.matmul(
                    psf(bkg, E1), XN(dc, tc * P, (tc + 1) * P),
                    self.rwt[:, dc * E1:(dc + 1) * E1],
                    start=(dc == 0), stop=False),
                    waits=[evs_c2[dc], self.bank_free.get(bkg)])
            ev_mm = pr.op("t", lambda e, tc=tc, bkg=bkg: e.matmul(
                psf(bkg, E1), CRB("ones", 0, P), CRB("rb"),
                start=False, stop=True), waits=[self.ECG])
            ev_mx = pr.op("v", lambda e, gc=gc, bkg=bkg: e.reduce_max(gc(0), psf(bkg, E1), axis=AX.X),
                          waits=[ev_mm])
            ev_nm = pr.op("v", lambda e, gc=gc: e.tensor_scalar(gc(0), gc(0), -1.0, None, ALU.mult),
                          waits=[ev_mx])
            ev_ex = pr.op("s", lambda e, bkg=bkg, gc=gc: e.activation(
                gc(1, 16), psf(bkg, E1), AF.Exp, bias=gc(0), accum_out=gc(17)),
                waits=[ev_nm])
            self.bank_free[bkg] = ev_ex
            ev_rs2 = pr.op("v", lambda e, gc=gc: e.reciprocal(gc(18), gc(17)), waits=[ev_ex])
            ev_p = pr.op("v", lambda e, gc=gc: e.tensor_scalar(
                gc(1, 16), gc(1, 16), gc(18), None, ALU.mult), waits=[ev_rs2])
            ev_m8 = pr.op("v", lambda e, gc=gc: e.max(out=gc(19, 8), in_=gc(1, 16)), waits=[ev_p])
            ev_in = pr.op("v", lambda e, gc=gc: e.tensor_scalar(
                gc(27, 16), gc(1, 16), gc(24), None, ALU.is_ge), waits=[ev_m8])
            ev_wr = pr.op("v", lambda e, gc=gc: e.tensor_tensor(
                gc(27, 16), gc(27, 16), gc(1, 16), ALU.mult), waits=[ev_in])
            ev_s = pr.op("v", lambda e, gc=gc: e.reduce_sum(gc(43), gc(27, 16), axis=AX.X),
                         waits=[ev_wr])
            ev_sp = pr.op("v", lambda e, gc=gc: e.tensor_scalar(gc(44), gc(43), 1e-10, None, ALU.add),
                          waits=[ev_s])
            ev_r2 = pr.op("v", lambda e, gc=gc: e.reciprocal(gc(45), gc(44)), waits=[ev_sp])
            self.ev_w1tm_ln1[tc] = pr.op("v", lambda e, tc=tc, gc=gc: e.tensor_scalar(
                self.w1tm[:, tc * E1:(tc + 1) * E1], gc(27, 16), gc(45), None, ALU.mult),
                waits=[ev_r2])
            pr.op("v", lambda e, tc=tc, gc=gc: e.tensor_tensor(
                ST("acc1", tc), gc(43), gc(45), ALU.mult), waits=[ev_r2])

        wfbc = [self.pv(32768 + e * 1024, BF16, 512) for e in range(E1)]
        qrow = self.pv(63488, BF16, 2048, 0, 1)
        self.ev_bc0 = {}
        prev_q = None
        for e_ in range(E1):
            evq = []
            for q4 in range(4):
                tc = q4
                bk = q4 % 2
                ev_tt2 = pr.op("t", lambda e, tc=tc, e_=e_, bk=bk: e.transpose(
                    psf(bk, P)[0:1, :], self.w1tm[:, tc * E1 + e_: tc * E1 + e_ + 1],
                    self.id_f[:]),
                    waits=[self.ev_w1tm_ln1[tc], self.bank_free.get(bk)])
                ev_cp = pr.op("v", lambda e, q4=q4, bk=bk: e.tensor_copy(
                    qrow[0:1, q4 * P:(q4 + 1) * P], psf(bk, P)[0:1, :]),
                    waits=[ev_tt2, prev_q])
                self.bank_free[bk] = ev_cp
                evq.append(ev_cp)
            ev_k1 = pr.op("t", lambda e: e.matmul(
                psf(2, 512), CRB("ones", 0, P), qrow[0:1, 0:512],
                start=True, stop=True),
                waits=evq + [self.bank_free.get(2), self.ECG])
            prev_q = ev_k1
            ev_cb = pr.op("v", lambda e, e_=e_: e.tensor_copy(wfbc[e_][:], psf(2, 512)),
                          waits=[ev_k1])
            self.bank_free[2] = ev_cb
            self.ev_bc0[e_] = ev_cb

    def l1_attn(self):
        pr, dr = self.pr, self.dr
        psf, psb, X, XN, CV, CRB, ST = (self.psf, self.psb, self.X, self.XN,
                                        self.CV, self.CRB, self.ST)
        latf = self.pv(0, BF16, 2 * 2048)
        wkvd = self.pv(8192, BF16, 8 * 256)
        kup = self.pv(12288, BF16, 2 * 256)
        vup = self.pv(13312, BF16, 2 * 256)
        qfpA = self.pv(0, BF16, 2048)
        qfpB = self.pv(4096, BF16, 2048)
        pt = [self.pv(8192 + i * 1024, BF16, 512) for i in range(4)]
        wqp = [self.pv(12288, BF16, 8 * 128), self.pv(14336, BF16, 8 * 128)]
        wo_sb = self.pv(0, BF16, 8 * 1024)
        wkL = self.pv(0, BF16, 8 * 256)
        wvL = self.pv(4096, BF16, 8 * 256)
        kf = [self.pv(16384, BF16, 2048), self.pv(20480, BF16, 2048)]
        vt = self.pv(24576, BF16, 16 * 260)
        of = self.pv(32896, BF16, 8 * 2048)
        SCR = 65792
        qA = self.pv(SCR, BF16, 2048)
        qB = self.pv(SCR + 4096, BF16, 2048)
        qD = self.pv(SCR + 8192, BF16, 2048)
        qda = self.pv(SCR + 8192, BF16, 1024)
        qdb = self.pv(SCR + 10240, BF16, 1024)
        sml = self.pv(SCR + 12288, F32, 64)
        s1bc = self.pv(SCR + 12544, BF16, 512)
        rr = self.pv(SCR + 13568, BF16, 512, 0, 1)
        bcn = self.pv(SCR + 14592, BF16, 512, 0, 64)
        utm = self.pv(SCR + 8192, BF16, 1024)
        kA = self.pv(32896, BF16, 4096)
        kB = self.pv(41088, BF16, 4096)
        kD = self.pv(49280, BF16, 4096)
        kda = self.pv(49280, BF16, 2048)
        kdb = self.pv(53376, BF16, 2048)
        s1row = self.pv(61568, BF16, 2048, 0, 1)

        def rope_norm_b(ev_src, bufA, bufB, bda, bdb, nseg, base, fold, rkey):
            W = nseg * 64
            n = NTC * W
            nh = NTC * nseg
            cosw = ("cosG" if base == "G" else "cosL")
            sinw = ("sinG" if base == "G" else "sinL")
            tb = {"cosG": 0, "sinG": 1, "cosL": 2, "sinL": 3}
            cosb = self.trig[:, tb[cosw] * NTC * 32:(tb[cosw] + 1) * NTC * 32].rearrange(
                "p (t f) -> p t f", f=32).unsqueeze(2).broadcast_to([P, NTC, nseg, 32])
            sinb = self.trig[:, tb[sinw] * NTC * 32:(tb[sinw] + 1) * NTC * 32].rearrange(
                "p (t f) -> p t f", f=32).unsqueeze(2).broadcast_to([P, NTC, nseg, 32])
            A4 = lambda b, o: b[:, 0:n].rearrange("p (t h j) -> p t h j", h=nseg, j=64)[:, :, :, o:64:2]
            D4 = lambda b: b[:, 0:n // 2].rearrange("p (t h j) -> p t h j", h=nseg, j=32)
            SEG = lambda b: b[:, 0:n].rearrange("p (s j) -> p s j", j=64)
            ev_sq = pr.op("s", lambda e: e.activation(qD[:, 0:n] if bufA is qA else kD[:, 0:n],
                                                      bufA[:, 0:n], AF.Square),
                          waits=[ev_src, rkey.get("red")])
            ev_rd = pr.op("v", lambda e: e.reduce_sum(
                sml[:, 0:nh], SEG(qD if bufA is qA else kD), axis=AX.X),
                waits=[ev_sq, rkey.get("out")])
            rkey["red"] = ev_rd
            ev_sr = pr.op("s", lambda e: e.activation(
                sml[:, 0:nh], sml[:, 0:nh], AF.Sqrt, bias=CV("eps6"), scale=1.0 / 64.0),
                waits=[ev_rd])
            ev_rc = pr.op("v", lambda e: e.reciprocal(sml[:, 0:nh], sml[:, 0:nh]), waits=[ev_sr])
            if fold != 1.0:
                ev_rc = pr.op("v", lambda e: e.tensor_scalar(
                    sml[:, 0:nh], sml[:, 0:nh], float(fold), None, ALU.mult), waits=[ev_rc])
            e1 = pr.op("v", lambda e: e.tensor_tensor(D4(bda), A4(bufA, 0), cosb, ALU.mult),
                       waits=[ev_src, ev_rd, self.ECY, rkey.get("out")])
            e2 = pr.op("g", lambda e: e.tensor_tensor(D4(bdb), A4(bufA, 1), sinb, ALU.mult),
                       waits=[ev_src, ev_rd, self.ECY, self.ECG, rkey.get("out")])
            e5 = pr.op("v", lambda e: e.tensor_tensor(A4(bufB, 0), D4(bda), D4(bdb), ALU.subtract),
                       waits=[e1, e2, rkey.get("out")])
            e3 = pr.op("v", lambda e: e.tensor_tensor(D4(bda), A4(bufA, 0), sinb, ALU.mult),
                       waits=[e5])
            e4 = pr.op("g", lambda e: e.tensor_tensor(D4(bdb), A4(bufA, 1), cosb, ALU.mult),
                       waits=[e5])
            e6 = pr.op("v", lambda e: e.tensor_tensor(A4(bufB, 1), D4(bda), D4(bdb), ALU.add),
                       waits=[e3, e4])
            ev_o = pr.op("v", lambda e: e.tensor_tensor(
                bufA[:, 0:n].rearrange("p (s j) -> p s j", j=64),
                bufB[:, 0:n].rearrange("p (s j) -> p s j", j=64),
                sml[:, 0:nh].unsqueeze(2).broadcast_to([P, nh, 64]), ALU.mult),
                waits=[e5, e6, ev_rc, rkey.get("ard")])
            rkey["out"] = ev_o
            return ev_o

        rkG = {}
        rkQ = {}

        ev_a = pr.op("v", lambda e: e.tensor_scalar(
            sml[:, 0:16], ST("var1", 0, 16), CV("eps5"), None, ALU.add), waits=[self.ECY])
        ev_b = pr.op("v", lambda e: e.reciprocal(sml[:, 16:32], sml[:, 0:16]), waits=[ev_a])
        ev_c = pr.op("v", lambda e: e.tensor_tensor(
            sml[:, 0:16], ST("var1", 0, 16), sml[:, 16:32], ALU.mult), waits=[ev_b])
        ev_d = pr.op("s", lambda e: e.activation(
            sml[:, 16:32], sml[:, 0:16], AF.Sqrt, bias=CV("eps6")), waits=[ev_c])
        ev_s1 = pr.op("v", lambda e: e.reciprocal(ST("s1", 0, 16), sml[:, 16:32]), waits=[ev_d])
        rkG["red"] = ev_s1
        rkQ["red"] = ev_s1
        ev_s1r = []
        for tc in range(NTC):
            ev_tt = pr.op("t", lambda e, tc=tc: e.transpose(
                psf(6, P)[0:1, :], ST("s1", tc), self.id_f[:]),
                waits=[ev_s1, self.bank_free.get(6)])
            ev_cp = pr.op("v", lambda e, tc=tc: e.tensor_copy(
                s1row[0:1, tc * P:(tc + 1) * P], psf(6, P)[0:1, :]), waits=[ev_tt])
            self.bank_free[6] = ev_cp
            ev_s1r.append(ev_cp)

        for dc in range(NDC):
            pr.op("g", lambda e, dc=dc: e.dma_start(
                out=wkvd[:, dc * 256:(dc + 1) * 256],
                in_=dr["L1_attn_g_kv_down"][dc * P:(dc + 1) * P, :]), sig="dkvd", sigv=16)
        ev_wkvd0 = ("dkvd", pr.sems["dkvd"][1])
        for lc in range(2):
            pr.op("g", lambda e, lc=lc: e.dma_start(
                out=kup[:, lc * 256:(lc + 1) * 256],
                in_=dr["L1_attn_g_k_up"][lc * P:(lc + 1) * P, :]), sig="dkup", sigv=16)
            pr.op("g", lambda e, lc=lc: e.dma_start(
                out=vup[:, lc * 256:(lc + 1) * 256],
                in_=dr["L1_attn_g_v_up"][lc * P:(lc + 1) * P, :]), sig="dvup", sigv=16)
        ev_kup = ("dkup", pr.sems["dkup"][1])
        ev_vup = ("dvup", pr.sems["dvup"][1])
        ev_wkvd = []
        for dc in range(NDC):
            ev_wkvd.append(pr.op("v", lambda e, dc=dc: e.tensor_scalar(
                wkvd[:, dc * 256:(dc + 1) * 256], wkvd[:, dc * 256:(dc + 1) * 256],
                CV("n1w", dc), None, ALU.mult), waits=[ev_wkvd0]))

        ev_latf = {}
        ev_s1bc_reader = None
        for tt in range(4):
            ev_k1 = pr.op("t", lambda e, tt=tt: e.matmul(
                psf(6, 512), CRB("ones", 0, P), s1row[0:1, tt * 512: tt * 512 + 512],
                start=True, stop=True),
                waits=ev_s1r + [self.bank_free.get(6), self.ECG])
            ev_sb = pr.op("v", lambda e: e.tensor_copy(s1bc[:], psf(6, 512)),
                          waits=[ev_k1, ev_s1bc_reader])
            self.bank_free[6] = ev_sb
            for lc in range(2):
                bk = (2 * tt + lc) % 2
                for dc in range(NDC):
                    ev_mm = pr.op("t", lambda e, dc=dc, lc=lc, tt=tt, bk=bk: e.matmul(
                        psf(bk, 512), wkvd[:, dc * 256 + lc * 128: dc * 256 + lc * 128 + 128],
                        XN(dc, tt * 512, (tt + 1) * 512),
                        start=(dc == 0), stop=(dc == NDC - 1)),
                        waits=[ev_wkvd[dc], self.bank_free.get(bk)])
                ev_ev = pr.op("v", lambda e, lc=lc, tt=tt, bk=bk: e.tensor_tensor(
                    latf[:, lc * 2048 + tt * 512: lc * 2048 + tt * 512 + 512],
                    psf(bk, 512), s1bc[:], ALU.mult), waits=[ev_mm, ev_sb])
                self.bank_free[bk] = ev_ev
                ev_latf[(lc, tt)] = ev_ev
                ev_s1bc_reader = ev_ev
        evl = list(ev_latf.values())

        ev_vt_ones = pr.op("g", lambda e: e.memset(vt[:], 1.0))
        ev_kA = []
        ev_vte = {}
        for tc in range(NTC):
            bk = 2 + tc % 2
            for lc in range(2):
                ev_mm = pr.op("t", lambda e, lc=lc, tc=tc, bk=bk: e.matmul(
                    psf(bk, 256), latf[:, lc * 2048 + tc * P: lc * 2048 + (tc + 1) * P],
                    kup[:, lc * 256:(lc + 1) * 256],
                    start=(lc == 0), stop=(lc == 1)),
                    waits=[ev_latf[(lc, tc // 4)], ev_kup, self.bank_free.get(bk)])
            ev_cp = pr.op("v", lambda e, tc=tc, bk=bk: e.tensor_copy(
                kA[:, tc * 256:(tc + 1) * 256], psf(bk, 256)),
                waits=[ev_mm, rkG.get("out"), rkG.get("ard")])
            self.bank_free[bk] = ev_cp
            ev_kA.append(ev_cp)
            bk2 = 4 + tc % 2
            for lc in range(2):
                ev_mm = pr.op("t", lambda e, lc=lc, tc=tc, bk2=bk2: e.matmul(
                    psf(bk2, 256), latf[:, lc * 2048 + tc * P: lc * 2048 + (tc + 1) * P],
                    vup[:, lc * 256:(lc + 1) * 256],
                    start=(lc == 0), stop=(lc == 1)),
                    waits=[ev_latf[(lc, tc // 4)], ev_vup, self.bank_free.get(bk2)])
            ev_c3 = pr.op("v", lambda e, tc=tc, bk2=bk2: e.tensor_copy(
                vt[:, tc * 260: tc * 260 + 260].rearrange("p (h u) -> p h u", u=65)[:, :, 0:64],
                psf(bk2, 256).rearrange("p (h u) -> p h u", u=64)),
                waits=[ev_mm, ev_vt_ones])
            self.bank_free[bk2] = ev_c3
            ev_vte[tc] = ev_c3
        ev_roK = rope_norm_b(ev_kA[-1] if False else ("pv", pr.sems["pv"][1]),
                             kA, kB, kda, kdb, 4, "G", 1.0, rkG)
        ev_kf = []
        for tc in range(NTC):
            for sl in range(2):
                bk2 = 6 + sl
                ev_tt = pr.op("t", lambda e, tc=tc, sl=sl, bk2=bk2: e.transpose(
                    psb(bk2, 128), kA[:, tc * 256 + sl * 128: tc * 256 + sl * 128 + 128],
                    self.id_b[:]),
                    waits=[ev_roK, self.bank_free.get(bk2)])
                ev_c2 = pr.op("s", lambda e, sl=sl, tc=tc, bk2=bk2: e.copy(
                    kf[sl][:, tc * P:(tc + 1) * P], psb(bk2, 128)), waits=[ev_tt])
                self.bank_free[bk2] = ev_c2
                ev_kf.append(ev_c2)
        rkG["ard"] = pr.now("t")

        state = {"ptr": [None, None, None, None], "rrr": None, "bcnr": None}

        def sdpa_heads(heads, kfS, vtS, vt_evs, ofS, ofchunk, src_sb, wname,
                       nw_name, s_name, base, maskf, mc_range, wq_sems, kf_evs,
                       src_gate=None):
            ev_ofd = {}
            qstate = {"prev_q_readers": None}
            pairs = list(range(heads[0] // 2, heads[0] // 2 + len(heads) // 2))

            def qprep1(pair):
                sl = wqn[0] % 2
                ev_wq = pr.op("g", lambda e, pair=pair, sl=sl, wname=wname: e.dma_start(
                    out=wqp[sl].rearrange("p (dc c) -> p dc c", c=128),
                    in_=dr[wname][:, pair * 128:(pair + 1) * 128].rearrange(
                        "(dc p) c -> p dc c", p=P)),
                    sig="dwq%d" % sl, sigv=16, waits=[wq_sems[sl]])
                wqn[0] += 1
                evs_sc = []
                for dc in range(NDC):
                    evs_sc.append(pr.op("g", lambda e, dc=dc, sl=sl: e.tensor_scalar(
                        wqp[sl][:, dc * 128:(dc + 1) * 128], wqp[sl][:, dc * 128:(dc + 1) * 128],
                        CV(nw_name, dc), None, ALU.mult), waits=[ev_wq]))
                for tc in range(NTC):
                    bk = 3
                    for dc in range(NDC):
                        ev_mm = pr.op("t", lambda e, dc=dc, tc=tc, sl=sl, bk=bk, src_sb=src_sb: e.matmul(
                            psf(bk, 128), src_sb(dc, tc),
                            wqp[sl][:, dc * 128:(dc + 1) * 128],
                            start=(dc == 0), stop=(dc == NDC - 1)),
                            waits=[evs_sc[dc], self.bank_free.get(bk), src_gate])
                    ev_cp = pr.op("v", lambda e, tc=tc, bk=bk, s_name=s_name: e.tensor_scalar(
                        qA[:, tc * 128:(tc + 1) * 128], psf(bk, 128), ST(s_name, tc), None, ALU.mult),
                        waits=[ev_mm, rkQ.get("out"), rkQ.get("ard")])
                    self.bank_free[bk] = ev_cp
                wq_sems[sl] = pr.now("t")
                return rope_norm_b(("pv", pr.sems["pv"][1]), qA, qB, qda, qdb,
                                   2, base, SHD, rkQ)

            def qprep2(pair, ev_roQ):
                bp = ((pair // 2) % 2) * 64
                ev_qf = []
                for tc in range(NTC):
                    bk2 = 3 + 2 * (tc % 2)
                    ev_tt = pr.op("t", lambda e, tc=tc, bk2=bk2: e.transpose(
                        psb(bk2, 128), qA[:, tc * 128:(tc + 1) * 128], self.id_b[:]),
                        waits=[ev_roQ, self.bank_free.get(bk2), qstate["prev_q_readers"]])
                    ev_c2 = pr.op("s", lambda e, tc=tc, bk2=bk2, bp=bp: e.copy(
                        qfpA[bp:bp + 64, tc * P:(tc + 1) * P], psb(bk2, 128)[0:64, :]),
                        waits=[ev_tt])
                    ev_c3 = pr.op("s", lambda e, tc=tc, bk2=bk2, bp=bp: e.copy(
                        qfpB[bp:bp + 64, tc * P:(tc + 1) * P], psb(bk2, 128)[64:128, :]),
                        waits=[ev_tt])
                    self.bank_free[bk2] = ev_c3
                    ev_qf += [ev_c2, ev_c3]
                rkQ["ard"] = pr.now("t")
                return ev_qf

            def flush_norm():
                pend = qstate.get("pend")
                if pend is None:
                    return
                h, nt, oc, bko, ev_p = pend
                ev_rr = pr.op("v", lambda e, bko=bko: self.lp(lambda: e.reciprocal(
                    rr[0:1, :], psf(bko, 512)[64:65, :])),
                    waits=[ev_p, state["rrr"]])
                ev_k1 = pr.op("t", lambda e: e.matmul(
                    psf(4, 512)[0:64, :], CRB("ones", 0, 64), rr[0:1, :],
                    start=True, stop=True),
                    waits=[ev_rr, self.bank_free.get(4)])
                state["rrr"] = ev_k1
                ev_bc = pr.op("v", lambda e: e.tensor_copy(
                    bcn[0:64, :], psf(4, 512)[0:64, :]),
                    waits=[ev_k1, state["bcnr"]])
                self.bank_free[4] = ev_bc
                ev_o = pr.op("v", lambda e, h=h, nt=nt, bko=bko, oc=oc: e.tensor_tensor(
                    ofS[(h % 2) * 64: (h % 2) * 64 + 64,
                        oc * 2048 + nt * 512: oc * 2048 + (nt + 1) * 512],
                    psf(bko, 512)[0:64, :], bcn[0:64, :], ALU.mult),
                    waits=[ev_p, ev_bc])
                state["bcnr"] = ev_o
                self.bank_free[bko] = ev_o
                ev_ofd[(h, nt)] = ev_o
                qstate["pend"] = None

            def one_head(pair, hh, ev_qf):
                h = 2 * pair + hh
                kvh = h // 4
                kslot, kbase = kvh // 2, (kvh % 2) * 64
                oc = ofchunk(h)
                for nt in range(4):
                    bko = 6 + nt % 2
                    mcs = mc_range(nt)
                    n_mc = len(mcs)
                    LOOKAHEAD = 2
                    masks = {}
                    for ii in range(n_mc + LOOKAHEAD):
                        if ii < n_mc:
                            i, mc = ii, mcs[ii]
                            bks = i % 3
                            sl2 = i % 4
                            qsb = qfpA if hh == 0 else qfpB
                            ev_s = pr.op("t", lambda e, mc=mc, nt=nt, kslot=kslot, kbase=kbase, bks=bks, qsb=qsb: e.matmul(
                                psf(bks, 512), kfS[kslot][kbase:kbase + 64, mc * P:(mc + 1) * P],
                                qsb[kbase:kbase + 64, nt * 512:(nt + 1) * 512],
                                start=True, stop=True),
                                waits=ev_qf + kf_evs + [self.bank_free.get(bks)])
                            ev_e = pr.op("s", lambda e, sl2=sl2, bks=bks: e.activation(
                                pt[sl2], psf(bks, 512), AF.Exp),
                                waits=[ev_s, state["ptr"][sl2]])
                            self.bank_free[bks] = ev_e
                            masks[i] = maskf(mc, nt, sl2, ev_e)
                        j = ii - LOOKAHEAD
                        if 0 <= j < n_mc:
                            mc_j = mcs[j]
                            sl2j = j % 4
                            last = (j == n_mc - 1)
                            ev_p = pr.op("t", lambda e, mc=mc_j, j=j, sl2=sl2j, bko=bko, last=last, kvh=kvh: e.matmul(
                                psf(bko, 512)[0:65, :],
                                vtS[:, mc * 260 + kvh * 65: mc * 260 + kvh * 65 + 65],
                                pt[sl2], start=(j == 0), stop=last),
                                waits=[masks[j], vt_evs[mc_j],
                                       self.bank_free.get(bko) if j == 0 else None])
                            state["ptr"][sl2j] = ev_p
                    pend_prev = qstate.get("pend")
                    qstate["pend"], qstate["_new"] = pend_prev, (h, nt, oc, bko, ev_p)
                    flush_norm()
                    qstate["pend"] = qstate.pop("_new")

            ev_ro_next = qprep1(pairs[0])
            ev_qf = qprep2(pairs[0], ev_ro_next)
            for idx, pair in enumerate(pairs):
                if idx + 1 < len(pairs):
                    ev_ro_next = qprep1(pairs[idx + 1])
                one_head(pair, 0, ev_qf)
                one_head(pair, 1, ev_qf)
                qstate["prev_q_readers"] = pr.now("t")
                if idx + 1 < len(pairs):
                    ev_qf = qprep2(pairs[idx + 1], ev_ro_next)
            flush_norm()
            return ev_ofd

        wqn = [0]
        kv_done = pr.now("t")
        wq_sems = [kv_done, kv_done]
        src_xn = lambda dc, tc: self.xn_sb[:, dc * T + tc * P: dc * T + (tc + 1) * P]
        src_u = lambda dc, tc: u_bf[:, dc * 2048 + tc * P: dc * 2048 + (tc + 1) * P]

        def mask_g(mc, nt, sl2, ev_e):
            if mc < 4 * nt:
                return ev_e
            b = nt * 512 - mc * P
            return pr.op("g" if sl2 % 2 else "v", lambda e, sl2=sl2, b=b: e.tensor_tensor(
                pt[sl2], pt[sl2], self.mbc[:, MZ + b: MZ + b + 512], ALU.mult),
                waits=[ev_e])

        ev_ofg = sdpa_heads(list(range(16)), kf, vt, ev_vte, of, lambda h: h // 2,
                            src_xn, "L1_attn_g_wq", "n1w", "s1", "G",
                            mask_g, lambda nt: list(range(0, 4 * nt + 4)),
                            wq_sems, ev_kf)

        kfvt_dead = pr.now("t")
        evxn = []
        for dc in range(NDC):
            evxn.append(pr.op("v", lambda e, dc=dc: e.tensor_tensor(
                X(dc), X(dc), XN(dc), ALU.add), waits=[kfvt_dead]))

        for oc in range(NDC):
            pr.op("g", lambda e, oc=oc: e.dma_start(
                out=wo_sb[:, oc * 1024:(oc + 1) * 1024],
                in_=dr["L1_attn_g_wo"][oc * P:(oc + 1) * P, :]), sig="dwog", sigv=16,
                waits=[kfvt_dead])
        ev_wo = ("dwog", pr.sems["dwog"][1])
        ev_u = {}
        for dc in range(NDC):
            for tt in range(4):
                bk = (4 * dc + tt) % 2
                for oc in range(NDC):
                    ev_mm = pr.op("t", lambda e, dc=dc, tt=tt, oc=oc, bk=bk: e.matmul(
                        psf(bk, 512), wo_sb[:, oc * 1024 + dc * 128: oc * 1024 + dc * 128 + 128],
                        of[:, oc * 2048 + tt * 512: oc * 2048 + tt * 512 + 512],
                        start=(oc == 0), stop=(oc == NDC - 1)),
                        waits=[ev_wo, self.bank_free.get(bk),
                               ev_ofg[(2 * oc, tt)], ev_ofg[(2 * oc + 1, tt)]])
                ev_ax = pr.op("v", lambda e, dc=dc, tt=tt, bk=bk: e.tensor_tensor(
                    X(dc, tt * 512, (tt + 1) * 512), X(dc, tt * 512, (tt + 1) * 512),
                    psf(bk, 512), ALU.add), waits=[ev_mm, evxn[dc]])
                ev_uu = pr.op("v", lambda e, dc=dc, tt=tt, bk=bk: e.tensor_tensor(
                    XN(dc, tt * 512, (tt + 1) * 512), psf(bk, 512),
                    XN(dc, tt * 512, (tt + 1) * 512), ALU.add),
                    waits=[ev_mm, evxn[dc]])
                self.bank_free[bk] = ev_uu
                ev_u[(dc, tt)] = ev_uu
        wo_done = pr.now("t")

        ev_s2l = []
        prev_utm_reader = None
        for tc in range(NTC):
            evs_t = []
            for dc in range(NDC):
                bk = 2 + dc % 2
                ev_tt = pr.op("t", lambda e, dc=dc, tc=tc, bk=bk: e.transpose(
                    psb(bk, 128), XN(dc, tc * P, (tc + 1) * P), self.id_b[:]),
                    waits=[ev_u[(dc, tc // 4)], self.bank_free.get(bk)])
                ev_cp = pr.op("v", lambda e, dc=dc, bk=bk: e.tensor_copy(
                    utm[:, dc * 128:(dc + 1) * 128], psb(bk, 128)),
                    waits=[ev_tt, prev_utm_reader])
                self.bank_free[bk] = ev_cp
                evs_t.append(ev_cp)
            acc_evs = []
            for b4 in range(4):
                acc_evs.append(pr.op("s", lambda e, b4=b4: e.activation(
                    qB[:, 0:256], utm[:, b4 * 256:(b4 + 1) * 256], AF.Square,
                    accum_out=sml[:, b4:b4 + 1]),
                    waits=evs_t + [rkQ.get("red"), acc_evs[-1] if acc_evs else None]))
            prev_utm_reader = acc_evs[-1]
            ev_s2a = pr.op("v", lambda e: e.reduce_sum(
                sml[:, 4:5], sml[:, 0:4], axis=AX.X), waits=acc_evs)
            rkQ["red"] = ev_s2a
            rkG["red"] = ev_s2a
            ev_s2b = pr.op("s", lambda e: e.activation(
                sml[:, 5:6], sml[:, 4:5], AF.Sqrt, bias=CV("eps6"), scale=1.0 / D),
                waits=[ev_s2a])
            ev_s2c = pr.op("v", lambda e, tc=tc: e.reciprocal(ST("s2", tc), sml[:, 5:6]),
                           waits=[ev_s2b])
            ev_s2l.append(ev_s2c)

        wo_readers_done = pr.now("t")
        for dc in range(NDC):
            pr.op("g", lambda e, dc=dc: e.dma_start(
                out=wkL[:, dc * 256:(dc + 1) * 256],
                in_=dr["L1_attn_l_wk"][dc * P:(dc + 1) * P, :]), sig="dwkL", sigv=16,
                waits=[wo_readers_done])
            pr.op("g", lambda e, dc=dc: e.dma_start(
                out=wvL[:, dc * 256:(dc + 1) * 256],
                in_=dr["L1_attn_l_wv"][dc * P:(dc + 1) * P, :]), sig="dwvL", sigv=16,
                waits=[wo_readers_done])
        ev_wkL0 = ("dwkL", pr.sems["dwkL"][1])
        ev_wvL0 = ("dwvL", pr.sems["dwvL"][1])
        ev_wkL = []
        ev_wvL = []
        for dc in range(NDC):
            ev_wkL.append(pr.op("v", lambda e, dc=dc: e.tensor_scalar(
                wkL[:, dc * 256:(dc + 1) * 256], wkL[:, dc * 256:(dc + 1) * 256],
                CV("n2w", dc), None, ALU.mult), waits=[ev_wkL0]))
            ev_wvL.append(pr.op("v", lambda e, dc=dc: e.tensor_scalar(
                wvL[:, dc * 256:(dc + 1) * 256], wvL[:, dc * 256:(dc + 1) * 256],
                CV("n2w", dc), None, ALU.mult), waits=[ev_wvL0]))
        ev_vtL_ones = pr.op("g", lambda e: e.memset(vt[:], 1.0), waits=[wo_done])
        ev_kA = []
        ev_vtLe = {}
        for tc in range(NTC):
            bk = 2 + tc % 2
            for dc in range(NDC):
                ev_mm = pr.op("t", lambda e, dc=dc, tc=tc, bk=bk: e.matmul(
                    psf(bk, 256), XN(dc, tc * P, (tc + 1) * P),
                    wkL[:, dc * 256:(dc + 1) * 256],
                    start=(dc == 0), stop=(dc == NDC - 1)),
                    waits=[ev_wkL[dc], self.bank_free.get(bk), ev_u[(dc, tc // 4)]])
            ev_cp = pr.op("v", lambda e, tc=tc, bk=bk: e.tensor_scalar(
                kA[:, tc * 256:(tc + 1) * 256], psf(bk, 256), ST("s2", tc), None, ALU.mult),
                waits=[ev_mm, ev_s2l[tc], rkG.get("out"), rkG.get("ard"), wo_readers_done])
            self.bank_free[bk] = ev_cp
            ev_kA.append(ev_cp)
            bk2 = 4 + tc % 2
            for dc in range(NDC):
                ev_mm = pr.op("t", lambda e, dc=dc, tc=tc, bk2=bk2: e.matmul(
                    psf(bk2, 256), XN(dc, tc * P, (tc + 1) * P),
                    wvL[:, dc * 256:(dc + 1) * 256],
                    start=(dc == 0), stop=(dc == NDC - 1)),
                    waits=[ev_wvL[dc], self.bank_free.get(bk2), ev_u[(dc, tc // 4)]])
            ev_c3 = pr.op("v", lambda e, tc=tc, bk2=bk2: e.tensor_scalar(
                vt[:, tc * 260: tc * 260 + 260].rearrange("p (h u) -> p h u", u=65)[:, :, 0:64],
                psf(bk2, 256).rearrange("p (h u) -> p h u", u=64),
                ST("s2", tc), None, ALU.mult),
                waits=[ev_mm, ev_vtL_ones, ev_s2l[tc]])
            self.bank_free[bk2] = ev_c3
            ev_vtLe[tc] = ev_c3
        ev_roKL = rope_norm_b(("pv", pr.sems["pv"][1]), kA, kB, kda, kdb, 4, "L", 1.0, rkG)
        ev_kfL = []
        for tc in range(NTC):
            for sl in range(2):
                bk2 = 6 + sl
                ev_tt = pr.op("t", lambda e, tc=tc, sl=sl, bk2=bk2: e.transpose(
                    psb(bk2, 128), kA[:, tc * 256 + sl * 128: tc * 256 + sl * 128 + 128],
                    self.id_b[:]),
                    waits=[ev_roKL, self.bank_free.get(bk2)])
                ev_c2 = pr.op("s", lambda e, sl=sl, tc=tc, bk2=bk2: e.copy(
                    kf[sl][:, tc * P:(tc + 1) * P], psb(bk2, 128)), waits=[ev_tt])
                self.bank_free[bk2] = ev_c2
                ev_kfL.append(ev_c2)
        rkG["ard"] = pr.now("t")

        def mask_l(mc, nt, sl2, ev_e):
            b = nt * 512 - mc * P
            return pr.op("g" if sl2 % 2 else "v", lambda e, sl2=sl2, b=b: e.tensor_tensor(
                pt[sl2], pt[sl2], self.mb2[:, MZ + b: MZ + b + 512], ALU.mult),
                waits=[ev_e])

        def mc_range_l(nt):
            return list(range(max(0, 4 * nt - 2), 4 * nt + 4))

        src_u = lambda dc, tc: self.xn_sb[:, dc * T + tc * P: dc * T + (tc + 1) * P]
        state["bcnr"] = wo_done
        u_done = pr.now("v")
        ev_ofl = sdpa_heads(list(range(16)), kf, vt, ev_vtLe, of, lambda h: h // 2,
                            src_u, "L1_attn_l_wq", "n2w", "s2", "L",
                            mask_l, mc_range_l, wq_sems, ev_kfL, src_gate=u_done)

        kfvt_dead2 = pr.now("t")
        for oc in range(NDC):
            pr.op("g", lambda e, oc=oc: e.dma_start(
                out=wo_sb[:, oc * 1024:(oc + 1) * 1024],
                in_=dr["L1_attn_l_wo"][oc * P:(oc + 1) * P, :]), sig="dwol", sigv=16,
                waits=[kfvt_dead2])
        ev_wol = ("dwol", pr.sems["dwol"][1])
        for dc in range(NDC):
            for tt in range(4):
                bk = (4 * dc + tt) % 2
                for oc in range(NDC):
                    ev_mm = pr.op("t", lambda e, dc=dc, tt=tt, oc=oc, bk=bk: e.matmul(
                        psf(bk, 512), wo_sb[:, oc * 1024 + dc * 128: oc * 1024 + dc * 128 + 128],
                        of[:, oc * 2048 + tt * 512: oc * 2048 + tt * 512 + 512],
                        start=(oc == 0), stop=(oc == NDC - 1)),
                        waits=[ev_wol, self.bank_free.get(bk),
                               ev_ofl[(2 * oc, tt)], ev_ofl[(2 * oc + 1, tt)]])
                ev_ax = pr.op("v", lambda e, dc=dc, tt=tt, bk=bk: e.tensor_tensor(
                    X(dc, tt * 512, (tt + 1) * 512), X(dc, tt * 512, (tt + 1) * 512),
                    psf(bk, 512), ALU.add), waits=[ev_mm, pr.now("v") if (dc == 0 and tt == 0) else None])
                self.bank_free[bk] = ev_ax

    def l1_ffn(self):
        pr, dr = self.pr, self.dr
        psf, psb, X, XN, CV, CRB, ST = (self.psf, self.psb, self.X, self.XN,
                                        self.CV, self.CRB, self.ST)
        ewb = [self.pv(0, BF16, 8 * 1024), self.pv(16384, BF16, 8 * 1024)]
        wfbc = [self.pv(32768 + e * 1024, BF16, 512) for e in range(E1)]
        xp = self.pv(49152, BF16, 8 * 512)
        raccbc = self.pv(57344, BF16, 512)
        tmp1 = self.pv(58368, F32, 512)
        accrow = self.pv(67584, BF16, 2048, 0, 1)
        raccrow = self.pv(71680, BF16, 2048, 0, 1)
        eps2row = self.pv(75776, BF16, 2048, 0, 1)
        qrow = self.pv(63488, BF16, 2048, 0, 1)

        ev_w1tm = {tc: None for tc in range(NTC)}
        ev_acc = {tc: None for tc in range(NTC)}

        ev_ar = []
        for tc in range(NTC):
            bk = 2 + tc % 2
            ev_tt = pr.op("t", lambda e, tc=tc, bk=bk: e.transpose(
                psf(bk, P)[0:1, :], ST("acc1", tc), self.id_f[:]),
                waits=[ev_acc[tc], self.bank_free.get(bk)])
            ev_cp = pr.op("v", lambda e, tc=tc, bk=bk: e.tensor_copy(
                accrow[0:1, tc * P:(tc + 1) * P], psf(bk, P)[0:1, :]), waits=[ev_tt])
            self.bank_free[bk] = ev_cp
            ev_ar.append(ev_cp)
        ev_rr2 = pr.op("v", lambda e: self.lp(lambda: e.reciprocal(raccrow[0:1, :], accrow[0:1, :])),
                       waits=ev_ar)
        ev_e2a = pr.op("v", lambda e: e.tensor_tensor(
            eps2row[0:1, :], accrow[0:1, :], accrow[0:1, :], ALU.mult), waits=ev_ar)
        ev_e2b = pr.op("v", lambda e: e.tensor_scalar(
            eps2row[0:1, :], eps2row[0:1, :], 1e-10, None, ALU.mult), waits=[ev_e2a])

        ewn = [0]
        ewfree = [None, None]
        for tt in range(4):
            if tt == 0:
                ev_bc = dict(self.ev_bc0)
                prev_q = None
            else:
                ev_bc = {}
                prev_q = None
            for e_ in ([] if tt == 0 else range(E1)):
                evq = []
                for q4 in range(4):
                    tc = tt * 4 + q4
                    bk = q4 % 2
                    ev_tt2 = pr.op("t", lambda e, tc=tc, e_=e_, bk=bk: e.transpose(
                        psf(bk, P)[0:1, :], self.w1tm[:, tc * E1 + e_: tc * E1 + e_ + 1],
                        self.id_f[:]),
                        waits=[ev_w1tm[tc], self.bank_free.get(bk)])
                    ev_cp = pr.op("v", lambda e, q4=q4, bk=bk: e.tensor_copy(
                        qrow[0:1, q4 * P:(q4 + 1) * P], psf(bk, P)[0:1, :]),
                        waits=[ev_tt2, prev_q])
                    self.bank_free[bk] = ev_cp
                    evq.append(ev_cp)
                ev_k1 = pr.op("t", lambda e: e.matmul(
                    psf(2, 512), CRB("ones", 0, P), qrow[0:1, 0:512],
                    start=True, stop=True),
                    waits=evq + [self.bank_free.get(2), self.ECG])
                prev_q = ev_k1
                ev_cb = pr.op("v", lambda e, e_=e_: e.tensor_copy(wfbc[e_][:], psf(2, 512)),
                              waits=[ev_k1, ewfree[0], ewfree[1]])
                self.bank_free[2] = ev_cb
                ev_bc[e_] = ev_cb
            ev_k2 = pr.op("t", lambda e, tt=tt: e.matmul(
                psf(3, 512), CRB("ones", 0, P), raccrow[0:1, tt * 512:(tt + 1) * 512],
                start=True, stop=True),
                waits=[ev_rr2, self.bank_free.get(3)])
            ev_rb = pr.op("v", lambda e: e.tensor_copy(raccbc[:], psf(3, 512)),
                          waits=[ev_k2, pr.now("v")])
            self.bank_free[3] = ev_rb
            ev_xp_reader = None
            ev_dc = {}
            for ie, e_ in enumerate(range(E1)):
                sl = ewn[0] % 2
                ev_ew = pr.op("g", lambda e, e_=e_, sl=sl: e.dma_start(
                    out=ewb[sl].rearrange("p (dk c) -> p dk c", c=1024),
                    in_=dr["L1_ffn_expert_w"][e_].rearrange("(dk p) c -> p dk c", p=P)),
                    sig="dew%d" % sl, sigv=16, waits=[ewfree[sl]])
                ewn[0] += 1
                ev_xps = []
                for dck in range(NDC):
                    ev_xps.append(pr.op("v", lambda e, dck=dck, tt=tt, e_=e_: e.tensor_tensor(
                        xp[:, dck * 512:(dck + 1) * 512],
                        XN(dck, tt * 512, (tt + 1) * 512), wfbc[e_][:], ALU.mult),
                        waits=[ev_bc[e_], ev_xp_reader]))
                last = None
                for dco in range(NDC):
                    for dck in range(NDC):
                        last = pr.op("t", lambda e, dco=dco, dck=dck, sl=sl, st=(ie == 0 and dck == 0): e.matmul(
                            psf(dco, 512), ewb[sl][:, dck * 1024 + dco * 128: dck * 1024 + dco * 128 + 128],
                            xp[:, dck * 512:(dck + 1) * 512],
                            start=st, stop=False),
                            waits=[ev_ew, ev_xps[dck],
                                   self.bank_free.get(dco) if ie == 0 and dck == 0 else None])
                    ev_dc[dco] = last
                ewfree[sl] = last
                ev_xp_reader = last
            prev_ax = None
            for dco in range(NDC):
                ev_ep = pr.op("t", lambda e, dco=dco, tt=tt: e.matmul(
                    psf(dco, 512), CRB("ones", 0, P), eps2row[0:1, tt * 512:(tt + 1) * 512],
                    start=False, stop=True),
                    waits=[ev_e2b, ev_dc[dco]])
                ev_t1 = pr.op("v", lambda e, dco=dco: e.tensor_tensor(
                    tmp1[:], psf(dco, 512), raccbc[:], ALU.mult),
                    waits=[ev_ep, ev_rb, prev_ax])
                ev_ax = pr.op("v", lambda e, dco=dco, tt=tt: e.tensor_tensor(
                    X(dco, tt * 512, (tt + 1) * 512), X(dco, tt * 512, (tt + 1) * 512),
                    tmp1[:], ALU.add), waits=[ev_t1])
                prev_ax = ev_ax
                self.bank_free[dco] = ev_ax


_NC_CACHE = {}


def kernel(**inputs):
    key = (STAGE, tuple(sorted(TAPS)))
    if key not in _NC_CACHE:
        _NC_CACHE[key] = K(STAGE, dict(TAPS)).build()
    nc = _NC_CACHE[key]

    ident, trigpack, mbc, mb2 = host_constants()
    cvec, _ = pack_cvec(inputs)
    crow, _ = pack_crow(inputs)

    used = {
        "L0_attn_Wv", "L0_attn_Wg", "L0_attn_Win", "L0_attn_Wout",
        "L0_ffn_Wg", "L0_ffn_W1", "L0_ffn_W2", "L0_ffn_b2",
        "L1_attn_g_wq", "L1_attn_g_kv_down", "L1_attn_g_k_up", "L1_attn_g_v_up",
        "L1_attn_g_wo", "L1_attn_l_wq", "L1_attn_l_wk", "L1_attn_l_wv",
        "L1_attn_l_wo", "L1_ffn_router_w", "L1_ffn_expert_w",
    }
    base = {k: np.ascontiguousarray(np.asarray(v, dtype=np.float32))
            for k, v in inputs.items() if k in used}
    base["c_ident"] = ident
    base["c_trig"] = trigpack
    base["c_mbc"] = mbc
    base["c_mb2"] = mb2
    base["c_cvec"] = cvec
    base["c_crow"] = crow
    x = np.ascontiguousarray(np.asarray(inputs["L0_attn_x"], dtype=np.float32))
    in_maps = []
    for c in range(8):
        m = dict(base)
        m["x_in"] = x[c]
        in_maps.append(m)
    res = run_bass_kernel_spmd(nc, in_maps, core_ids=list(range(8)))
    return np.stack([np.asarray(r["out"]) for r in res.results], axis=0)
